# version 22
# baseline (speedup 1.0000x reference)
import contextlib
import numpy as np
import ml_dtypes
import concourse.bass as bass
from concourse.bass import AP
import concourse.mybir as mybir
from concourse.bass_utils import run_bass_kernel_spmd

F32 = mybir.dt.float32
BF16 = mybir.dt.bfloat16
AF = mybir.ActivationFunctionType
ALU = mybir.AluOpType

T, D, H, Kd = 2048, 1024, 16, 64
C = 64
NCH = T // C
SLAB = 8
NSLAB = NCH // SLAB
TSLAB = C * SLAB
NTT = T // 128
NEG = -1.0e30
LN_KD = float(np.log(Kd ** -0.5))
LNRN_MAX = float(np.log(1e12))
EPS_LN = 1e-5

ENGINES = ("tensor", "vector", "scalar", "gpsimd", "sync")
QS, QG, QA = "qs", "qg", "qa"
ET, EV, EA, EP = "et", "ev", "ea", "ep"


class Sched:
    NPOOL = 8

    def __init__(self):
        self.prog = {e: [] for e in ENGINES}
        self.sems = {}
        self.cnt = {}
        self.lastwait = {}
        self.lastop = {}
        self.qpool = {}
        self.quse = {}
        for q in (QS, QG, QA):
            self.qpool[q] = ["%s%d" % (q, i) for i in range(self.NPOOL)]
            self.quse[q] = 0
            for nm in self.qpool[q]:
                self.sem(nm)

    def sem(self, name):
        self.cnt.setdefault(name, 0)

    def wait(self, eng, ev):
        if ev is None:
            return
        if isinstance(ev, (list, tuple)) and ev and isinstance(ev[0], (list, tuple)):
            for e2 in ev:
                self.wait(eng, e2)
            return
        sem, val = ev
        if val <= 0:
            return
        key = (eng, sem)
        if self.lastwait.get(key, 0) >= val:
            return
        self.lastwait[key] = val
        self.prog[eng].append(("w", sem, val))

    AUTOSER = True

    def emit(self, eng, fn, deps=(), sig=None, dma=False):
        if Sched.AUTOSER and not dma and eng in ("vector", "scalar", "gpsimd"):
            self.wait(eng, self.lastop.get(eng))
        for d in deps:
            self.wait(eng, d)
        ev = None
        if sig is not None:
            self.sem(sig)
            inc = 16 if dma else 1
            self.cnt[sig] += inc
            ev = (sig, self.cnt[sig])
            self.prog[eng].append(("i", fn, sig, inc))
            if not dma:
                self.lastop[eng] = ev
        else:
            self.prog[eng].append(("n", fn))
        return ev

    def now(self, sem):
        if sem in self.qpool:
            return [(nm, self.cnt.get(nm, 0)) for nm in self.qpool[sem]]
        return (sem, self.cnt.get(sem, 0))

    def barrier(self):
        evs = [self.now(x) for x in (ET, EV, EA, EP)]
        for q in (QS, QG, QA):
            evs.extend(self.now(q))
        return evs

    def replay(self, eng_name, eng):
        for item in self.prog[eng_name]:
            if item[0] == "w":
                eng.wait_ge(self.sems[item[1]], item[2])
            elif item[0] == "i":
                item[1](eng).then_inc(self.sems[item[2]], item[3])
            else:
                item[1](eng)


def hr(h):
    return 64 * (h % 2)


class MK:
    def __init__(self, debug=()):
        self.nc = bass.Bass("TRN2", target_bir_lowering=False)
        self.s = Sched()
        for q in (QS, QG, QA, ET, EV, EA, EP):
            self.s.sem(q)
        self.debug = list(debug)
        self._declare_dram()
        self._declare_sbuf()

    def _declare_dram(self):
        nc = self.nc
        def dinp(name, shape, dtype=F32):
            return nc.dram_tensor(name, list(shape), dtype, kind="ExternalInput").ap()
        self.x_in = dinp("x", (T, D))
        self.wT = {k: dinp("w%sT" % k, (D, D)) for k in "qkvo"}
        self.wgbT = dinp("wgbT", (D, 32))
        self.convw_in = dinp("convw", (128, 96))
        self.convb_in = dinp("convb", (128, 24))
        self.g1p = dinp("g1p", (2, 4, 1024))
        self.g2p = dinp("g2p", (2, 4, 256))
        self.fbias = dinp("fbias", (1, D))
        self.i128 = dinp("ident128", (128, 128), BF16)
        self.i64d = dinp("ident64d", (128, 64), BF16)
        self.i96f = dinp("ident96f", (96, 96))
        self.i128f = dinp("ident128f", (128, 128))
        self.maskLd_in = dinp("maskLd", (128, 64))
        self.selb3_in = dinp("selb3", (16, 1024), BF16)
        self.maskUd_in = dinp("maskUd", (128, 64))
        self.out_x = nc.dram_tensor("out", [T, D], F32, kind="ExternalOutput").ap()
        self.scr_w = nc.dram_tensor("scr_w", [2, 256, 1024], F32, kind="Internal").ap()
        self.scr_mp = nc.dram_tensor("scr_mp", [2, NCH, 8, 64, 64], BF16, kind="Internal").ap()
        self.scr_t16 = nc.dram_tensor("scr_t16", [NSLAB, 128, 4, 256], F32, kind="Internal").ap()
        self.scr_t16t = nc.dram_tensor("scr_t16t", [NSLAB, 128, 4, 256], F32, kind="Internal").ap()
        self.scr_flat = nc.dram_tensor("scr_flat", [2, H, T], F32, kind="Internal").ap()
        self.scr_x2t = nc.dram_tensor("scr_x2t", [8, 128, T], BF16, kind="Internal").ap()
        self.scr_tc = nc.dram_tensor("scr_tc", [NCH, 64, 96], F32, kind="Internal").ap()
        self.dbg_out = {}
        for key, _, shape, dtype in self.debug:
            self.dbg_out[key] = nc.dram_tensor(
                "dbg_" + key, list(shape), dtype, kind="ExternalOutput").ap()

    def _alias(self, name, shape, dtype, backing, byte_off=0):
        nc = self.nc
        base = nc.lookup_mloc(backing).addr
        return nc.alloc_sbuf_tensor_at(name, list(shape), dtype,
                                       offset=base + byte_off)

    def _declare_sbuf(self):
        nc = self.nc
        def sbuf(name, shape, dtype=F32):
            return nc.alloc_sbuf_tensor(name, list(shape), dtype)
        U8 = mybir.dt.uint8
        self.ovl1 = sbuf("ovl1", (128, 32768), U8)
        self.ovl2 = sbuf("ovl2", (128, 16384), U8)
        self.ovl3 = sbuf("ovl3", (128, 8192), U8)
        self.ovl4 = sbuf("ovl4", (128, 8192), U8)
        self.Rt = [sbuf("R%d" % i, (128, D)) for i in range(NTT)]
        self.big4 = [self._alias("big4_%d" % i, (128, 2048), BF16, self.ovl2, i * 4096)
                     for i in range(4)]
        self.wring = [sbuf("wring%d" % i, (128, 2048), BF16) for i in range(2)]
        self.x2s = sbuf("x2s", (128, 8 * 512), BF16)
        self.xln = [self._alias("xln%d" % i, (128, D), BF16, self.ovl4, i * 2048)
                    for i in range(2)]
        self.xTt = [self._alias("xTt%d" % i, (128, 128 * 8), BF16, self.ovl4,
                                4096 + i * 2048) for i in range(2)]
        self.ln_stats = [sbuf("lnst%d" % i, (128, 12)) for i in range(2)]
        self.ln_cols = [sbuf("lncol%d" % i, (128, 4)) for i in range(2)]
        self.c_i128 = sbuf("ci128", (128, 128), BF16)
        self.c_i64d = sbuf("ci64d", (128, 64), BF16)
        self.c_i96f = sbuf("ci96f", (96, 96))
        self.c_maskLd = sbuf("cmaskLd", (128, 64))
        self.c_maskUd = sbuf("cmaskUd", (128, 64))
        self.c_ones = sbuf("cones", (1, 128), BF16)
        self.c_fbias = sbuf("cfbias", (1, D), BF16)
        self.c_gbw = sbuf("cgbw", (128, 8 * 32), BF16)
        self.c_convw = sbuf("cconvw", (128, 96))
        self.c_convb = sbuf("cconvb", (128, 24))
        self.gc = self._alias("gc", (16, T), F32, self.ovl1, 0)
        self.lnbc = self._alias("lnbc", (16, T), F32, self.ovl1, 8192)
        self.lgc = self._alias("lgc", (16, T), F32, self.ovl1, 16384)
        self.zstack = self._alias("zstack", (96, T), F32, self.ovl1, 24576)
        self.ztmp = [self._alias("ztmp%d" % i, (16, 512), F32, self.ovl3,
                                 4096 + i * 2048) for i in range(2)]
        self.tcst = sbuf("tcst", (64, 96))
        self.tcols2 = [sbuf("tcols2_%d" % i, (128, 96)) for i in range(2)]
        self.wb = self._alias("wb", (128, 1024), F32, self.ovl3, 0)
        self.rawt = sbuf("rawt", (128, 3 + TSLAB))
        self.cacc = sbuf("cacc", (128, TSLAB))
        self.tails = sbuf("tails", (128, 8 * 3 * 3))
        self.g1s = [self._alias("g1s0", (4, 1024), BF16, self.cacc, 0),
                    self._alias("g1s1", (4, 1024), BF16, self.rawt, 0)]
        self.g2s = [self._alias("g2s%d" % i, (4, 256), BF16, self.x2s, i * 512)
                    for i in range(2)]
        self.qcs = self._alias("qcs", (128, 8 * TSLAB), BF16, self.ovl1, 0)
        self.kcs = self._alias("kcs", (128, 8 * TSLAB), BF16, self.ovl1, 8192)
        self.vcs = self._alias("vcs", (128, 8 * TSLAB), BF16, self.ovl1, 16384)
        self.mps2 = [sbuf("mps2_%d" % i, (128, 512), BF16) for i in range(2)]
        self.m32s = sbuf("m32s", (128, 2048), BF16)
        self.ams = [sbuf("ams%d" % i, (128, 512), BF16) for i in range(SLAB)]
        self.khat = [sbuf("khat%d" % i, (128, 512), BF16) for i in range(SLAB)]
        self.cc = [sbuf("cc%d" % i, (128, 80)) for i in range(SLAB)]
        self.lgrow = sbuf("lgrow", (128, 512))
        self.c_selb = sbuf("cselb", (16, 1024), BF16)
        self.lgT2 = sbuf("lgT2", (16, 64))
        self.lgs = sbuf("lgs", (16, 192), BF16)
        self.dgr = self._alias("dgr", (128, 512), BF16, self.ovl2, 2048)
        self.vtm = [sbuf("vtm%d" % i, (128, 512), BF16) for i in range(SLAB)]
        self.ksq = self._alias("ksq", (128, 512), F32, self.ovl4, 6144)
        self.msk = self._alias("msk", (128, 512), F32, self.ovl3, 4096)
        self.sstb = self._alias("sstb", (128, 4 * 256), BF16, self.ovl3, 4096)
        self.ktmb = self._alias("ktmb", (128, 512), BF16, self.ovl2, 4096)
        self.subtmp = self._alias("subtmp", (128, 4 * 16 * 15), BF16, self.ovl3, 6144)
        self.msk2 = self._alias("msk2", (128, 512), F32, self.ovl3, 6144)
        self.rbt = self._alias("rbt", (128, 512), F32, self.ovl4, 0)
        self.cst = sbuf("cst", (128, 3))
        self.c_i128f = sbuf("ci128f", (128, 128))
        self.oc = [sbuf("oc%d" % i, (128, 512), BF16) for i in range(2)]
        self.oT = self._alias("oT", (128, 8 * 128), BF16, self.ovl4, 2048)
        self.Sm = sbuf("Sm", (128, 512))
        self.Sb = sbuf("Sb", (128, 512), BF16)
        self.ubf = sbuf("ubf", (128, 512), BF16)
        self.rbf = sbuf("rbf", (128, 512), BF16)
        self.sst = self._alias("sst", (128, 4 * 256), F32, self.ovl1, 24576)
        self.tstr = self._alias("tstr", (128, 4 * 256), F32, self.ovl1, 28672)
        self.tstrT = self._alias("tstrT", (128, 4 * 256), F32, self.ovl3, 0)
        self.stripA = self._alias("stripA", (128, 2048), BF16, self.ovl2, 0)
        self.stripB = self._alias("stripB", (128, 2048), BF16, self.ovl2, 4096)
        self.t32t = self._alias("t32t", (128, 2048), BF16, self.ovl2, 8192)
        self.zstr = self._alias("zstr", (128, 2048), BF16, self.ovl2, 12288)
        self.pxs = sbuf("pxs", (128, 4096), BF16)
        self.pp = [nc.alloc_psum_tensor("pp%d" % i, [128, 512], F32) for i in range(6)]

    def bwt(self, j):
        return self.big4[j // 2][:, (j % 2) * 1024:(j % 2) * 1024 + 1024]

    def mm(self, out, lhsT, rhs, deps=(), start=True, stop=True, tp=None):
        def f(e, out=out, lhsT=lhsT, rhs=rhs, start=start, stop=stop, tp=tp):
            return e.matmul(out, lhsT, rhs, start=start, stop=stop, tile_position=tp)
        return self.s.emit("tensor", f, deps=deps, sig=ET)

    def dve(self, fn, deps=()):
        return self.s.emit("vector", fn, deps=deps, sig=EV)

    def act(self, fn, deps=()):
        return self.s.emit("scalar", fn, deps=deps, sig=EA)

    def pool(self, fn, deps=()):
        return self.s.emit("gpsimd", fn, deps=deps, sig=EP)

    def dma(self, q, out, in_, deps=()):
        s = self.s
        eng = {"qs": "sync", "qg": "gpsimd", "qa": "scalar"}[q]
        idx = s.quse[q] % Sched.NPOOL
        s.quse[q] += 1
        nm = s.qpool[q][idx]
        prev = s.cnt.get(nm, 0)
        if prev:
            s.wait(eng, (nm, prev))

        def f(e, out=out, in_=in_):
            return e.dma_start(out=out, in_=in_)
        return s.emit(eng, f, deps=deps, sig=nm, dma=True)

    def build(self, upto=9):
        self.p0_loads()
        if MK.SUB >= 10 and MK.SUB < 13:
            self.ev_bigw_done = None
        if upto >= 2:
            self.p_ln_ffn(1)
        if upto >= 3:
            self.p3_ln2()
        if upto >= 4:
            self.p4_gbeta()
        if upto >= 5:
            self.p5_delta(upto)
        if upto >= 9:
            self.p_ln_ffn(2)
        self.p9_out()
        self.p_debug()
        self.finalize()
        return self.nc

    def p0_loads(self):
        xr = self.x_in.rearrange("(n p) m -> n p m", p=128)
        for dst, src in ((self.c_i128, self.i128), (self.c_i64d, self.i64d),
                         (self.c_i96f, self.i96f), (self.c_i128f, self.i128f),
                         (self.c_maskLd, self.maskLd_in),
                         (self.c_selb, self.selb3_in),
                         (self.c_maskUd, self.maskUd_in),
                         (self.c_convw, self.convw_in), (self.c_convb, self.convb_in)):
            self.dma(QS, dst[:], src)
        self.ev_cs = self.s.now(QS)
        self.dma(QG, self.c_fbias[:], self.fbias)
        self.ev_R = None
        self.dma(QG, self.c_gbw[:].rearrange("p (i j) -> p i j", i=8),
                 self.wgbT.rearrange("(i p) j -> p i j", p=128))
        for t in range(2):
            self.dma(QG, self.g1s[t][:], self.g1p[t])
            self.dma(QG, self.g2s[t][:], self.g2p[t])
        self.ev_cg = self.s.now(QG)
        self.ev_R = [self.dma((QG, QA)[i % 2], self.Rt[i][:], xr[i])
                     for i in range(NTT)]
        self.dve(lambda e: e.memset(self.c_ones[:], 1.0))
        self.dve(lambda e: e.memset(self.cst[:, 0:1], EPS_LN))
        self.dve(lambda e: e.memset(self.cst[:, 1:2], 1.0))
        self.dve(lambda e: e.memset(self.cst[:, 2:3], LN_KD))
        self.dve(lambda e: e.memset(self.zstack[:], 0.0))
        self.dve(lambda e: e.memset(self.Sm[:], 0.0))
        self.dve(lambda e: e.memset(self.tails[:], 0.0))
        self.ev_zero = self.dve(lambda e: e.memset(self.Sb[:], 0.0))
        pass

    def p1_ffnw(self, extra=(), reload=False):
        s = self.s
        ev_wb = None
        if reload:
            for t in range(2):
                self.dma(QG, self.g1s[t][:], self.g1p[t], deps=extra)
                self.dma(QG, self.g2s[t][:], self.g2p[t], deps=extra)
            self.ev_cg = s.now(QG)
        ev_wb_t = [(), ()]
        for t in range(2):
            evd = []
            for mt in range(2):
                evs = []
                for nk in range(2):
                    ev_mm = self.mm(self.pp[nk][:, 0:512],
                                    self.g2s[t][:, mt * 128:(mt + 1) * 128],
                                    self.g1s[t][:, nk * 512:(nk + 1) * 512],
                                    deps=(self.ev_cg, ev_wb) + tuple(extra))
                    evs.append(self.dve(
                        lambda e, nk=nk: e.tensor_copy(
                            self.wb[:, nk * 512:(nk + 1) * 512], self.pp[nk][:, 0:512]),
                        deps=(ev_mm,)))
                ev_wb = self.dma((QS, QG)[mt],
                                 self.scr_w[t, mt * 128:(mt + 1) * 128, :],
                                 self.wb[:], deps=evs)
                evd.append(ev_wb)
            ev_wb_t[t] = tuple(evd)
        self.ev_bigw = []
        self.ev_piece = [None] * 8
        ev_cp = None
        qrr = (QS, QG, QA)
        for dt in range(4):
            for t in range(2):
                ev_in = []
                for d in range(8):
                    perm = AP(self.scr_w.tensor,
                              self.scr_w.offset + t * 262144 + (dt * 8 + d) * 32,
                              [[1024, 16], [16384, 16], [1, 32]])
                    ev_in.append(self.dma(
                        qrr[d % 3], self.wb[d * 16:(d + 1) * 16, 0:512], perm,
                        deps=ev_wb_t[0] + ev_wb_t[1]
                        + ((ev_cp,) if ev_cp else ())))
                ev_in = tuple(ev_in)
                src = self.wb[:, 0:512].rearrange("p (e b) -> p b e", e=16)
                if t == 0:
                    dsts = [(self.bwt(dt)[:, 0:512], 1.0),
                            (self.bwt(4 + dt)[:, 512:1024], 1.0)]
                else:
                    dsts = [(self.bwt(dt)[:, 512:1024], 1.0),
                            (self.bwt(4 + dt)[:, 0:512], -1.0)]
                for dst, sc in dsts:
                    ov = dst.rearrange("p (b e) -> p b e", b=32)
                    if sc == 1.0:
                        ev_cp = self.dve(lambda e, ov=ov, src=src: e.tensor_copy(
                            ov, src), deps=ev_in)
                    else:
                        ev_cp = self.dve(
                            lambda e, ov=ov, src=src: e.tensor_scalar_mul(ov, src, -1.0),
                            deps=ev_in)
                self.ev_bigw.append(ev_cp)
            self.ev_piece[dt] = ev_cp
            self.ev_piece[4 + dt] = ev_cp
        self.ev_bigw_done = ev_cp

    def _emit_ln(self, i, par, extra):
        st, co = self.ln_stats[par], self.ln_cols[par]
        ev1 = self.dve(lambda e, i=i: e.bn_stats(st[:, 0:6], self.Rt[i][:, 0:512]),
                       deps=extra)
        ev2 = self.dve(lambda e, i=i: e.bn_stats(st[:, 6:12], self.Rt[i][:, 512:1024]))
        ev3 = self.dve(lambda e: e.bn_aggr(co[:, 0:2], st[:, 0:12]), deps=(ev1, ev2))
        if not hasattr(self, "ln_rd"):
            self.ln_rd = [(), ()]
        ev4 = self.act(lambda e: e.activation(co[:, 2:3], co[:, 1:2], AF.Ln,
                                              bias=self.cst[:, 0:1]),
                       deps=(ev3,) + tuple(self.ln_rd[par]))
        ev5 = self.act(lambda e: e.activation(co[:, 2:3], co[:, 2:3], AF.Exp,
                                              scale=-0.5), deps=(ev4,))
        ev7 = self.dve(lambda e: e.scalar_tensor_tensor(
            co[:, 3:4], co[:, 0:1], -1.0, co[:, 2:3], ALU.mult, ALU.mult),
            deps=(ev5,))
        self.ln_rd[par] = (ev7,)
        return self.act(lambda e, i=i, par=par: e.activation(
            self.xln[par][:], self.Rt[i][:], AF.Identity, bias=co[:, 3:4],
            scale=co[:, 2:3]), deps=(ev7, ev5))

    SUB = 3
    NTILES = NTT

    def p_ln_ffn(self, which):
        s = self.s
        evs = [None]
        if which == 1:
            pre = [self.ev_cs]
            self.ev_ffn = [None] * NTT
        else:
            pre = s.barrier()
        sub = MK.SUB if MK.SUB < 10 else MK.SUB - 10
        ev_ffn_prev = [None, None]
        ev_tp_prev = [None] * 8
        nt = min(NTT, MK.NTILES)

        def emit_tile_ln(i):
            par = i % 2
            extra = list(pre) + ([self.ev_R[i]] if which == 1 else [])
            if MK.SUB >= 10:
                return self.act(lambda e, i=i, par=par: e.activation(
                    self.xln[par][:], self.Rt[i][:], AF.Copy), deps=extra)
            return self._emit_ln(i, par, extra)

        ap_pend = {0: emit_tile_ln(0)} if nt else {}
        if nt > 1:
            ap_pend[1] = emit_tile_ln(1)
        if MK.SUB < 10 or MK.SUB >= 13:
            if which == 1:
                self.p1_ffnw()
            else:
                self.p1_ffnw(extra=tuple(s.barrier()), reload=True)
        for i in range(nt):
            par = i % 2
            ev_ap = ap_pend.pop(i)
            if i + 1 < nt and i + 1 not in ap_pend:
                ap_pend[i + 1] = emit_tile_ln(i + 1)
            if sub < 2:
                continue
            ev_tp = []
            slot_ev = [ev_tp_prev[6], ev_tp_prev[7]]
            for j in range(8):
                pslc = self.pp[2 + (j % 2)][:, 0:128]
                ev_mm = self.mm(pslc, self.xln[par][:, j * 128:(j + 1) * 128],
                                self.c_i128[:],
                                deps=(ev_ap, self.ev_cs, slot_ev[j % 2]))
                ev_e = self.act(
                    lambda e, j=j, par=par, pslc=pslc: e.activation(
                        self.xTt[par][:, j * 128:(j + 1) * 128], pslc, AF.Copy),
                    deps=(ev_mm,))
                ev_tp.append(ev_e)
                slot_ev[j % 2] = ev_e
            ev_tp_prev = ev_tp
            if sub < 3:
                continue
            evs = []
            for half in range(2):
                pb = self.pp[half]
                dep0 = [ev_tp[j] for j in range(8)] + [ev_ffn_prev[half]]
                ev_last = None
                for j in range(8):
                    ev_last = self.mm(
                        pb[:, 0:512], self.xTt[par][:, j * 128:(j + 1) * 128],
                        self.bwt(j)[:, half * 512:(half + 1) * 512],
                        deps=(tuple(dep0) + (self.ev_piece[j],)) if j == 0
                        else (self.ev_piece[j],), start=(j == 0), stop=False)
                ev_last = self.mm(
                    pb[:, 0:512], self.c_ones[0:1, :],
                    self.c_fbias[0:1, half * 512:(half + 1) * 512],
                    deps=(self.ev_cg,), start=False, stop=True)
                evs.append(self.dve(
                    lambda e, pb=pb, i=i, half=half: e.scalar_tensor_tensor(
                        self.Rt[i][:, half * 512:(half + 1) * 512], pb[:, 0:512], 0.5,
                        self.Rt[i][:, half * 512:(half + 1) * 512], ALU.mult, ALU.add),
                    deps=(ev_last,)))
            ev_ffn_prev = evs
            if which == 1:
                self.ev_ffn[i] = evs[-1]
        self.ev_ffn_last = evs

    def p3_ln2(self):
        s = self.s
        ev_t = [None, None]
        ev_dump = [None, None]
        self.ev_x2t = []
        ap_pend = {0: self._emit_ln(0, 0, (self.ev_ffn[0],))}
        for i in range(NTT):
            par = i % 2
            ev_ap = ap_pend.pop(i)
            if i + 1 < NTT:
                ap_pend[i + 1] = self._emit_ln(i + 1, (i + 1) % 2,
                                               (self.ev_ffn[i + 1],))
            for j in range(8):
                pslc = self.pp[4 + (j % 2)][:, 0:128]
                ev_mm = self.mm(pslc, self.xln[par][:, j * 128:(j + 1) * 128],
                                self.c_i128[:], deps=(ev_ap, ev_t[j % 2]))
                ev_t[j % 2] = self.act(
                    lambda e, j=j, par=par, pslc=pslc: e.activation(
                        self.xTt[par][:, j * 128:(j + 1) * 128], pslc, AF.Copy),
                    deps=(ev_mm, ev_dump[par]))
            ev_dump[par] = self.dma(
                QS, self.scr_x2t[:, :, i * 128:(i + 1) * 128].rearrange("a p c -> p a c"),
                self.xTt[par][:].rearrange("p (a c) -> p a c", a=8),
                deps=(ev_t[0], ev_t[1]))
            self.ev_x2t.append(ev_dump[par])
        self.ev_x2t_all = tuple(self.ev_x2t)
        self.ev_x2t_done = self.ev_x2t[-1]

    P4SUB = 4
    P4OPS = 4
    P5SUB = 4
    CHSER = 0

    def p4_gbeta(self):
        s = self.s
        evz = []
        for sl4 in range(4):
            ev_ld = self.dma(QS, self.x2s[:],
                             self.scr_x2t[:, :, sl4 * 512:(sl4 + 1) * 512]
                             .rearrange("a p c -> p a c"),
                             deps=tuple(self.ev_x2t[sl4 * 4:sl4 * 4 + 4]) + (s.now(ET),))
            for gb in range(2):
                pb = self.pp[gb]
                ev_last = None
                for j in range(8):
                    ev_last = self.mm(
                        pb[0:16, 0:512],
                        self.c_gbw[:, j * 32 + gb * 16: j * 32 + gb * 16 + 16],
                        self.x2s[:, j * 512:(j + 1) * 512],
                        deps=(ev_ld, self.ev_cg, s.now(EV), s.now(EA)) if j == 0 else (),
                        start=(j == 0), stop=(j == 7))
                plane = self.gc if gb == 0 else self.lnbc
                za, zb2 = self.ztmp
                if MK.P4SUB < 1:
                    continue
                e1 = self.act(lambda e, pb=pb: e.activation(za[:], pb[0:16, 0:512],
                                                            AF.Abs), deps=(ev_last,))
                e1 = self.act(lambda e: e.activation(za[:], za[:], AF.Exp, scale=-1.0),
                              deps=(e1,))
                e1 = self.act(lambda e: e.activation(za[:], za[:], AF.Ln,
                                                     bias=self.cst[0:16, 1:2]),
                              deps=(e1,))
                e3 = e1
                if MK.P4OPS >= 4:
                    e4 = self.dve(lambda e, pb=pb: e.tensor_scalar_min(
                        zb2[:], pb[0:16, 0:512], 0.0), deps=(ev_last, e1))
                else:
                    e4 = self.dve(lambda e, pb=pb: e.tensor_copy(zb2[:], za[:]),
                                  deps=(e1,))
                evz.append(self.dve(lambda e, plane=plane, sl4=sl4: e.tensor_sub(
                    plane[:, sl4 * 512:(sl4 + 1) * 512], zb2[:], za[:]),
                    deps=(e3, e4, s.now(EV))))
        if MK.P4SUB < 2:
            self.ev_tcols = None
            self.ev_lgflat = None
            return
        ev_cs = []
        for ch in range(NCH):
            ev_cs.append(self.dve(lambda e, ch=ch: e.tensor_tensor_scan(
                self.lgc[:, ch * 64:(ch + 1) * 64], self.gc[:, ch * 64:(ch + 1) * 64],
                self.gc[:, ch * 64:(ch + 1) * 64], 0.0, ALU.add, ALU.bypass),
                deps=evz if ch == 0 else ()))
        ez1 = self.dve(lambda e: e.tensor_copy(self.zstack[0:16, :], self.gc[:]),
                       deps=tuple(ev_cs))
        ez2 = self.dve(lambda e: e.tensor_copy(self.zstack[32:48, :], self.lnbc[:]),
                       deps=(ez1,))
        ez3 = self.dve(lambda e: e.tensor_copy(self.zstack[64:80, :], self.lgc[:]),
                       deps=(ez2,))
        if MK.P4SUB < 3:
            self.ev_tcols = None
            self.ev_lgflat = None
            return
        evt = None
        ev_tc = None
        for ch in range(NCH):
            pv = self.pp[ch % 2][0:64, 0:96]
            ev_mm = self.mm(pv, self.zstack[:, ch * 64:(ch + 1) * 64], self.c_i96f[:],
                            deps=(ez1, ez2, ez3, evt, self.ev_cs, ev_tc))
            evt = self.dve(lambda e, pv=pv: e.tensor_copy(self.tcst[:], pv),
                           deps=(ev_mm, ev_tc))
            ev_tc = self.dma(QS, self.scr_tc[ch], self.tcst[:], deps=(evt,))
        self.ev_tcols = ev_tc
        if MK.P4SUB < 4:
            self.ev_lgflat = None
            return
        self.ev_lgflat = self.dma(QS, self.scr_flat[0], self.lgc[:], deps=(ev_cs[-1],))

    def p5_delta(self, upto):
        s = self.s
        self.ev_S = [self.ev_zero] * H
        self.ev_wo_R = None
        self.ev_oc_free = None
        nslab = NSLAB if upto >= 6 else 1
        self.nslab_built = nslab
        for sl in range(nslab):
            self._slab_qkv(sl)
            if MK.P5SUB >= 2:
                self._slab_prep(sl)
            if MK.P5SUB >= 3:
                self._slab_solve(sl)
            if MK.P5SUB >= 4:
                self._slab_chain(sl)
    def _slab_qkv(self, sl):
        s = self.s
        ev_x2 = self.dma(QS, self.x2s[:],
                         self.scr_x2t[:, :, sl * 512:(sl + 1) * 512]
                         .rearrange("a p c -> p a c"),
                         deps=tuple(self.ev_x2t[sl * 4:sl * 4 + 4]) + (s.now(ET),))
        strips = {"q": self.qcs, "k": self.kcs, "v": self.vcs}
        wbase = {"q": 0, "k": 4, "v": 8}
        self.ev_strip = {}
        ev_conv_last = None
        for wi, w in enumerate("qkv"):
            wsrc = self.wT[w].rearrange("(j p) c -> p j c", p=128)
            ev_w = {}
            for ct in range(8):
                qt = ct // 2
                slot = self.wring[qt % 2]
                if ct % 2 == 0:
                    ev_w[qt] = self.dma(
                        QG, slot[:].rearrange("p (j cc) -> p j cc", j=8),
                        wsrc[:, :, qt * 256:(qt + 1) * 256],
                        deps=(s.now(ET),))
                pb = self.pp[ct % 2]
                dep0 = (ev_x2, ev_w[qt], s.now(EV), s.now(EP), s.now(EA))
                ev_last = None
                for j in range(8):
                    c0 = j * 256 + (ct % 2) * 128
                    ev_last = self.mm(
                        pb[:, 0:512], slot[:, c0:c0 + 128],
                        self.x2s[:, j * 512:(j + 1) * 512],
                        deps=dep0 if j == 0 else (), start=(j == 0), stop=(j == 7))
                tl = self.tails[:, (wi * 8 + ct) * 3:(wi * 8 + ct) * 3 + 3]
                dpre = [ev_conv_last]
                if wi == 0 and ct == 0:
                    dpre += [s.now(ET), self.ev_lgflat, self.ev_tcols,
                             s.now(EV), s.now(EA)]
                e_pref = self.dve(lambda e, tl=tl: e.tensor_copy(self.rawt[:, 0:3], tl),
                                  deps=dpre)
                e_raw = self.act(lambda e, pb=pb: e.activation(
                    self.rawt[:, 3:3 + 512], pb[:, 0:512], AF.Copy),
                    deps=(ev_last,))
                self.dve(lambda e, tl=tl: e.tensor_copy(tl, self.rawt[:, 512:515]),
                         deps=(e_raw, e_pref))

                def wc(j_, ct=ct, w=w):
                    cix = ct * 12 + wbase[w] + j_
                    return self.c_convw[:, cix:cix + 1]
                bc = self.c_convb[:, ct * 3 + wi:ct * 3 + wi + 1]
                e_c0 = self.act(lambda e, wc=wc, bc=bc: e.activation(
                    self.cacc[:], self.rawt[:, 3:515], AF.Identity, bias=bc, scale=wc(3)),
                    deps=(e_pref, e_raw, self.ev_cs))
                e_c1 = self.dve(lambda e, wc=wc: e.scalar_tensor_tensor(
                    self.cacc[:], self.rawt[:, 2:514], wc(2), self.cacc[:],
                    ALU.mult, ALU.add), deps=(e_c0,))
                e_c2 = self.dve(lambda e, wc=wc: e.scalar_tensor_tensor(
                    self.cacc[:], self.rawt[:, 1:513], wc(1), self.cacc[:],
                    ALU.mult, ALU.add), deps=(e_c1,))
                e_c3 = self.dve(lambda e, wc=wc: e.scalar_tensor_tensor(
                    self.cacc[:], self.rawt[:, 0:512], wc(0), self.cacc[:],
                    ALU.mult, ALU.add), deps=(e_c2,))
                ev_sp = self.act(lambda e, w=w, ct=ct, strips=strips: e.activation(
                    strips[w][:, ct * 512:(ct + 1) * 512], self.cacc[:], AF.Copy),
                    deps=(e_c3,))
                self.ev_strip[(w, ct)] = ev_sp
                ev_conv_last = ev_sp
        self.ev_strip_done = ev_conv_last

    def _kc(self, h, chl, strip):
        return strip[hr(h):hr(h) + 64,
                     (h // 2) * 512 + chl * 64:(h // 2) * 512 + chl * 64 + 64]

    def ccv(self, chl, slot):
        return self.cc[chl][:].rearrange("p (g s) -> p g s", s=10)[:, :, slot]

    def ccp(self, chl, slot, par):
        return self.cc[chl][par * 64:par * 64 + 64].rearrange(
            "p (g s) -> p g s", s=10)[:, :, slot]

    def ccol(self, chl, slot, h):
        pa = h // 2
        r0 = hr(h)
        return self.cc[chl][r0:r0 + 64, pa * 10 + slot:pa * 10 + slot + 1]

    def _slab_prep(self, sl):
        s = self.s
        self.ev_prep = []
        if not hasattr(self, "ev_mp_out"):
            self.ev_mp_out = [(), ()]
            self.ev_lgrd = ()
            self.ev_tc2rd = [(), ()]
            self.ev_ppK_rd = None
            self.ev_ppV_rd = None
            self.ev_kc_prev = None
            self.ev_bc_prev = None
            self.ev_lgt_rd = ()
            self.ev_ktm_rd = ()
            self.ev_ksq_rd = None
        bd = getattr(self, "bankdep", {})
        ppK, ppV = self.pp[2], self.pp[3]
        for chl in range(SLAB):
            ch = sl * SLAB + chl
            ev_tc2a = self.dma(QA, self.tcols2[chl % 2][0:64, :], self.scr_tc[ch],
                               deps=(self.ev_tcols,) + tuple(self.ev_tc2rd[chl % 2]))
            ev_tc2 = (ev_tc2a, self.dma(
                QA, self.tcols2[chl % 2][64:128, :].rearrange(
                    "p (b c) -> p b c", c=32)[:, :, 0:8],
                AP(self.scr_tc.tensor, self.scr_tc.offset + ch * 64 * 96 + 8,
                   [[96, 64], [32, 3], [1, 8]])))
            ev_lgt = self.dma(QA, self.lgT2[:],
                              self.scr_flat[0][:, ch * 64:ch * 64 + 64],
                              deps=(self.ev_lgflat,) + tuple(self.ev_lgt_rd))
            e_h = self.dve(lambda e: e.tensor_copy(self.lgs[:, 0:64], self.lgT2[:]),
                           deps=(ev_lgt, self.ev_bc_prev))
            e_h = self.dve(lambda e: e.tensor_sub(
                self.lgT2[:], self.lgT2[:], self.lgs[:, 0:64]), deps=(e_h,))
            e_h = self.dve(lambda e: e.tensor_copy(self.lgs[:, 64:128], self.lgT2[:]),
                           deps=(e_h,))
            e_h = self.dve(lambda e: e.tensor_sub(
                self.lgT2[:], self.lgT2[:], self.lgs[:, 64:128]), deps=(e_h,))
            e_h = self.dve(lambda e: e.tensor_copy(self.lgs[:, 128:192], self.lgT2[:]),
                           deps=(e_h,))
            self.ev_lgt_rd = (e_h,)
            ev_bc = None
            for pa in range(8):
                for k3 in range(3):
                    ev_bc = self.mm(
                        ppV[:, pa * 64:pa * 64 + 64],
                        self.c_selb[:, pa * 128:(pa + 1) * 128],
                        self.lgs[:, k3 * 64:k3 * 64 + 64],
                        deps=(e_h, self.ev_ppV_rd, bd.get("pO2"))
                        if pa == 0 and k3 == 0 else (),
                        start=(k3 == 0), stop=(k3 == 2))
            self.ev_bc_prev = ev_bc
            ev_lgrow = self.act(lambda e: e.activation(self.lgrow[:], ppV[:], AF.Copy),
                                deps=(ev_bc,) + tuple(self.ev_lgrd))
            ev_lg = (ev_lgrow,)
            ev_kt = []
            for h in range(H):
                r0 = hr(h)
                dep = (self.ev_strip[("k", h // 2)],)
                if h == 0:
                    dep += (self.ev_ppK_rd, bd.get("pO1"))
                ev_kt.append(self.mm(
                    ppK[r0:r0 + 64, (h // 2) * 64:(h // 2) * 64 + 64],
                    self._kc(h, chl, self.kcs), self.c_i64d[r0:r0 + 64, :],
                    deps=dep, tp=(r0, r0)))
            ev_ktmb = self.dve(
                lambda e: e.tensor_copy(self.ktmb[:], ppK[:]),
                deps=(ev_kt[-1], s.now(ET)) + tuple(self.ev_ktm_rd) if chl == 0
                else (ev_kt[-1], self.ev_kc_prev) + tuple(self.ev_ktm_rd))
            ev_sq = self.act(lambda e: e.activation(self.ksq[:], ppK[:], AF.Square),
                             deps=(ev_ktmb, self.ev_ksq_rd))
            ev_ss = self.dve(lambda e, chl=chl: e.reduce_sum(
                self.ccv(chl, 0), self.ksq[:].rearrange("p (g x) -> p g x", g=8),
                mybir.AxisListType.X), deps=(ev_sq, s.now(EV), s.now(EA)))
            self.ev_ksq_rd = ev_ss
            ev_ln = self.act(lambda e, chl=chl: e.activation(
                self.ccv(chl, 8), self.ccv(chl, 0), AF.Ln), deps=(ev_ss,))
            ev_lnrn = self.dve(lambda e, chl=chl: e.tensor_scalar(
                self.ccv(chl, 0), self.ccv(chl, 8), -0.5, LNRN_MAX,
                ALU.mult, ALU.min), deps=(ev_ln, ev_ss))
            ev_rn = self.act(lambda e, chl=chl: e.activation(
                self.ccv(chl, 9), self.ccv(chl, 0), AF.Exp), deps=(ev_lnrn,))
            ev_eL = self.act(lambda e, chl=chl: e.activation(
                self.ccv(chl, 5), self.lgrow[:].rearrange(
                    "p (g x) -> p g x", g=8)[:, :, 63], AF.Exp),
                deps=ev_lg + (ev_lnrn,))
            tc = lambda c0, chl=chl: self.tcols2[chl % 2][:, c0:c0 + 8]
            cp = lambda sl_, chl=chl: self.ccv(chl, sl_)
            ee1 = self.dve(lambda e, tc=tc, cp=cp: e.scalar_tensor_tensor(
                cp(8), tc(0), -1.0, tc(64), ALU.mult, ALU.add),
                deps=ev_tc2 + (ev_lnrn,))
            ev_m = self.dve(lambda e, tc=tc, cp=cp: e.tensor_add(
                cp(1), cp(8), tc(32)), deps=(ee1,))
            ev_e1 = self.act(lambda e, cp=cp: e.activation(cp(8), cp(1), AF.Exp),
                             deps=(ev_m,))
            ee3 = self.dve(lambda e, cp=cp: e.tensor_scalar_mul(cp(2), cp(8), -1.0),
                           deps=(ev_e1,))
            e_b = self.act(lambda e, tc=tc, cp=cp: e.activation(
                cp(3), tc(32), AF.Exp), deps=(ev_lnrn,))
            e_k = self.act(lambda e, tc=tc, cp=cp: e.activation(
                cp(6), tc(64), AF.Exp, bias=self.cst[:, 2:3]), deps=(ev_lnrn,))
            ev_E = self.dve(lambda e, tc=tc, cp=cp: e.tensor_scalar_mul(
                cp(4), tc(64), -1.0), deps=ev_tc2 + (ev_lnrn,))
            Lv = self.lgrow[:].rearrange("p (g x) -> p g x", g=8)[:, :, 63]
            ee4 = self.dve(lambda e, tc=tc, cp=cp: e.tensor_sub(
                cp(8), cp(0), tc(64)), deps=(ee3,))
            e_t = self.dve(lambda e, cp=cp, Lv=Lv: e.tensor_add(cp(8), cp(8), Lv),
                           deps=(ee4,) + ev_lg)
            ev_kh = self.act(lambda e, cp=cp: e.activation(cp(7), cp(8), AF.Exp),
                             deps=(e_t,))
            ev_ms = (ev_m,)
            ev_Es = (ev_E,)
            ev_khs = (ev_kh,)
            ev_ts = (e_t,)
            rdt = [ee1, ev_m, e_b, e_k, ev_E, ee4]
            ev_khat = self.dve(lambda e, chl=chl: e.tensor_mul(
                self.khat[chl][:].rearrange("p (g x) -> p g x", g=8),
                self.ktmb[:].rearrange("p (g x) -> p g x", g=8),
                self.ccv(chl, 7)[:, :, None].broadcast_to((128, 8, 64))),
                deps=tuple(ev_khs) + (ev_ktmb,))
            self.ev_ktm_rd = (ev_khat,)
            ev_dgr = self.dve(lambda e, chl=chl: e.tensor_mul(
                self.dgr[:].rearrange("p (g x) -> p g x", g=8),
                self.c_i64d[:][:, None, :].broadcast_to((128, 8, 64)),
                self.ccv(chl, 9)[:, :, None].broadcast_to((128, 8, 64))),
                deps=(ev_rn, self.ev_kc_prev if chl else s.now(ET)))
            ev_kc = None
            for h in range(H):
                pa = h // 2
                r0 = hr(h)
                ev_kc = self.mm(
                    ppK[r0:r0 + 64, pa * 64:pa * 64 + 64],
                    self.ktmb[r0:r0 + 64, pa * 64:pa * 64 + 64],
                    self.dgr[r0:r0 + 64, pa * 64:pa * 64 + 64],
                    deps=(ev_dgr, ev_ktmb, ev_sq) if h == 0 else (), tp=(r0, r0))
            self.ev_kc_prev = ev_kc
            ev_kce = self.dve(lambda e, chl=chl: e.tensor_copy(
                self.kcs[:].rearrange("p (g t) -> p g t", g=8)
                [:, :, chl * 64:chl * 64 + 64],
                ppK[:].rearrange("p (g x) -> p g x", g=8)), deps=(ev_kc,))
            ev_vt = []
            for h in range(H):
                r0 = hr(h)
                dep = (self.ev_strip[("v", h // 2)],)
                if h == 0:
                    dep += (ev_lgrow, self.ev_ppV_rd)
                ev_vt.append(self.mm(
                    ppV[r0:r0 + 64, (h // 2) * 64:(h // 2) * 64 + 64],
                    self._kc(h, chl, self.vcs), self.c_i64d[r0:r0 + 64, :],
                    deps=dep, tp=(r0, r0)))
            ev_vtm = self.dve(lambda e, chl=chl: e.tensor_copy(
                self.vtm[chl][:], ppV[:]), deps=(ev_vt[-1],))
            ev_g = ev_a = None
            for h in range(H):
                r0 = hr(h)
                ev_g = self.mm(
                    ppK[r0:r0 + 64, (h // 2) * 64:(h // 2) * 64 + 64],
                    self._kc(h, chl, self.kcs), self._kc(h, chl, self.kcs),
                    deps=(ev_kce,) if h == 0 else (), tp=(r0, r0))
            for h in range(H):
                r0 = hr(h)
                ev_a = self.mm(
                    ppV[r0:r0 + 64, (h // 2) * 64:(h // 2) * 64 + 64],
                    self._kc(h, chl, self.kcs), self._kc(h, chl, self.qcs),
                    deps=(ev_vtm,) if h == 0 else (), tp=(r0, r0))
            mskt = self.msk if chl % 2 == 0 else self.rbt
            mskv = lambda mskt=mskt: mskt[:].rearrange("p (g x) -> p g x", g=8)
            msk2v = lambda: self.msk2[:].rearrange("p (g x) -> p g x", g=8)
            lgv = lambda: self.lgrow[:].rearrange("p (g x) -> p g x", g=8)
            mb = self.ccv(chl, 1)[:, :, None].broadcast_to((128, 8, 64))
            ecb = self.ccv(chl, 4)[:, :, None].broadcast_to((128, 8, 64))
            mLv = self.c_maskLd[:][:, None, :].broadcast_to((128, 8, 64))
            mUv = self.c_maskUd[:][:, None, :].broadcast_to((128, 8, 64))
            e1 = self.dve(lambda e, mskv=mskv, mb=mb: e.scalar_tensor_tensor(
                mskv(), lgv(), -1.0, mb, ALU.mult, ALU.add),
                deps=ev_lg + tuple(ev_ms) + (s.now(EV),))
            e3 = self.dve(lambda e, mskv=mskv, mLv=mLv: e.tensor_add(
                mskv(), mskv(), mLv), deps=(e1,))
            e4 = self.act(lambda e, mskt=mskt: e.activation(mskt[:], mskt[:], AF.Exp),
                          deps=(e3,))
            e5 = self.dve(lambda e, chl=chl, mskt=mskt, ppK=ppK: e.tensor_mul(
                self.mps2[chl % 2][:], mskt[:], ppK[:]),
                deps=(e4, ev_g) + tuple(self.ev_mp_out[chl % 2]))
            self.ev_ppK_rd = e5
            f1 = self.dve(lambda e, msk2v=msk2v, ecb=ecb: e.tensor_add(
                msk2v(), lgv(), ecb), deps=ev_lg + tuple(ev_Es) + (s.now(EV),))
            f2 = self.dve(lambda e, msk2v=msk2v, mUv=mUv: e.tensor_add(
                msk2v(), msk2v(), mUv), deps=(f1,))
            f3 = self.act(lambda e: e.activation(self.msk2[:], self.msk2[:], AF.Exp),
                          deps=(f2,))
            f4 = self.dve(lambda e, chl=chl, ppV=ppV: e.tensor_mul(
                self.ams[chl][:], self.msk2[:], ppV[:]), deps=(f3, ev_a))
            self.ev_ppV_rd = f4
            ev_mpa = self.dma(QA, self.scr_mp[0, ch].rearrange("pa s r -> s pa r"),
                              self.mps2[chl % 2][0:64, :].rearrange(
                                  "s (pa r) -> s pa r", pa=8), deps=(e5,))
            ev_mp = (ev_mpa,
                     self.dma(QG, self.scr_mp[1, ch].rearrange("pa s r -> s pa r"),
                              self.mps2[chl % 2][64:128, :].rearrange(
                                  "s (pa r) -> s pa r", pa=8), deps=(e5,)))
            self.ev_mp_out[chl % 2] = ev_mp
            self.ev_lgrd = (e1, f1, ev_eL) + tuple(ev_ts)
            self.ev_tc2rd[chl % 2] = tuple(rdt)
            self.ev_prep.append((ev_mp, f4, ev_khat, ev_vtm))

    def _slab_solve(self, sl):
        s = self.s
        deps = sum((tuple(p[0]) for p in self.ev_prep), ()) + (
            s.now(ET), s.now(EV), s.now(EP))
        SMP = self.scr_mp
        rs = 512
        cs = 128 * rs
        base = sl * SLAB * cs
        CST, PST = NCH * 8 * 64 * 64, 8 * 64 * 64
        b0 = sl * SLAB * PST
        ev_g1 = []
        for par in range(2):
            for jb in range(4):
                ev_g1.append(self.dma(
                    QS, self.sstb[par * 64:par * 64 + 64,
                                  jb * 256:(jb + 1) * 256].rearrange(
                        "p (r c) -> p r c", r=16),
                    AP(SMP.tensor, SMP.offset + par * CST + b0 + jb * 1040,
                       [[4096, 64], [64, 16], [1, 16]]), deps=deps))
        ev_g1 = tuple(ev_g1)
        ev_g2 = []
        for par in range(2):
            for a in range(2):
                ev_g2.append(self.dma(
                    QS, self.stripA[par * 64:par * 64 + 16, :].rearrange(
                        "r (q a2 c) -> r q a2 c", a2=2, c=16)[:, :, a],
                    AP(SMP.tensor, SMP.offset + par * CST + b0
                       + (16 + 32 * a) * 64 + 32 * a,
                       [[64, 16], [4096, 64], [1, 16]]), deps=deps))
        ev_g2 = tuple(ev_g2)
        ev_i0 = self.dve(lambda e: e.memset(self.tstr[:], 0.0),
                         deps=(s.now(ET), s.now(QG)))
        ev_i1 = self.dve(lambda e: e.memset(
            self.tstr[:].rearrange("p (b r c) -> p b r c", b=4, r=16)
            .rearrange("p b r c -> p b r c")[:, :, 0, 0] if False else
            AP(self.tstr.ap().tensor, self.tstr.ap().offset,
               [self.tstr.ap().ap[0], [256, 4], [17, 16]]), 1.0), deps=(ev_i0,))
        ev_i2 = self.dve(lambda e: e.tensor_copy(self.sst[:], self.sstb[:]),
                         deps=ev_g1)
        ev_pr = (ev_i1, ev_i2)
        for r in range(1, 16):
            def fmul(e, r=r):
                tv = AP(self.tstr.ap().tensor, self.tstr.ap().offset,
                        [self.tstr.ap().ap[0], [256, 4], [1, 16], [16, r]])
                mv = AP(self.sst.ap().tensor, self.sst.ap().offset + r * 16,
                        [self.sst.ap().ap[0], [256, 4], [0, 16], [1, r]])
                ov = AP(self.subtmp.ap().tensor, self.subtmp.ap().offset,
                        [self.subtmp.ap().ap[0], [240, 4], [15, 16], [1, r]])
                return e.tensor_mul(ov, tv, mv)
            e1 = self.dve(fmul, deps=ev_pr)
            def fred(e, r=r):
                iv = AP(self.subtmp.ap().tensor, self.subtmp.ap().offset,
                        [self.subtmp.ap().ap[0], [240, 4], [15, 16], [1, r]])
                ov = AP(self.cacc.ap().tensor, self.cacc.ap().offset,
                        [self.cacc.ap().ap[0], [16, 4], [1, 16], [0, 1]])
                return e.reduce_sum(ov, iv, mybir.AxisListType.X)
            e2 = self.dve(fred, deps=(e1,))
            def fsub(e, r=r):
                rv = AP(self.tstr.ap().tensor, self.tstr.ap().offset + r * 16,
                        [self.tstr.ap().ap[0], [256, 4], [1, 16]])
                sv = AP(self.cacc.ap().tensor, self.cacc.ap().offset,
                        [self.cacc.ap().ap[0], [16, 4], [1, 16]])
                return e.tensor_sub(rv, rv, sv)
            ev_pr = (self.dve(fsub, deps=(e2,)),)
        ev_sub = (s.now(EV), s.now(EP))
        ev_tt = self.dve(lambda e: e.tensor_copy(
            self.tstrT[:].rearrange("p (b x y) -> p b x y", b=4, x=16),
            self.tstr[:].rearrange("p (b y x) -> p b x y", b=4, y=16)), deps=ev_sub)
        ev_d1 = self.dma(QS, self.scr_t16[sl],
                         self.tstr[:].rearrange("p (b x) -> p b x", b=4), deps=ev_sub)
        ev_d2 = self.dma(QS, self.scr_t16t[sl],
                         self.tstrT[:].rearrange("p (b x) -> p b x", b=4), deps=(ev_tt,))
        ev_z = self.dve(lambda e: e.memset(self.t32t[:], 0.0), deps=(s.now(ET),))
        ts = 4 * 256
        T16T = self.scr_t16t
        T16 = self.scr_t16
        lands = []
        for par in range(2):
            for a in range(2):
                p0 = par * 64 + a * 32
                dst = self.t32t[p0:p0 + 16, :].rearrange(
                    "r (q c) -> r q c", c=32)[:, :, 0:16]
                lands.append(self.dma(
                    QG, dst,
                    AP(T16T.tensor,
                       T16T.offset + sl * 128 * ts + par * 64 * ts + a * 512,
                       [[16, 16], [ts, 64], [1, 16]]),
                    deps=(ev_d2, ev_z)))
                dst = self.t32t[p0 + 16:p0 + 32, :].rearrange(
                    "r (q c) -> r q c", c=32)[:, :, 16:32]
                lands.append(self.dma(
                    QG, dst,
                    AP(T16T.tensor,
                       T16T.offset + sl * 128 * ts + par * 64 * ts + a * 512 + 256,
                       [[16, 16], [ts, 64], [1, 16]]),
                    deps=(ev_d2, ev_z)))
        for par in range(2):
            p0 = par * 64
            for a in range(2):
                dst = self.stripA[p0 + 32:p0 + 48, :].rearrange(
                    "r (q a2 c) -> r q a2 c", a2=2, c=16)[:, :, a]
                lands.append(self.dma(
                    QG, dst,
                    AP(T16.tensor,
                       T16.offset + sl * 128 * ts + par * 64 * ts + a * 512,
                       [[16, 16], [ts, 64], [1, 16]]),
                    deps=(ev_d1,) + ev_g2))
                dst = self.stripB[p0:p0 + 16, :].rearrange(
                    "r (q a2 c) -> r q a2 c", a2=2, c=16)[:, :, a]
                lands.append(self.dma(
                    QG, dst,
                    AP(T16T.tensor,
                       T16T.offset + sl * 128 * ts + par * 64 * ts + a * 512 + 256,
                       [[16, 16], [ts, 64], [1, 16]]),
                    deps=(ev_d2,)))
            lands.append(self.dma(
                QS, self.m32s[p0 + 32:p0 + 64, :].rearrange(
                    "i (q j) -> i q j", j=32),
                AP(SMP.tensor, SMP.offset + par * CST + b0 + 32 * 64,
                   [[64, 32], [4096, 64], [1, 32]]),
                deps=(deps + (s.now(ET),)) if par == 0 else ()))
        ev_land = tuple(lands) + ev_g2
        ev_x16 = []
        ev_ev = (s.now(EV),)
        for par in range(2):
            r0 = par * 64
            for a in range(2):
                for qh in range(2):
                    ev_mmb = []
                    for qq in range(32):
                        q = qh * 32 + qq
                        col = (q * 2 + a) * 16
                        d0 = ()
                        if qq == 0:
                            d0 = (ev_land + ev_ev) if (par == 0 and a == 0
                                                       and qh == 0) else ev_ev
                        ev_mmb.append(self.mm(
                            self.pp[4][r0:r0 + 16, qq * 16:qq * 16 + 16],
                            self.stripA[r0:r0 + 16, col:col + 16],
                            self.stripB[r0:r0 + 16, col:col + 16],
                            deps=d0, tp=(r0, r0)))
                    dstv = self.stripB[r0 + 32:r0 + 48].rearrange(
                        "r (q a c) -> r q a c", a=2, c=16)[:, qh * 32:qh * 32 + 32, a]
                    ev_ev0 = self.dve(lambda e, dstv=dstv, r0=r0: e.tensor_copy(
                        dstv, self.pp[4][r0:r0 + 16, 0:512].rearrange(
                            "p (b x) -> p b x", b=32)), deps=(ev_mmb[-1],))
                    ev_ev = (ev_ev0,)
                    ev_mmc = []
                    for qq in range(32):
                        q = qh * 32 + qq
                        col = (q * 2 + a) * 16
                        ev_mmc.append(self.mm(
                            self.pp[5][r0:r0 + 16, qq * 16:qq * 16 + 16],
                            self.stripA[r0 + 32:r0 + 48, col:col + 16],
                            self.stripB[r0 + 32:r0 + 48, col:col + 16],
                            deps=(ev_ev0, ev_x16[-1]) if qq == 0 and ev_x16
                            else ((ev_ev0,) if qq == 0 else ()), tp=(r0 + 32, r0)))
                    dstx = self.t32t[r0 + a * 32:r0 + a * 32 + 16].rearrange(
                        "r (q c) -> r q c", c=32)[:, qh * 32:qh * 32 + 32, 16:32]
                    ev_x16.append(self.dve(lambda e, dstx=dstx, r0=r0:
                                           e.tensor_scalar_mul(
                        dstx, self.pp[5][r0:r0 + 16, 0:512].rearrange(
                            "p (b x) -> p b x", b=32), -1.0), deps=(ev_mmc[-1],)))
        ev_zs = self.dve(lambda e: e.transpose(self.zstr[:], self.t32t[:]),
                         deps=tuple(ev_x16))
        ev_x32 = []
        ev_pe = None
        for par in range(2):
            r0 = par * 64
            for qh in range(4):
                ev_pm = []
                for qq in range(16):
                    q = qh * 16 + qq
                    ev_pm.append(self.mm(
                        self.pp[4][r0 + 32:r0 + 64, qq * 32:qq * 32 + 32],
                        self.m32s[r0 + 32:r0 + 64, q * 32:q * 32 + 32],
                        self.t32t[r0 + 32:r0 + 64, q * 32:q * 32 + 32],
                        deps=(ev_zs, ev_pe) if qq == 0 else (),
                        tp=(r0 + 32, r0 + 32)))
                ev_pe = self.dve(lambda e, r0=r0, qh=qh: e.tensor_copy(
                    self.pxs[r0:r0 + 32, qh * 512:(qh + 1) * 512],
                    self.pp[4][r0 + 32:r0 + 64, 0:512]), deps=(ev_pm[-1],))
                ev_xm = []
                for qq in range(16):
                    q = qh * 16 + qq
                    ev_xm.append(self.mm(
                        self.pp[5][r0:r0 + 32, qq * 32:qq * 32 + 32],
                        self.zstr[r0:r0 + 32, q * 32:q * 32 + 32],
                        self.pxs[r0:r0 + 32, qh * 512 + qq * 32:qh * 512 + qq * 32 + 32],
                        deps=(ev_pe, ev_x32[-1] if ev_x32 else None)
                        if qq == 0 else (), tp=(r0, r0)))
                ev_x32.append(self.dve(lambda e, r0=r0, qh=qh: e.tensor_scalar_mul(
                    self.pxs[r0:r0 + 32, qh * 512:(qh + 1) * 512],
                    self.pp[5][r0:r0 + 32, 0:512], -1.0), deps=(ev_xm[-1],)))
        for par in range(2):
            r0 = par * 64
            ev_x32.append(self.dve(
                lambda e, r0=r0: e.tensor_copy(
                    self.pxs[r0 + 32:r0 + 64, 0:2048],
                    self.t32t[r0 + 32:r0 + 64, 0:2048]),
                deps=tuple(ev_x32)))
        self.ev_solve = tuple(ev_x32)

    def _slab_chain(self, sl):
        s = self.s
        bar = tuple(s.barrier())
        ev_wo = {}
        pY, pU, pO1, pO2, pDS = (self.pp[0], self.pp[1], self.pp[2],
                                 self.pp[3], self.pp[4])
        if not hasattr(self, "bankdep"):
            self.bankdep = {}
        bd = self.bankdep
        gv = lambda t: t[:].rearrange("p (g x) -> p g x", g=8)
        for chl in range(SLAB):
            ch = sl * SLAB + chl
            cb = lambda slot, chl=chl: self.ccv(chl, slot)[:, :, None] \
                .broadcast_to((128, 8, 64))
            evY = evO1 = None
            for h in range(H):
                par, pa = h % 2, h // 2
                r0 = 64 * par
                sl2 = lambda t, r0=r0, pa=pa: t[r0:r0 + 64, pa * 64:pa * 64 + 64]
                kc = self._kc(h, chl, self.kcs)
                qc = self._kc(h, chl, self.qcs)
                dep0 = bar + (self.ev_S[h], self.ev_wo_R, self.ev_solve[-1],
                              bd.get("pY"), bd.get("pO1")) if h == 0 else ()
                evY = self.mm(sl2(pY), kc, sl2(self.Sb), deps=dep0, tp=(r0, r0))
                evO1 = self.mm(sl2(pO1), qc, sl2(self.Sb), deps=(), tp=(r0, r0))
            e_r1 = self.dve(lambda e, chl=chl, cb=cb: e.tensor_mul(
                gv(self.rbt), gv(pY), cb(2)), deps=(evY, evO1))
            e_m = self.dve(lambda e, chl=chl, cb=cb: e.tensor_mul(
                gv(self.msk), gv(self.vtm[chl]), cb(3)),
                deps=(self.ev_prep[chl][3],))
            e_r2 = self.dve(lambda e: e.tensor_add(gv(self.rbf), gv(self.msk),
                                                   gv(self.rbt)),
                            deps=(e_r1, e_m))
            bd["msk"] = e_r2
            evA = None
            for h in range(H):
                par, pa = h % 2, h // 2
                r0 = 64 * par
                q = chl * 8 + pa
                d0 = (e_r2, bd.get("pU")) if h == 0 else ()
                self.mm(pU[r0:r0 + 32, pa * 64:pa * 64 + 64],
                        self.t32t[r0:r0 + 32, q * 32:q * 32 + 32],
                        self.rbf[r0:r0 + 32, pa * 64:pa * 64 + 64],
                        deps=d0, tp=(r0, r0))
                evA = self.mm(pU[r0 + 32:r0 + 64, pa * 64:pa * 64 + 64],
                              self.pxs[r0:r0 + 64, q * 32:q * 32 + 32],
                              self.rbf[r0:r0 + 64, pa * 64:pa * 64 + 64],
                              start=True, stop=True, tp=(r0, r0 + 32))
            evU = self.dve(lambda e: e.tensor_copy(self.ubf[:], pU[:]), deps=(evA,))
            evO2 = evDS = None
            for h in range(H):
                par, pa = h % 2, h // 2
                r0 = 64 * par
                sl2 = lambda t, r0=r0, pa=pa: t[r0:r0 + 64, pa * 64:pa * 64 + 64]
                d0 = (evU, bd.get("pO2"), bd.get("pDS")) if h == 0 else ()
                evO2 = self.mm(sl2(pO2), sl2(self.ams[chl]), sl2(self.ubf),
                               deps=d0, tp=(r0, r0))
                evDS = self.mm(sl2(pDS), sl2(self.khat[chl]), sl2(self.ubf),
                               deps=(), tp=(r0, r0))
            e_o1 = self.dve(lambda e, cb=cb: e.tensor_mul(
                gv(self.rbt), gv(pO1), cb(6)), deps=(evO2, bd.get("rbt")))
            e_oc = self.dve(lambda e, chl=chl: e.tensor_add(
                self.oc[chl % 2][:], self.rbt[:], pO2[:]),
                deps=(e_o1, evO2, self.ev_oc_free))
            bd["rbt"] = e_oc
            e_s1 = self.pool(lambda e, cb=cb: e.tensor_mul(
                gv(self.Sm), gv(self.Sm), cb(5)),
                deps=(self.ev_S[0], bd.get("sm")))
            e_s2 = self.dve(lambda e: e.tensor_add(self.Sm[:], self.Sm[:], pDS[:]),
                            deps=(e_s1, evDS))
            bd["sm"] = e_s2
            ev_cast = self.act(lambda e: e.activation(self.Sb[:], self.Sm[:], AF.Copy),
                               deps=(e_s2,))
            for h in range(H):
                self.ev_S[h] = ev_cast
            bd["pY"] = e_r1
            bd["pO1"] = e_o1
            bd["pU"] = evU
            bd["pO2"] = e_oc
            bd["pDS"] = e_s2
            if chl % 2 == 1:
                self._pair_wo(sl, chl, ev_wo)

    def _pair_wo(self, sl, chl, ev_wo):
        s = self.s
        ch = sl * SLAB + chl
        ti = (ch - 1) // 2
        wsrc = self.wT["o"].rearrange("(j p) c -> p j c", p=128)
        ev_ot = []
        for h in range(H):
            par, pa = h % 2, h // 2
            r0 = 64 * par
            for cpar in range(2):
                pb = self.pp[4 + pa // 4]
                ev_ot.append(self.mm(
                    pb[r0:r0 + 64, (pa % 4) * 128 + cpar * 64:
                       (pa % 4) * 128 + cpar * 64 + 64],
                    self.oc[(chl - 1 + cpar) % 2][r0:r0 + 64, pa * 64:pa * 64 + 64],
                    self.c_i64d[r0:r0 + 64, :],
                    deps=(s.now(EV),) if (h == 0 and cpar == 0) else (),
                    tp=(r0, r0)))
        ev_oT1 = self.dve(lambda e: e.tensor_copy(self.oT[:, 0:512], self.pp[4][:]),
                          deps=(ev_ot[-1],))
        ev_oT2 = self.dve(lambda e: e.tensor_copy(self.oT[:, 512:1024], self.pp[5][:]))
        self.ev_oc_free = ev_oT2
        ev_last = [None, None]
        for j in range(8):
            qt = j // 2
            slot = self.wring[qt % 2]
            if j % 2 == 0:
                ev_wo[qt] = self.dma(
                    QG, slot[:].rearrange("p (jj c) -> p jj c", jj=2),
                    wsrc[:, qt * 2:qt * 2 + 2, :], deps=(s.now(ET),))
            for half in range(2):
                c0 = (j % 2) * 1024 + half * 512
                ev_last[half] = self.mm(
                    self.pp[2 + half][:, 0:512], self.oT[:, j * 128:(j + 1) * 128],
                    slot[:, c0:c0 + 512],
                    deps=(ev_oT1, ev_oT2, ev_wo[qt], s.now(EV)) if j == 0 else
                    ((ev_wo[qt],) if j % 2 == 0 and half == 0 else ()),
                    start=(j == 0), stop=(j == 7))
        for half in range(2):
            self.ev_wo_R = self.dve(lambda e, ti=ti, half=half: e.tensor_add(
                self.Rt[ti][:, half * 512:(half + 1) * 512],
                self.Rt[ti][:, half * 512:(half + 1) * 512],
                self.pp[2 + half][:, 0:512]), deps=(ev_last[half],))

    def p9_out(self):
        s = self.s
        deps = s.barrier()
        for i in range(NTT):
            self.dma(QS, self.out_x[i * 128:(i + 1) * 128, :], self.Rt[i][:],
                     deps=deps if i == 0 else ())
        self.s.emit("gpsimd", lambda e: e.engine_nop(), deps=(s.now(QS),), sig=EP)

    def p_debug(self):
        s = self.s
        deps = s.barrier()
        first = True
        for key, get_ap, shape, dtype in self.debug:
            ap = get_ap(self)
            self.dma(QS, self.dbg_out[key], ap, deps=deps if first else ())
            first = False
        if self.debug:
            self.s.emit("gpsimd", lambda e: e.engine_nop(), deps=(s.now(QS),), sig=EP)

    def finalize(self):
        nc, s = self.nc, self.s
        with contextlib.ExitStack() as es:
            for name in list(s.cnt):
                s.sems[name] = es.enter_context(nc.semaphore(name))
            with nc.Block() as block:
                block.tensor(lambda e: s.replay("tensor", e))
                block.vector(lambda e: s.replay("vector", e))
                block.scalar(lambda e: s.replay("scalar", e))
                block.gpsimd(lambda e: s.replay("gpsimd", e))
                block.sync(lambda e: s.replay("sync", e))


AUTOSER_MODE = False
PERM_GB = [0, 2, 4, 6, 8, 10, 12, 14, 1, 3, 5, 7, 9, 11, 13, 15]


def _stage_inputs(inp):
    bf = ml_dtypes.bfloat16
    f32 = np.float32
    x = np.ascontiguousarray(inp["L0_attn_x"], dtype=f32)
    com = {}
    com["wqT"] = np.ascontiguousarray(np.asarray(inp["L0_attn_Wq"], f32).T)
    com["wkT"] = np.ascontiguousarray(np.asarray(inp["L0_attn_Wk"], f32).T)
    com["wvT"] = np.ascontiguousarray(np.asarray(inp["L0_attn_Wv"], f32).T)
    com["woT"] = np.ascontiguousarray(np.asarray(inp["L0_attn_Wo"], f32).T)
    wg = np.asarray(inp["L0_attn_Wg"], f32).T[:, PERM_GB]
    wb = np.asarray(inp["L0_attn_Wb"], f32).T[:, PERM_GB]
    com["wgbT"] = np.ascontiguousarray(np.concatenate([wg, wb], 1))
    cw = np.zeros((1024, 12), f32)
    cw[:, 0:4] = np.asarray(inp["L0_attn_cq_w"], f32)[:, 0, :]
    cw[:, 4:8] = np.asarray(inp["L0_attn_ck_w"], f32)[:, 0, :]
    cw[:, 8:12] = np.asarray(inp["L0_attn_cv_w"], f32)[:, 0, :]
    com["convw"] = np.ascontiguousarray(cw.reshape(8, 128, 12).transpose(1, 0, 2)
                                        .reshape(128, 96))
    cb = np.stack([np.asarray(inp["L0_attn_cq_b"], f32),
                   np.asarray(inp["L0_attn_ck_b"], f32),
                   np.asarray(inp["L0_attn_cv_b"], f32)], 1)
    com["convb"] = np.ascontiguousarray(cb.reshape(8, 128, 3).transpose(1, 0, 2)
                                        .reshape(128, 24))
    g1 = []
    g2 = []
    for k1, k2 in (("L0_ffn_G1_R", "L0_ffn_G2_R"), ("L0_ffn_G1_I", "L0_ffn_G2_I")):
        G1 = np.asarray(inp[k1], f32)[0]
        g1.append(np.ascontiguousarray(G1.transpose(1, 2, 0).reshape(4, 1024)))
        G2 = np.asarray(inp[k2], f32)[:, :, 0, :]
        g2.append(np.ascontiguousarray(G2.reshape(4, 256)))
    com["g1p"] = np.ascontiguousarray(np.stack(g1))
    com["g2p"] = np.ascontiguousarray(np.stack(g2))
    com["fbias"] = np.ascontiguousarray(np.concatenate(
        [np.asarray(inp["L0_ffn_bias_real"], f32),
         np.asarray(inp["L0_ffn_bias_imag"], f32)])[None, :])
    com["ident128"] = np.eye(128, dtype=f32).astype(bf)
    com["ident64d"] = np.ascontiguousarray(
        np.concatenate([np.eye(64, dtype=f32)] * 2, 0)).astype(bf)
    com["ident96f"] = np.eye(96, dtype=f32)
    com["ident128f"] = np.eye(128, dtype=f32)
    sel = np.zeros((16, 1024), f32)
    for pa in range(8):
        for s in range(128):
            sel[pa + 8 * (s // 64), pa * 128 + s] = 1.0
    com["selb3"] = sel.astype(bf)
    r = np.arange(64)
    ml64 = np.where(r[None, :] < r[:, None], 0.0, NEG).astype(f32)
    mu64 = np.where(r[:, None] <= r[None, :], LN_KD, NEG).astype(f32)
    com["maskLd"] = np.ascontiguousarray(np.concatenate([ml64, ml64], 0))
    com["maskUd"] = np.ascontiguousarray(np.concatenate([mu64, mu64], 0))
    in_maps = []
    for b in range(8):
        m = dict(com)
        m["x"] = np.ascontiguousarray(x[b])
        in_maps.append(m)
    return in_maps


_BUILT = {}


def _get_nc(upto=9, debug=()):
    key = (upto, tuple(k for k, _, _, _ in debug))
    if key not in _BUILT:
        mk = MK(debug=debug)
        mk.build(upto=upto)
        _BUILT[key] = mk
    return _BUILT[key]


def kernel(**inputs):
    Sched.AUTOSER = AUTOSER_MODE
    in_maps = _stage_inputs(inputs)
    mk = _get_nc()
    res = run_bass_kernel_spmd(mk.nc, in_maps, core_ids=list(range(8)))
    return np.stack([np.asarray(r["out"], np.float32) for r in res.results])


# version 23 (final)
# speedup vs baseline: 1.1606x; 1.1606x over previous
import contextlib
import numpy as np
import ml_dtypes
import concourse.bass as bass
from concourse.bass import AP
import concourse.mybir as mybir
from concourse.bass_utils import run_bass_kernel_spmd

F32 = mybir.dt.float32
BF16 = mybir.dt.bfloat16
AF = mybir.ActivationFunctionType
ALU = mybir.AluOpType

T, D, H, Kd = 2048, 1024, 16, 64
C = 64
NCH = T // C
SLAB = 8
NSLAB = NCH // SLAB
TSLAB = C * SLAB
NTT = T // 128
NEG = -1.0e30
LN_KD = float(np.log(Kd ** -0.5))
LNRN_MAX = float(np.log(1e12))
EPS_LN = 1e-5

ENGINES = ("tensor", "vector", "scalar", "gpsimd", "sync")
QS, QG, QA = "qs", "qg", "qa"
ET, EV, EA, EP = "et", "ev", "ea", "ep"


class Sched:
    NPOOL = 8

    def __init__(self):
        self.prog = {e: [] for e in ENGINES}
        self.sems = {}
        self.cnt = {}
        self.lastwait = {}
        self.lastop = {}
        self.qpool = {}
        self.quse = {}
        for q in (QS, QG, QA):
            self.qpool[q] = ["%s%d" % (q, i) for i in range(self.NPOOL)]
            self.quse[q] = 0
            for nm in self.qpool[q]:
                self.sem(nm)

    def sem(self, name):
        self.cnt.setdefault(name, 0)

    def wait(self, eng, ev):
        if ev is None:
            return
        if isinstance(ev, (list, tuple)) and ev and isinstance(ev[0], (list, tuple)):
            for e2 in ev:
                self.wait(eng, e2)
            return
        sem, val = ev
        if val <= 0:
            return
        key = (eng, sem)
        if self.lastwait.get(key, 0) >= val:
            return
        self.lastwait[key] = val
        self.prog[eng].append(("w", sem, val))

    AUTOSER = True

    def emit(self, eng, fn, deps=(), sig=None, dma=False):
        if Sched.AUTOSER and not dma and eng in ("vector", "scalar", "gpsimd"):
            self.wait(eng, self.lastop.get(eng))
        for d in deps:
            self.wait(eng, d)
        ev = None
        if sig is not None:
            self.sem(sig)
            inc = 16 if dma else 1
            self.cnt[sig] += inc
            ev = (sig, self.cnt[sig])
            self.prog[eng].append(("i", fn, sig, inc))
            if not dma:
                self.lastop[eng] = ev
        else:
            self.prog[eng].append(("n", fn))
        return ev

    def now(self, sem):
        if sem in self.qpool:
            return [(nm, self.cnt.get(nm, 0)) for nm in self.qpool[sem]]
        return (sem, self.cnt.get(sem, 0))

    def barrier(self):
        evs = [self.now(x) for x in (ET, EV, EA, EP)]
        for q in (QS, QG, QA):
            evs.extend(self.now(q))
        return evs

    def replay(self, eng_name, eng):
        for item in self.prog[eng_name]:
            if item[0] == "w":
                eng.wait_ge(self.sems[item[1]], item[2])
            elif item[0] == "i":
                item[1](eng).then_inc(self.sems[item[2]], item[3])
            else:
                item[1](eng)


def hr(h):
    return 64 * (h % 2)


class MK:
    def __init__(self, debug=()):
        self.nc = bass.Bass("TRN2", target_bir_lowering=False)
        self.s = Sched()
        for q in (QS, QG, QA, ET, EV, EA, EP):
            self.s.sem(q)
        self.debug = list(debug)
        self._declare_dram()
        self._declare_sbuf()

    def _declare_dram(self):
        nc = self.nc
        def dinp(name, shape, dtype=F32):
            return nc.dram_tensor(name, list(shape), dtype, kind="ExternalInput").ap()
        self.x_in = dinp("x", (T, D))
        self.wT = {k: dinp("w%sT" % k, (D, D)) for k in "qkvo"}
        self.wgbT = dinp("wgbT", (D, 32))
        self.convw_in = dinp("convw", (128, 96))
        self.convb_in = dinp("convb", (128, 24))
        self.g1p = dinp("g1p", (2, 4, 1024))
        self.g2p = dinp("g2p", (2, 4, 256))
        self.fbias = dinp("fbias", (1, D))
        self.i128 = dinp("ident128", (128, 128), BF16)
        self.i64d = dinp("ident64d", (128, 64), BF16)
        self.i96f = dinp("ident96f", (96, 96))
        self.i128f = dinp("ident128f", (128, 128))
        self.maskLd_in = dinp("maskLd", (128, 64))
        self.selb3_in = dinp("selb3", (16, 1024), BF16)
        self.maskUd_in = dinp("maskUd", (128, 64))
        self.out_x = nc.dram_tensor("out", [T, D], F32, kind="ExternalOutput").ap()
        self.scr_w = nc.dram_tensor("scr_w", [2, 256, 1024], F32, kind="Internal").ap()
        self.scr_mp = nc.dram_tensor("scr_mp", [2, NCH, 8, 64, 64], BF16, kind="Internal").ap()
        self.scr_t16 = nc.dram_tensor("scr_t16", [NSLAB, 128, 4, 256], F32, kind="Internal").ap()
        self.scr_t16t = nc.dram_tensor("scr_t16t", [NSLAB, 128, 4, 256], F32, kind="Internal").ap()
        self.scr_flat = nc.dram_tensor("scr_flat", [2, H, T], F32, kind="Internal").ap()
        self.scr_x2t = nc.dram_tensor("scr_x2t", [8, 128, T], BF16, kind="Internal").ap()
        self.scr_tc = nc.dram_tensor("scr_tc", [NCH, 64, 96], F32, kind="Internal").ap()
        self.dbg_out = {}
        for key, _, shape, dtype in self.debug:
            self.dbg_out[key] = nc.dram_tensor(
                "dbg_" + key, list(shape), dtype, kind="ExternalOutput").ap()

    def _alias(self, name, shape, dtype, backing, byte_off=0):
        nc = self.nc
        base = nc.lookup_mloc(backing).addr
        return nc.alloc_sbuf_tensor_at(name, list(shape), dtype,
                                       offset=base + byte_off)

    def _declare_sbuf(self):
        nc = self.nc
        def sbuf(name, shape, dtype=F32):
            return nc.alloc_sbuf_tensor(name, list(shape), dtype)
        U8 = mybir.dt.uint8
        self.ovl1 = sbuf("ovl1", (128, 32768), U8)
        self.ovl2 = sbuf("ovl2", (128, 16384), U8)
        self.ovl3 = sbuf("ovl3", (128, 8192), U8)
        self.ovl4 = sbuf("ovl4", (128, 8192), U8)
        self.Rt = [sbuf("R%d" % i, (128, D)) for i in range(NTT)]
        self.big4 = [self._alias("big4_%d" % i, (128, 2048), BF16, self.ovl2, i * 4096)
                     for i in range(4)]
        self.wring = [sbuf("wring%d" % i, (128, 2048), BF16) for i in range(2)]
        self.x2s = sbuf("x2s", (128, 8 * 512), BF16)
        self.xln = [self._alias("xln%d" % i, (128, D), BF16, self.ovl4, i * 2048)
                    for i in range(2)]
        self.xTt = [self._alias("xTt%d" % i, (128, 128 * 8), BF16, self.ovl4,
                                4096 + i * 2048) for i in range(2)]
        self.ln_stats = [sbuf("lnst%d" % i, (128, 12)) for i in range(2)]
        self.ln_cols = [sbuf("lncol%d" % i, (128, 4)) for i in range(2)]
        self.c_i128 = sbuf("ci128", (128, 128), BF16)
        self.c_i64d = sbuf("ci64d", (128, 64), BF16)
        self.c_i96f = sbuf("ci96f", (96, 96))
        self.c_maskLd = sbuf("cmaskLd", (128, 64))
        self.c_maskUd = sbuf("cmaskUd", (128, 64))
        self.c_ones = sbuf("cones", (1, 128), BF16)
        self.c_fbias = sbuf("cfbias", (1, D), BF16)
        self.c_gbw = sbuf("cgbw", (128, 8 * 32), BF16)
        self.c_convw = sbuf("cconvw", (128, 96))
        self.c_convb = sbuf("cconvb", (128, 24))
        self.gc = self._alias("gc", (16, T), F32, self.ovl1, 0)
        self.lnbc = self._alias("lnbc", (16, T), F32, self.ovl1, 8192)
        self.lgc = self._alias("lgc", (16, T), F32, self.ovl1, 16384)
        self.zstack = self._alias("zstack", (96, T), F32, self.ovl1, 24576)
        self.ztmp = [self._alias("ztmp%d" % i, (16, 512), F32, self.ovl3,
                                 4096 + i * 2048) for i in range(2)]
        self.tcst = sbuf("tcst", (64, 96))
        self.tcols2 = [sbuf("tcols2_%d" % i, (128, 96)) for i in range(2)]
        self.wb = self._alias("wb", (128, 1024), F32, self.ovl3, 0)
        self.rawt = sbuf("rawt", (128, 3 + TSLAB))
        self.cacc = sbuf("cacc", (128, TSLAB))
        self.tails = sbuf("tails", (128, 8 * 3 * 3))
        self.g1s = [self._alias("g1s0", (4, 1024), BF16, self.cacc, 0),
                    self._alias("g1s1", (4, 1024), BF16, self.rawt, 0)]
        self.g2s = [self._alias("g2s%d" % i, (4, 256), BF16, self.x2s, i * 512)
                    for i in range(2)]
        self.qcs = self._alias("qcs", (128, 8 * TSLAB), BF16, self.ovl1, 0)
        self.kcs = self._alias("kcs", (128, 8 * TSLAB), BF16, self.ovl1, 8192)
        self.vcs = self._alias("vcs", (128, 8 * TSLAB), BF16, self.ovl1, 16384)
        self.mps2 = [sbuf("mps2_%d" % i, (128, 512), BF16) for i in range(2)]
        self.m32s = sbuf("m32s", (128, 2048), BF16)
        self.ams = [sbuf("ams%d" % i, (128, 512), BF16) for i in range(SLAB)]
        self.khat = [sbuf("khat%d" % i, (128, 512), BF16) for i in range(SLAB)]
        self.cc = [sbuf("cc%d" % i, (128, 80)) for i in range(SLAB)]
        self.lgrow = sbuf("lgrow", (128, 512))
        self.c_selb = sbuf("cselb", (16, 1024), BF16)
        self.lgT2 = sbuf("lgT2", (16, 64))
        self.lgs = sbuf("lgs", (16, 192), BF16)
        self.dgr = self._alias("dgr", (128, 512), BF16, self.ovl2, 2048)
        self.vtm = [sbuf("vtm%d" % i, (128, 512), BF16) for i in range(SLAB)]
        self.ksq = self._alias("ksq", (128, 512), F32, self.ovl4, 6144)
        self.msk = self._alias("msk", (128, 512), F32, self.ovl3, 4096)
        self.sstb = self._alias("sstb", (128, 4 * 256), BF16, self.ovl3, 4096)
        self.ktmb = self._alias("ktmb", (128, 512), BF16, self.ovl2, 4096)
        self.subtmp = self._alias("subtmp", (128, 4 * 16 * 15), BF16, self.ovl3, 6144)
        self.msk2 = self._alias("msk2", (128, 512), F32, self.ovl3, 6144)
        self.rbt = self._alias("rbt", (128, 512), F32, self.ovl4, 0)
        self.cst = sbuf("cst", (128, 3))
        self.c_i128f = sbuf("ci128f", (128, 128))
        self.oc = [sbuf("oc%d" % i, (128, 512), BF16) for i in range(2)]
        self.oT = self._alias("oT", (128, 8 * 128), BF16, self.ovl4, 2048)
        self.Sm = sbuf("Sm", (128, 512))
        self.Sb = sbuf("Sb", (128, 512), BF16)
        self.ubf = sbuf("ubf", (128, 512), BF16)
        self.rbf = sbuf("rbf", (128, 512), BF16)
        self.sst = self._alias("sst", (128, 4 * 256), F32, self.ovl1, 24576)
        self.tstr = self._alias("tstr", (128, 4 * 256), F32, self.ovl1, 28672)
        self.tstrT = self._alias("tstrT", (128, 4 * 256), F32, self.ovl3, 0)
        self.stripA = self._alias("stripA", (128, 2048), BF16, self.ovl2, 0)
        self.stripB = self._alias("stripB", (128, 2048), BF16, self.ovl2, 4096)
        self.t32t = self._alias("t32t", (128, 2048), BF16, self.ovl2, 8192)
        self.zstr = self._alias("zstr", (128, 2048), BF16, self.ovl2, 12288)
        self.pxs = sbuf("pxs", (128, 4096), BF16)
        self.pp = [nc.alloc_psum_tensor("pp%d" % i, [128, 512], F32) for i in range(6)]

    def bwt(self, j):
        return self.big4[j // 2][:, (j % 2) * 1024:(j % 2) * 1024 + 1024]

    def mm(self, out, lhsT, rhs, deps=(), start=True, stop=True, tp=None):
        def f(e, out=out, lhsT=lhsT, rhs=rhs, start=start, stop=stop, tp=tp):
            return e.matmul(out, lhsT, rhs, start=start, stop=stop, tile_position=tp)
        return self.s.emit("tensor", f, deps=deps, sig=ET)

    def dve(self, fn, deps=()):
        return self.s.emit("vector", fn, deps=deps, sig=EV)

    def act(self, fn, deps=()):
        return self.s.emit("scalar", fn, deps=deps, sig=EA)

    def pool(self, fn, deps=()):
        return self.s.emit("gpsimd", fn, deps=deps, sig=EP)

    def dma(self, q, out, in_, deps=()):
        s = self.s
        eng = {"qs": "sync", "qg": "gpsimd", "qa": "scalar"}[q]
        idx = s.quse[q] % Sched.NPOOL
        s.quse[q] += 1
        nm = s.qpool[q][idx]
        prev = s.cnt.get(nm, 0)
        if prev:
            s.wait(eng, (nm, prev))

        def f(e, out=out, in_=in_):
            return e.dma_start(out=out, in_=in_)
        return s.emit(eng, f, deps=deps, sig=nm, dma=True)

    def build(self, upto=9):
        self.p0_loads()
        if MK.SUB >= 10 and MK.SUB < 13:
            self.ev_bigw_done = None
        if upto >= 2:
            self.p_ln_ffn(1)
        if upto >= 3:
            self.p3_ln2()
        if upto >= 4:
            self.p4_gbeta()
        if upto >= 5:
            self.p5_delta(upto)
        if upto >= 9:
            self.p_ln_ffn(2)
        self.p9_out()
        self.p_debug()
        self.finalize()
        return self.nc

    def p0_loads(self):
        xr = self.x_in.rearrange("(n p) m -> n p m", p=128)
        for dst, src in ((self.c_i128, self.i128), (self.c_i64d, self.i64d),
                         (self.c_i96f, self.i96f), (self.c_i128f, self.i128f),
                         (self.c_maskLd, self.maskLd_in),
                         (self.c_selb, self.selb3_in),
                         (self.c_maskUd, self.maskUd_in),
                         (self.c_convw, self.convw_in), (self.c_convb, self.convb_in)):
            self.dma(QS, dst[:], src)
        self.ev_cs = self.s.now(QS)
        self.dma(QG, self.c_fbias[:], self.fbias)
        self.ev_R = None
        self.dma(QG, self.c_gbw[:].rearrange("p (i j) -> p i j", i=8),
                 self.wgbT.rearrange("(i p) j -> p i j", p=128))
        for t in range(2):
            self.dma(QG, self.g1s[t][:], self.g1p[t])
            self.dma(QG, self.g2s[t][:], self.g2p[t])
        self.ev_cg = self.s.now(QG)
        self.ev_R = [self.dma((QG, QA)[i % 2], self.Rt[i][:], xr[i])
                     for i in range(NTT)]
        self.dve(lambda e: e.memset(self.c_ones[:], 1.0))
        self.dve(lambda e: e.memset(self.cst[:, 0:1], EPS_LN))
        self.dve(lambda e: e.memset(self.cst[:, 1:2], 1.0))
        self.dve(lambda e: e.memset(self.cst[:, 2:3], LN_KD))
        self.dve(lambda e: e.memset(self.zstack[:], 0.0))
        self.dve(lambda e: e.memset(self.Sm[:], 0.0))
        self.dve(lambda e: e.memset(self.tails[:], 0.0))
        self.ev_zero = self.dve(lambda e: e.memset(self.Sb[:], 0.0))
        pass

    def p1_ffnw(self, extra=(), reload=False):
        s = self.s
        ev_wb = None
        if reload:
            for t in range(2):
                self.dma(QG, self.g1s[t][:], self.g1p[t], deps=extra)
                self.dma(QG, self.g2s[t][:], self.g2p[t], deps=extra)
            self.ev_cg = s.now(QG)
        ev_wb_t = [(), ()]
        for t in range(2):
            evd = []
            for mt in range(2):
                evs = []
                for nk in range(2):
                    ev_mm = self.mm(self.pp[nk][:, 0:512],
                                    self.g2s[t][:, mt * 128:(mt + 1) * 128],
                                    self.g1s[t][:, nk * 512:(nk + 1) * 512],
                                    deps=(self.ev_cg, ev_wb) + tuple(extra))
                    evs.append(self.dve(
                        lambda e, nk=nk: e.tensor_copy(
                            self.wb[:, nk * 512:(nk + 1) * 512], self.pp[nk][:, 0:512]),
                        deps=(ev_mm,)))
                ev_wb = self.dma((QS, QG)[mt],
                                 self.scr_w[t, mt * 128:(mt + 1) * 128, :],
                                 self.wb[:], deps=evs)
                evd.append(ev_wb)
            ev_wb_t[t] = tuple(evd)
        self.ev_bigw = []
        self.ev_piece = [None] * 8
        ev_cp = None
        qrr = (QS, QG, QA)
        for dt in range(4):
            for t in range(2):
                ev_in = []
                for d in range(8):
                    perm = AP(self.scr_w.tensor,
                              self.scr_w.offset + t * 262144 + (dt * 8 + d) * 32,
                              [[1024, 16], [16384, 16], [1, 32]])
                    ev_in.append(self.dma(
                        qrr[d % 3], self.wb[d * 16:(d + 1) * 16, 0:512], perm,
                        deps=ev_wb_t[0] + ev_wb_t[1]
                        + ((ev_cp,) if ev_cp else ())))
                ev_in = tuple(ev_in)
                src = self.wb[:, 0:512].rearrange("p (e b) -> p b e", e=16)
                if t == 0:
                    dsts = [(self.bwt(dt)[:, 0:512], 1.0),
                            (self.bwt(4 + dt)[:, 512:1024], 1.0)]
                else:
                    dsts = [(self.bwt(dt)[:, 512:1024], 1.0),
                            (self.bwt(4 + dt)[:, 0:512], -1.0)]
                for dst, sc in dsts:
                    ov = dst.rearrange("p (b e) -> p b e", b=32)
                    if sc == 1.0:
                        ev_cp = self.dve(lambda e, ov=ov, src=src: e.tensor_copy(
                            ov, src), deps=ev_in)
                    else:
                        ev_cp = self.dve(
                            lambda e, ov=ov, src=src: e.tensor_scalar_mul(ov, src, -1.0),
                            deps=ev_in)
                self.ev_bigw.append(ev_cp)
            self.ev_piece[dt] = ev_cp
            self.ev_piece[4 + dt] = ev_cp
        self.ev_bigw_done = ev_cp

    def _emit_ln(self, i, par, extra):
        st, co = self.ln_stats[par], self.ln_cols[par]
        ev1 = self.dve(lambda e, i=i: e.bn_stats(st[:, 0:6], self.Rt[i][:, 0:512]),
                       deps=extra)
        ev2 = self.dve(lambda e, i=i: e.bn_stats(st[:, 6:12], self.Rt[i][:, 512:1024]))
        ev3 = self.dve(lambda e: e.bn_aggr(co[:, 0:2], st[:, 0:12]), deps=(ev1, ev2))
        if not hasattr(self, "ln_rd"):
            self.ln_rd = [(), ()]
        ev4 = self.act(lambda e: e.activation(co[:, 2:3], co[:, 1:2], AF.Ln,
                                              bias=self.cst[:, 0:1]),
                       deps=(ev3,) + tuple(self.ln_rd[par]))
        ev5 = self.act(lambda e: e.activation(co[:, 2:3], co[:, 2:3], AF.Exp,
                                              scale=-0.5), deps=(ev4,))
        ev7 = self.dve(lambda e: e.scalar_tensor_tensor(
            co[:, 3:4], co[:, 0:1], -1.0, co[:, 2:3], ALU.mult, ALU.mult),
            deps=(ev5,))
        self.ln_rd[par] = (ev7,)
        return self.act(lambda e, i=i, par=par: e.activation(
            self.xln[par][:], self.Rt[i][:], AF.Identity, bias=co[:, 3:4],
            scale=co[:, 2:3]), deps=(ev7, ev5))

    SUB = 3
    NTILES = NTT

    def p_ln_ffn(self, which):
        s = self.s
        evs = [None]
        if which == 1:
            pre = [self.ev_cs]
            self.ev_ffn = [None] * NTT
        else:
            pre = s.barrier()
        sub = MK.SUB if MK.SUB < 10 else MK.SUB - 10
        ev_ffn_prev = [None, None]
        ev_tp_prev = [None] * 8
        nt = min(NTT, MK.NTILES)

        def emit_tile_ln(i):
            par = i % 2
            extra = list(pre) + ([self.ev_R[i]] if which == 1 else [])
            if MK.SUB >= 10:
                return self.act(lambda e, i=i, par=par: e.activation(
                    self.xln[par][:], self.Rt[i][:], AF.Copy), deps=extra)
            return self._emit_ln(i, par, extra)

        ap_pend = {0: emit_tile_ln(0)} if nt else {}
        if nt > 1:
            ap_pend[1] = emit_tile_ln(1)
        if MK.SUB < 10 or MK.SUB >= 13:
            if which == 1:
                self.p1_ffnw()
            else:
                self.p1_ffnw(extra=tuple(s.barrier()), reload=True)
        for i in range(nt):
            par = i % 2
            ev_ap = ap_pend.pop(i)
            if i + 1 < nt and i + 1 not in ap_pend:
                ap_pend[i + 1] = emit_tile_ln(i + 1)
            if sub < 2:
                continue
            ev_tp = []
            slot_ev = [ev_tp_prev[6], ev_tp_prev[7]]
            for j in range(8):
                pslc = self.pp[2 + (j % 2)][:, 0:128]
                ev_mm = self.mm(pslc, self.xln[par][:, j * 128:(j + 1) * 128],
                                self.c_i128[:],
                                deps=(ev_ap, self.ev_cs, slot_ev[j % 2]))
                ev_e = self.act(
                    lambda e, j=j, par=par, pslc=pslc: e.activation(
                        self.xTt[par][:, j * 128:(j + 1) * 128], pslc, AF.Copy),
                    deps=(ev_mm,))
                ev_tp.append(ev_e)
                slot_ev[j % 2] = ev_e
            ev_tp_prev = ev_tp
            if sub < 3:
                continue
            evs = []
            for half in range(2):
                pb = self.pp[half]
                dep0 = [ev_tp[j] for j in range(8)] + [ev_ffn_prev[half]]
                ev_last = None
                for j in range(8):
                    ev_last = self.mm(
                        pb[:, 0:512], self.xTt[par][:, j * 128:(j + 1) * 128],
                        self.bwt(j)[:, half * 512:(half + 1) * 512],
                        deps=(tuple(dep0) + (self.ev_piece[j],)) if j == 0
                        else (self.ev_piece[j],), start=(j == 0), stop=False)
                ev_last = self.mm(
                    pb[:, 0:512], self.c_ones[0:1, :],
                    self.c_fbias[0:1, half * 512:(half + 1) * 512],
                    deps=(self.ev_cg,), start=False, stop=True)
                evs.append(self.dve(
                    lambda e, pb=pb, i=i, half=half: e.scalar_tensor_tensor(
                        self.Rt[i][:, half * 512:(half + 1) * 512], pb[:, 0:512], 0.5,
                        self.Rt[i][:, half * 512:(half + 1) * 512], ALU.mult, ALU.add),
                    deps=(ev_last,)))
            ev_ffn_prev = evs
            if which == 1:
                self.ev_ffn[i] = evs[-1]
        self.ev_ffn_last = evs

    def p3_ln2(self):
        s = self.s
        ev_t = [None, None]
        ev_dump = [None, None]
        self.ev_x2t = []
        ap_pend = {0: self._emit_ln(0, 0, (self.ev_ffn[0],))}
        for i in range(NTT):
            par = i % 2
            ev_ap = ap_pend.pop(i)
            if i + 1 < NTT:
                ap_pend[i + 1] = self._emit_ln(i + 1, (i + 1) % 2,
                                               (self.ev_ffn[i + 1],))
            for j in range(8):
                pslc = self.pp[4 + (j % 2)][:, 0:128]
                ev_mm = self.mm(pslc, self.xln[par][:, j * 128:(j + 1) * 128],
                                self.c_i128[:], deps=(ev_ap, ev_t[j % 2]))
                ev_t[j % 2] = self.act(
                    lambda e, j=j, par=par, pslc=pslc: e.activation(
                        self.xTt[par][:, j * 128:(j + 1) * 128], pslc, AF.Copy),
                    deps=(ev_mm, ev_dump[par]))
            ev_dump[par] = self.dma(
                QS, self.scr_x2t[:, :, i * 128:(i + 1) * 128].rearrange("a p c -> p a c"),
                self.xTt[par][:].rearrange("p (a c) -> p a c", a=8),
                deps=(ev_t[0], ev_t[1]))
            self.ev_x2t.append(ev_dump[par])
        self.ev_x2t_all = tuple(self.ev_x2t)
        self.ev_x2t_done = self.ev_x2t[-1]

    P4SUB = 4
    P4OPS = 4
    P5SUB = 4
    CHSER = 0

    def p4_gbeta(self):
        s = self.s
        evz = []
        for sl4 in range(4):
            ev_ld = self.dma(QS, self.x2s[:],
                             self.scr_x2t[:, :, sl4 * 512:(sl4 + 1) * 512]
                             .rearrange("a p c -> p a c"),
                             deps=tuple(self.ev_x2t[sl4 * 4:sl4 * 4 + 4]) + (s.now(ET),))
            for gb in range(2):
                pb = self.pp[gb]
                ev_last = None
                for j in range(8):
                    ev_last = self.mm(
                        pb[0:16, 0:512],
                        self.c_gbw[:, j * 32 + gb * 16: j * 32 + gb * 16 + 16],
                        self.x2s[:, j * 512:(j + 1) * 512],
                        deps=(ev_ld, self.ev_cg, s.now(EV), s.now(EA)) if j == 0 else (),
                        start=(j == 0), stop=(j == 7))
                plane = self.gc if gb == 0 else self.lnbc
                za, zb2 = self.ztmp
                if MK.P4SUB < 1:
                    continue
                e1 = self.act(lambda e, pb=pb: e.activation(za[:], pb[0:16, 0:512],
                                                            AF.Abs), deps=(ev_last,))
                e1 = self.act(lambda e: e.activation(za[:], za[:], AF.Exp, scale=-1.0),
                              deps=(e1,))
                e1 = self.act(lambda e: e.activation(za[:], za[:], AF.Ln,
                                                     bias=self.cst[0:16, 1:2]),
                              deps=(e1,))
                e3 = e1
                if MK.P4OPS >= 4:
                    e4 = self.dve(lambda e, pb=pb: e.tensor_scalar_min(
                        zb2[:], pb[0:16, 0:512], 0.0), deps=(ev_last, e1))
                else:
                    e4 = self.dve(lambda e, pb=pb: e.tensor_copy(zb2[:], za[:]),
                                  deps=(e1,))
                evz.append(self.dve(lambda e, plane=plane, sl4=sl4: e.tensor_sub(
                    plane[:, sl4 * 512:(sl4 + 1) * 512], zb2[:], za[:]),
                    deps=(e3, e4, s.now(EV))))
        if MK.P4SUB < 2:
            self.ev_tcols = None
            self.ev_lgflat = None
            return
        ev_cs = []
        for ch in range(NCH):
            ev_cs.append(self.dve(lambda e, ch=ch: e.tensor_tensor_scan(
                self.lgc[:, ch * 64:(ch + 1) * 64], self.gc[:, ch * 64:(ch + 1) * 64],
                self.gc[:, ch * 64:(ch + 1) * 64], 0.0, ALU.add, ALU.bypass),
                deps=evz if ch == 0 else ()))
        ez1 = self.dve(lambda e: e.tensor_copy(self.zstack[0:16, :], self.gc[:]),
                       deps=tuple(ev_cs))
        ez2 = self.dve(lambda e: e.tensor_copy(self.zstack[32:48, :], self.lnbc[:]),
                       deps=(ez1,))
        ez3 = self.dve(lambda e: e.tensor_copy(self.zstack[64:80, :], self.lgc[:]),
                       deps=(ez2,))
        if MK.P4SUB < 3:
            self.ev_tcols = None
            self.ev_lgflat = None
            return
        evt = None
        ev_tc = None
        for ch in range(NCH):
            pv = self.pp[ch % 2][0:64, 0:96]
            ev_mm = self.mm(pv, self.zstack[:, ch * 64:(ch + 1) * 64], self.c_i96f[:],
                            deps=(ez1, ez2, ez3, evt, self.ev_cs, ev_tc))
            evt = self.dve(lambda e, pv=pv: e.tensor_copy(self.tcst[:], pv),
                           deps=(ev_mm, ev_tc))
            ev_tc = self.dma(QS, self.scr_tc[ch], self.tcst[:], deps=(evt,))
        self.ev_tcols = ev_tc
        if MK.P4SUB < 4:
            self.ev_lgflat = None
            return
        self.ev_lgflat = self.dma(QS, self.scr_flat[0], self.lgc[:], deps=(ev_cs[-1],))

    def p5_delta(self, upto):
        s = self.s
        self.ev_S = [self.ev_zero] * H
        self.ev_wo_R = None
        self.ev_oc_free = None
        nslab = NSLAB if upto >= 6 else 1
        self.nslab_built = nslab
        for sl in range(nslab):
            self._slab_qkv(sl)
            if MK.P5SUB >= 2:
                self._slab_prep(sl)
            if MK.P5SUB >= 3:
                self._slab_solve(sl)
            if MK.P5SUB >= 4:
                self._slab_chain(sl)
    def _slab_qkv(self, sl):
        s = self.s
        ev_x2 = self.dma(QS, self.x2s[:],
                         self.scr_x2t[:, :, sl * 512:(sl + 1) * 512]
                         .rearrange("a p c -> p a c"),
                         deps=tuple(self.ev_x2t[sl * 4:sl * 4 + 4]) + (s.now(ET),))
        strips = {"q": self.qcs, "k": self.kcs, "v": self.vcs}
        wbase = {"q": 0, "k": 4, "v": 8}
        self.ev_strip = {}
        ev_conv_last = None
        for wi, w in enumerate("qkv"):
            wsrc = self.wT[w].rearrange("(j p) c -> p j c", p=128)
            ev_w = {}
            for ct in range(8):
                qt = ct // 2
                slot = self.wring[qt % 2]
                if ct % 2 == 0:
                    ev_w[qt] = self.dma(
                        QG, slot[:].rearrange("p (j cc) -> p j cc", j=8),
                        wsrc[:, :, qt * 256:(qt + 1) * 256],
                        deps=(s.now(ET),))
                pb = self.pp[ct % 2]
                dep0 = (ev_x2, ev_w[qt], s.now(EV), s.now(EP), s.now(EA))
                ev_last = None
                for j in range(8):
                    c0 = j * 256 + (ct % 2) * 128
                    ev_last = self.mm(
                        pb[:, 0:512], slot[:, c0:c0 + 128],
                        self.x2s[:, j * 512:(j + 1) * 512],
                        deps=dep0 if j == 0 else (), start=(j == 0), stop=(j == 7))
                tl = self.tails[:, (wi * 8 + ct) * 3:(wi * 8 + ct) * 3 + 3]
                dpre = [ev_conv_last]
                if wi == 0 and ct == 0:
                    dpre += [s.now(ET), self.ev_lgflat, self.ev_tcols,
                             s.now(EV), s.now(EA)]
                e_pref = self.dve(lambda e, tl=tl: e.tensor_copy(self.rawt[:, 0:3], tl),
                                  deps=dpre)
                e_raw = self.dve(lambda e, pb=pb: e.tensor_copy(
                    self.rawt[:, 3:3 + 512], pb[:, 0:512]), deps=(ev_last,))
                self.dve(lambda e, tl=tl: e.tensor_copy(tl, self.rawt[:, 512:515]),
                         deps=(e_raw, e_pref))

                def wc(j_, ct=ct, w=w):
                    cix = ct * 12 + wbase[w] + j_
                    return self.c_convw[:, cix:cix + 1]
                bc = self.c_convb[:, ct * 3 + wi:ct * 3 + wi + 1]
                e_c0 = self.act(lambda e, wc=wc, bc=bc: e.activation(
                    self.cacc[:], self.rawt[:, 3:515], AF.Identity, bias=bc, scale=wc(3)),
                    deps=(e_pref, e_raw, self.ev_cs))
                e_c1 = self.dve(lambda e, wc=wc: e.scalar_tensor_tensor(
                    self.cacc[:], self.rawt[:, 2:514], wc(2), self.cacc[:],
                    ALU.mult, ALU.add), deps=(e_c0,))
                e_c2 = self.dve(lambda e, wc=wc: e.scalar_tensor_tensor(
                    self.cacc[:], self.rawt[:, 1:513], wc(1), self.cacc[:],
                    ALU.mult, ALU.add), deps=(e_c1,))
                e_c3 = self.dve(lambda e, wc=wc: e.scalar_tensor_tensor(
                    self.cacc[:], self.rawt[:, 0:512], wc(0), self.cacc[:],
                    ALU.mult, ALU.add), deps=(e_c2,))
                ev_sp = self.dve(lambda e, w=w, ct=ct, strips=strips: e.tensor_copy(
                    strips[w][:, ct * 512:(ct + 1) * 512], self.cacc[:]), deps=(e_c3,))
                self.ev_strip[(w, ct)] = ev_sp
                ev_conv_last = ev_sp
        self.ev_strip_done = ev_conv_last

    def _kc(self, h, chl, strip):
        return strip[hr(h):hr(h) + 64,
                     (h // 2) * 512 + chl * 64:(h // 2) * 512 + chl * 64 + 64]

    def ccv(self, chl, slot):
        return self.cc[chl][:].rearrange("p (g s) -> p g s", s=10)[:, :, slot]

    def ccp(self, chl, slot, par):
        return self.cc[chl][par * 64:par * 64 + 64].rearrange(
            "p (g s) -> p g s", s=10)[:, :, slot]

    def ccol(self, chl, slot, h):
        pa = h // 2
        r0 = hr(h)
        return self.cc[chl][r0:r0 + 64, pa * 10 + slot:pa * 10 + slot + 1]

    def _slab_prep(self, sl):
        s = self.s
        self.ev_prep = []
        if not hasattr(self, "ev_mp_out"):
            self.ev_mp_out = [(), ()]
            self.ev_lgrd = ()
            self.ev_tc2rd = [(), ()]
            self.ev_ppK_rd = None
            self.ev_ppV_rd = None
            self.ev_kc_prev = None
            self.ev_bc_prev = None
            self.ev_lgt_rd = ()
            self.ev_ktm_rd = ()
            self.ev_ksq_rd = None
        bd = getattr(self, "bankdep", {})
        ppK, ppV = self.pp[2], self.pp[3]
        for chl in range(SLAB):
            ch = sl * SLAB + chl
            ev_tc2a = self.dma(QA, self.tcols2[chl % 2][0:64, :], self.scr_tc[ch],
                               deps=(self.ev_tcols,) + tuple(self.ev_tc2rd[chl % 2]))
            ev_tc2 = (ev_tc2a, self.dma(
                QA, self.tcols2[chl % 2][64:128, :].rearrange(
                    "p (b c) -> p b c", c=32)[:, :, 0:8],
                AP(self.scr_tc.tensor, self.scr_tc.offset + ch * 64 * 96 + 8,
                   [[96, 64], [32, 3], [1, 8]])))
            ev_lgt = self.dma(QA, self.lgT2[:],
                              self.scr_flat[0][:, ch * 64:ch * 64 + 64],
                              deps=(self.ev_lgflat,) + tuple(self.ev_lgt_rd))
            e_h = self.dve(lambda e: e.tensor_copy(self.lgs[:, 0:64], self.lgT2[:]),
                           deps=(ev_lgt, self.ev_bc_prev))
            e_h = self.dve(lambda e: e.tensor_sub(
                self.lgT2[:], self.lgT2[:], self.lgs[:, 0:64]), deps=(e_h,))
            e_h = self.dve(lambda e: e.tensor_copy(self.lgs[:, 64:128], self.lgT2[:]),
                           deps=(e_h,))
            e_h = self.dve(lambda e: e.tensor_sub(
                self.lgT2[:], self.lgT2[:], self.lgs[:, 64:128]), deps=(e_h,))
            e_h = self.dve(lambda e: e.tensor_copy(self.lgs[:, 128:192], self.lgT2[:]),
                           deps=(e_h,))
            self.ev_lgt_rd = (e_h,)
            ev_bc = None
            for pa in range(8):
                for k3 in range(3):
                    ev_bc = self.mm(
                        ppV[:, pa * 64:pa * 64 + 64],
                        self.c_selb[:, pa * 128:(pa + 1) * 128],
                        self.lgs[:, k3 * 64:k3 * 64 + 64],
                        deps=(e_h, self.ev_ppV_rd, bd.get("pO2"))
                        if pa == 0 and k3 == 0 else (),
                        start=(k3 == 0), stop=(k3 == 2))
            self.ev_bc_prev = ev_bc
            ev_lgrow = self.act(lambda e: e.activation(self.lgrow[:], ppV[:], AF.Copy),
                                deps=(ev_bc,) + tuple(self.ev_lgrd))
            ev_lg = (ev_lgrow,)
            ev_kt = []
            for h in range(H):
                r0 = hr(h)
                dep = (self.ev_strip[("k", h // 2)],)
                if h == 0:
                    dep += (self.ev_ppK_rd, bd.get("pO1"))
                ev_kt.append(self.mm(
                    ppK[r0:r0 + 64, (h // 2) * 64:(h // 2) * 64 + 64],
                    self._kc(h, chl, self.kcs), self.c_i64d[r0:r0 + 64, :],
                    deps=dep, tp=(r0, r0)))
            ev_ktmb = self.dve(
                lambda e: e.tensor_copy(self.ktmb[:], ppK[:]),
                deps=(ev_kt[-1], s.now(ET)) + tuple(self.ev_ktm_rd) if chl == 0
                else (ev_kt[-1], self.ev_kc_prev) + tuple(self.ev_ktm_rd))
            ev_sq = self.act(lambda e: e.activation(self.ksq[:], ppK[:], AF.Square),
                             deps=(ev_ktmb, self.ev_ksq_rd))
            ev_ss = self.dve(lambda e, chl=chl: e.reduce_sum(
                self.ccv(chl, 0), self.ksq[:].rearrange("p (g x) -> p g x", g=8),
                mybir.AxisListType.X), deps=(ev_sq, s.now(EV), s.now(EA)))
            self.ev_ksq_rd = ev_ss
            ev_ln = self.act(lambda e, chl=chl: e.activation(
                self.ccv(chl, 8), self.ccv(chl, 0), AF.Ln), deps=(ev_ss,))
            ev_lnrn = self.dve(lambda e, chl=chl: e.tensor_scalar(
                self.ccv(chl, 0), self.ccv(chl, 8), -0.5, LNRN_MAX,
                ALU.mult, ALU.min), deps=(ev_ln, ev_ss))
            ev_rn = self.act(lambda e, chl=chl: e.activation(
                self.ccv(chl, 9), self.ccv(chl, 0), AF.Exp), deps=(ev_lnrn,))
            ev_eL = self.act(lambda e, chl=chl: e.activation(
                self.ccv(chl, 5), self.lgrow[:].rearrange(
                    "p (g x) -> p g x", g=8)[:, :, 63], AF.Exp),
                deps=ev_lg + (ev_lnrn,))
            tc = lambda c0, chl=chl: self.tcols2[chl % 2][:, c0:c0 + 8]
            cp = lambda sl_, chl=chl: self.ccv(chl, sl_)
            ee1 = self.dve(lambda e, tc=tc, cp=cp: e.scalar_tensor_tensor(
                cp(8), tc(0), -1.0, tc(64), ALU.mult, ALU.add),
                deps=ev_tc2 + (ev_lnrn,))
            ev_m = self.dve(lambda e, tc=tc, cp=cp: e.tensor_add(
                cp(1), cp(8), tc(32)), deps=(ee1,))
            ev_e1 = self.act(lambda e, cp=cp: e.activation(cp(8), cp(1), AF.Exp),
                             deps=(ev_m,))
            ee3 = self.dve(lambda e, cp=cp: e.tensor_scalar_mul(cp(2), cp(8), -1.0),
                           deps=(ev_e1,))
            e_b = self.act(lambda e, tc=tc, cp=cp: e.activation(
                cp(3), tc(32), AF.Exp), deps=(ev_lnrn,))
            e_k = self.act(lambda e, tc=tc, cp=cp: e.activation(
                cp(6), tc(64), AF.Exp, bias=self.cst[:, 2:3]), deps=(ev_lnrn,))
            ev_E = self.dve(lambda e, tc=tc, cp=cp: e.tensor_scalar_mul(
                cp(4), tc(64), -1.0), deps=ev_tc2 + (ev_lnrn,))
            Lv = self.lgrow[:].rearrange("p (g x) -> p g x", g=8)[:, :, 63]
            ee4 = self.dve(lambda e, tc=tc, cp=cp: e.tensor_sub(
                cp(8), cp(0), tc(64)), deps=(ee3,))
            e_t = self.dve(lambda e, cp=cp, Lv=Lv: e.tensor_add(cp(8), cp(8), Lv),
                           deps=(ee4,) + ev_lg)
            ev_kh = self.act(lambda e, cp=cp: e.activation(cp(7), cp(8), AF.Exp),
                             deps=(e_t,))
            ev_ms = (ev_m,)
            ev_Es = (ev_E,)
            ev_khs = (ev_kh,)
            ev_ts = (e_t,)
            rdt = [ee1, ev_m, e_b, e_k, ev_E, ee4]
            ev_khat = self.dve(lambda e, chl=chl: e.tensor_mul(
                self.khat[chl][:].rearrange("p (g x) -> p g x", g=8),
                self.ktmb[:].rearrange("p (g x) -> p g x", g=8),
                self.ccv(chl, 7)[:, :, None].broadcast_to((128, 8, 64))),
                deps=tuple(ev_khs) + (ev_ktmb,))
            self.ev_ktm_rd = (ev_khat,)
            ev_dgr = self.dve(lambda e, chl=chl: e.tensor_mul(
                self.dgr[:].rearrange("p (g x) -> p g x", g=8),
                self.c_i64d[:][:, None, :].broadcast_to((128, 8, 64)),
                self.ccv(chl, 9)[:, :, None].broadcast_to((128, 8, 64))),
                deps=(ev_rn, self.ev_kc_prev if chl else s.now(ET)))
            ev_kc = None
            for h in range(H):
                pa = h // 2
                r0 = hr(h)
                ev_kc = self.mm(
                    ppK[r0:r0 + 64, pa * 64:pa * 64 + 64],
                    self.ktmb[r0:r0 + 64, pa * 64:pa * 64 + 64],
                    self.dgr[r0:r0 + 64, pa * 64:pa * 64 + 64],
                    deps=(ev_dgr, ev_ktmb, ev_sq) if h == 0 else (), tp=(r0, r0))
            self.ev_kc_prev = ev_kc
            ev_kce = self.dve(lambda e, chl=chl: e.tensor_copy(
                self.kcs[:].rearrange("p (g t) -> p g t", g=8)
                [:, :, chl * 64:chl * 64 + 64],
                ppK[:].rearrange("p (g x) -> p g x", g=8)), deps=(ev_kc,))
            ev_vt = []
            for h in range(H):
                r0 = hr(h)
                dep = (self.ev_strip[("v", h // 2)],)
                if h == 0:
                    dep += (ev_lgrow, self.ev_ppV_rd)
                ev_vt.append(self.mm(
                    ppV[r0:r0 + 64, (h // 2) * 64:(h // 2) * 64 + 64],
                    self._kc(h, chl, self.vcs), self.c_i64d[r0:r0 + 64, :],
                    deps=dep, tp=(r0, r0)))
            ev_vtm = self.dve(lambda e, chl=chl: e.tensor_copy(
                self.vtm[chl][:], ppV[:]), deps=(ev_vt[-1],))
            ev_g = ev_a = None
            for h in range(H):
                r0 = hr(h)
                ev_g = self.mm(
                    ppK[r0:r0 + 64, (h // 2) * 64:(h // 2) * 64 + 64],
                    self._kc(h, chl, self.kcs), self._kc(h, chl, self.kcs),
                    deps=(ev_kce,) if h == 0 else (), tp=(r0, r0))
            for h in range(H):
                r0 = hr(h)
                ev_a = self.mm(
                    ppV[r0:r0 + 64, (h // 2) * 64:(h // 2) * 64 + 64],
                    self._kc(h, chl, self.kcs), self._kc(h, chl, self.qcs),
                    deps=(ev_vtm,) if h == 0 else (), tp=(r0, r0))
            mskt = self.msk if chl % 2 == 0 else self.rbt
            mskv = lambda mskt=mskt: mskt[:].rearrange("p (g x) -> p g x", g=8)
            msk2v = lambda: self.msk2[:].rearrange("p (g x) -> p g x", g=8)
            lgv = lambda: self.lgrow[:].rearrange("p (g x) -> p g x", g=8)
            mb = self.ccv(chl, 1)[:, :, None].broadcast_to((128, 8, 64))
            ecb = self.ccv(chl, 4)[:, :, None].broadcast_to((128, 8, 64))
            mLv = self.c_maskLd[:][:, None, :].broadcast_to((128, 8, 64))
            mUv = self.c_maskUd[:][:, None, :].broadcast_to((128, 8, 64))
            e1 = self.dve(lambda e, mskv=mskv, mb=mb: e.scalar_tensor_tensor(
                mskv(), lgv(), -1.0, mb, ALU.mult, ALU.add),
                deps=ev_lg + tuple(ev_ms) + (s.now(EV),))
            e3 = self.dve(lambda e, mskv=mskv, mLv=mLv: e.tensor_add(
                mskv(), mskv(), mLv), deps=(e1,))
            e4 = self.act(lambda e, mskt=mskt: e.activation(mskt[:], mskt[:], AF.Exp),
                          deps=(e3,))
            e5 = self.dve(lambda e, chl=chl, mskt=mskt, ppK=ppK: e.tensor_mul(
                self.mps2[chl % 2][:], mskt[:], ppK[:]),
                deps=(e4, ev_g) + tuple(self.ev_mp_out[chl % 2]))
            self.ev_ppK_rd = e5
            f1 = self.dve(lambda e, msk2v=msk2v, ecb=ecb: e.tensor_add(
                msk2v(), lgv(), ecb), deps=ev_lg + tuple(ev_Es) + (s.now(EV),))
            f2 = self.dve(lambda e, msk2v=msk2v, mUv=mUv: e.tensor_add(
                msk2v(), msk2v(), mUv), deps=(f1,))
            f3 = self.act(lambda e: e.activation(self.msk2[:], self.msk2[:], AF.Exp),
                          deps=(f2,))
            f4 = self.dve(lambda e, chl=chl, ppV=ppV: e.tensor_mul(
                self.ams[chl][:], self.msk2[:], ppV[:]), deps=(f3, ev_a))
            self.ev_ppV_rd = f4
            ev_mpa = self.dma(QA, self.scr_mp[0, ch].rearrange("pa s r -> s pa r"),
                              self.mps2[chl % 2][0:64, :].rearrange(
                                  "s (pa r) -> s pa r", pa=8), deps=(e5,))
            ev_mp = (ev_mpa,
                     self.dma(QG, self.scr_mp[1, ch].rearrange("pa s r -> s pa r"),
                              self.mps2[chl % 2][64:128, :].rearrange(
                                  "s (pa r) -> s pa r", pa=8), deps=(e5,)))
            self.ev_mp_out[chl % 2] = ev_mp
            self.ev_lgrd = (e1, f1, ev_eL) + tuple(ev_ts)
            self.ev_tc2rd[chl % 2] = tuple(rdt)
            self.ev_prep.append((ev_mp, f4, ev_khat, ev_vtm))

    def _slab_solve(self, sl):
        s = self.s
        deps = sum((tuple(p[0]) for p in self.ev_prep), ()) + (
            s.now(ET), s.now(EV), s.now(EP))
        SMP = self.scr_mp
        rs = 512
        cs = 128 * rs
        base = sl * SLAB * cs
        CST, PST = NCH * 8 * 64 * 64, 8 * 64 * 64
        b0 = sl * SLAB * PST
        ev_g1 = []
        for par in range(2):
            for jb in range(4):
                ev_g1.append(self.dma(
                    QS, self.sstb[par * 64:par * 64 + 64,
                                  jb * 256:(jb + 1) * 256].rearrange(
                        "p (r c) -> p r c", r=16),
                    AP(SMP.tensor, SMP.offset + par * CST + b0 + jb * 1040,
                       [[4096, 64], [64, 16], [1, 16]]), deps=deps))
        ev_g1 = tuple(ev_g1)
        ev_g2 = []
        for par in range(2):
            for a in range(2):
                ev_g2.append(self.dma(
                    QS, self.stripA[par * 64:par * 64 + 16, :].rearrange(
                        "r (q a2 c) -> r q a2 c", a2=2, c=16)[:, :, a],
                    AP(SMP.tensor, SMP.offset + par * CST + b0
                       + (16 + 32 * a) * 64 + 32 * a,
                       [[64, 16], [4096, 64], [1, 16]]), deps=deps))
        ev_g2 = tuple(ev_g2)
        ev_i0 = self.dve(lambda e: e.memset(self.tstr[:], 0.0),
                         deps=(s.now(ET), s.now(QG)))
        ev_i1 = self.dve(lambda e: e.memset(
            self.tstr[:].rearrange("p (b r c) -> p b r c", b=4, r=16)
            .rearrange("p b r c -> p b r c")[:, :, 0, 0] if False else
            AP(self.tstr.ap().tensor, self.tstr.ap().offset,
               [self.tstr.ap().ap[0], [256, 4], [17, 16]]), 1.0), deps=(ev_i0,))
        ev_i2 = self.dve(lambda e: e.tensor_copy(self.sst[:], self.sstb[:]),
                         deps=ev_g1)
        ev_pr = (ev_i1, ev_i2)
        for r in range(1, 16):
            def fmul(e, r=r):
                tv = AP(self.tstr.ap().tensor, self.tstr.ap().offset,
                        [self.tstr.ap().ap[0], [256, 4], [1, 16], [16, r]])
                mv = AP(self.sst.ap().tensor, self.sst.ap().offset + r * 16,
                        [self.sst.ap().ap[0], [256, 4], [0, 16], [1, r]])
                ov = AP(self.subtmp.ap().tensor, self.subtmp.ap().offset,
                        [self.subtmp.ap().ap[0], [240, 4], [15, 16], [1, r]])
                return e.tensor_mul(ov, tv, mv)
            e1 = self.dve(fmul, deps=ev_pr)
            def fred(e, r=r):
                iv = AP(self.subtmp.ap().tensor, self.subtmp.ap().offset,
                        [self.subtmp.ap().ap[0], [240, 4], [15, 16], [1, r]])
                ov = AP(self.cacc.ap().tensor, self.cacc.ap().offset,
                        [self.cacc.ap().ap[0], [16, 4], [1, 16], [0, 1]])
                return e.reduce_sum(ov, iv, mybir.AxisListType.X)
            e2 = self.dve(fred, deps=(e1,))
            def fsub(e, r=r):
                rv = AP(self.tstr.ap().tensor, self.tstr.ap().offset + r * 16,
                        [self.tstr.ap().ap[0], [256, 4], [1, 16]])
                sv = AP(self.cacc.ap().tensor, self.cacc.ap().offset,
                        [self.cacc.ap().ap[0], [16, 4], [1, 16]])
                return e.tensor_sub(rv, rv, sv)
            ev_pr = (self.dve(fsub, deps=(e2,)),)
        ev_sub = (s.now(EV), s.now(EP))
        ev_tt = self.dve(lambda e: e.tensor_copy(
            self.tstrT[:].rearrange("p (b x y) -> p b x y", b=4, x=16),
            self.tstr[:].rearrange("p (b y x) -> p b x y", b=4, y=16)), deps=ev_sub)
        ev_d1 = self.dma(QS, self.scr_t16[sl],
                         self.tstr[:].rearrange("p (b x) -> p b x", b=4), deps=ev_sub)
        ev_d2 = self.dma(QS, self.scr_t16t[sl],
                         self.tstrT[:].rearrange("p (b x) -> p b x", b=4), deps=(ev_tt,))
        ev_z = self.dve(lambda e: e.memset(self.t32t[:], 0.0), deps=(s.now(ET),))
        ts = 4 * 256
        T16T = self.scr_t16t
        T16 = self.scr_t16
        lands = []
        for par in range(2):
            for a in range(2):
                p0 = par * 64 + a * 32
                dst = self.t32t[p0:p0 + 16, :].rearrange(
                    "r (q c) -> r q c", c=32)[:, :, 0:16]
                lands.append(self.dma(
                    QG, dst,
                    AP(T16T.tensor,
                       T16T.offset + sl * 128 * ts + par * 64 * ts + a * 512,
                       [[16, 16], [ts, 64], [1, 16]]),
                    deps=(ev_d2, ev_z)))
                dst = self.t32t[p0 + 16:p0 + 32, :].rearrange(
                    "r (q c) -> r q c", c=32)[:, :, 16:32]
                lands.append(self.dma(
                    QG, dst,
                    AP(T16T.tensor,
                       T16T.offset + sl * 128 * ts + par * 64 * ts + a * 512 + 256,
                       [[16, 16], [ts, 64], [1, 16]]),
                    deps=(ev_d2, ev_z)))
        for par in range(2):
            p0 = par * 64
            for a in range(2):
                dst = self.stripA[p0 + 32:p0 + 48, :].rearrange(
                    "r (q a2 c) -> r q a2 c", a2=2, c=16)[:, :, a]
                lands.append(self.dma(
                    QG, dst,
                    AP(T16.tensor,
                       T16.offset + sl * 128 * ts + par * 64 * ts + a * 512,
                       [[16, 16], [ts, 64], [1, 16]]),
                    deps=(ev_d1,) + ev_g2))
                dst = self.stripB[p0:p0 + 16, :].rearrange(
                    "r (q a2 c) -> r q a2 c", a2=2, c=16)[:, :, a]
                lands.append(self.dma(
                    QG, dst,
                    AP(T16T.tensor,
                       T16T.offset + sl * 128 * ts + par * 64 * ts + a * 512 + 256,
                       [[16, 16], [ts, 64], [1, 16]]),
                    deps=(ev_d2,)))
            lands.append(self.dma(
                QS, self.m32s[p0 + 32:p0 + 64, :].rearrange(
                    "i (q j) -> i q j", j=32),
                AP(SMP.tensor, SMP.offset + par * CST + b0 + 32 * 64,
                   [[64, 32], [4096, 64], [1, 32]]),
                deps=(deps + (s.now(ET),)) if par == 0 else ()))
        ev_land = tuple(lands) + ev_g2
        ev_x16 = []
        ev_ev = (s.now(EV),)
        for par in range(2):
            r0 = par * 64
            for a in range(2):
                for qh in range(2):
                    ev_mmb = []
                    for qq in range(32):
                        q = qh * 32 + qq
                        col = (q * 2 + a) * 16
                        d0 = ()
                        if qq == 0:
                            d0 = (ev_land + ev_ev) if (par == 0 and a == 0
                                                       and qh == 0) else ev_ev
                        ev_mmb.append(self.mm(
                            self.pp[4][r0:r0 + 16, qq * 16:qq * 16 + 16],
                            self.stripA[r0:r0 + 16, col:col + 16],
                            self.stripB[r0:r0 + 16, col:col + 16],
                            deps=d0, tp=(r0, r0)))
                    dstv = self.stripB[r0 + 32:r0 + 48].rearrange(
                        "r (q a c) -> r q a c", a=2, c=16)[:, qh * 32:qh * 32 + 32, a]
                    ev_ev0 = self.dve(lambda e, dstv=dstv, r0=r0: e.tensor_copy(
                        dstv, self.pp[4][r0:r0 + 16, 0:512].rearrange(
                            "p (b x) -> p b x", b=32)), deps=(ev_mmb[-1],))
                    ev_ev = (ev_ev0,)
                    ev_mmc = []
                    for qq in range(32):
                        q = qh * 32 + qq
                        col = (q * 2 + a) * 16
                        ev_mmc.append(self.mm(
                            self.pp[5][r0:r0 + 16, qq * 16:qq * 16 + 16],
                            self.stripA[r0 + 32:r0 + 48, col:col + 16],
                            self.stripB[r0 + 32:r0 + 48, col:col + 16],
                            deps=(ev_ev0, ev_x16[-1]) if qq == 0 and ev_x16
                            else ((ev_ev0,) if qq == 0 else ()), tp=(r0 + 32, r0)))
                    dstx = self.t32t[r0 + a * 32:r0 + a * 32 + 16].rearrange(
                        "r (q c) -> r q c", c=32)[:, qh * 32:qh * 32 + 32, 16:32]
                    ev_x16.append(self.dve(lambda e, dstx=dstx, r0=r0:
                                           e.tensor_scalar_mul(
                        dstx, self.pp[5][r0:r0 + 16, 0:512].rearrange(
                            "p (b x) -> p b x", b=32), -1.0), deps=(ev_mmc[-1],)))
        ev_zs = self.dve(lambda e: e.transpose(self.zstr[:], self.t32t[:]),
                         deps=tuple(ev_x16))
        ev_x32 = []
        ev_pe = None
        for par in range(2):
            r0 = par * 64
            for qh in range(4):
                ev_pm = []
                for qq in range(16):
                    q = qh * 16 + qq
                    ev_pm.append(self.mm(
                        self.pp[4][r0 + 32:r0 + 64, qq * 32:qq * 32 + 32],
                        self.m32s[r0 + 32:r0 + 64, q * 32:q * 32 + 32],
                        self.t32t[r0 + 32:r0 + 64, q * 32:q * 32 + 32],
                        deps=(ev_zs, ev_pe) if qq == 0 else (),
                        tp=(r0 + 32, r0 + 32)))
                ev_pe = self.dve(lambda e, r0=r0, qh=qh: e.tensor_copy(
                    self.pxs[r0:r0 + 32, qh * 512:(qh + 1) * 512],
                    self.pp[4][r0 + 32:r0 + 64, 0:512]), deps=(ev_pm[-1],))
                ev_xm = []
                for qq in range(16):
                    q = qh * 16 + qq
                    ev_xm.append(self.mm(
                        self.pp[5][r0:r0 + 32, qq * 32:qq * 32 + 32],
                        self.zstr[r0:r0 + 32, q * 32:q * 32 + 32],
                        self.pxs[r0:r0 + 32, qh * 512 + qq * 32:qh * 512 + qq * 32 + 32],
                        deps=(ev_pe, ev_x32[-1] if ev_x32 else None)
                        if qq == 0 else (), tp=(r0, r0)))
                ev_x32.append(self.dve(lambda e, r0=r0, qh=qh: e.tensor_scalar_mul(
                    self.pxs[r0:r0 + 32, qh * 512:(qh + 1) * 512],
                    self.pp[5][r0:r0 + 32, 0:512], -1.0), deps=(ev_xm[-1],)))
        for par in range(2):
            r0 = par * 64
            ev_x32.append(self.dve(
                lambda e, r0=r0: e.tensor_copy(
                    self.pxs[r0 + 32:r0 + 64, 0:2048],
                    self.t32t[r0 + 32:r0 + 64, 0:2048]),
                deps=tuple(ev_x32)))
        self.ev_solve = tuple(ev_x32)

    def _slab_chain(self, sl):
        s = self.s
        bar = tuple(s.barrier())
        ev_wo = {}
        pY, pU, pO1, pO2, pDS = (self.pp[0], self.pp[1], self.pp[2],
                                 self.pp[3], self.pp[4])
        if not hasattr(self, "bankdep"):
            self.bankdep = {}
        bd = self.bankdep
        gv = lambda t: t[:].rearrange("p (g x) -> p g x", g=8)
        for chl in range(SLAB):
            ch = sl * SLAB + chl
            cb = lambda slot, chl=chl: self.ccv(chl, slot)[:, :, None] \
                .broadcast_to((128, 8, 64))
            evY = evO1 = None
            for h in range(H):
                par, pa = h % 2, h // 2
                r0 = 64 * par
                sl2 = lambda t, r0=r0, pa=pa: t[r0:r0 + 64, pa * 64:pa * 64 + 64]
                kc = self._kc(h, chl, self.kcs)
                qc = self._kc(h, chl, self.qcs)
                dep0 = bar + (self.ev_S[h], self.ev_wo_R, self.ev_solve[-1],
                              bd.get("pY"), bd.get("pO1")) if h == 0 else ()
                evY = self.mm(sl2(pY), kc, sl2(self.Sb), deps=dep0, tp=(r0, r0))
                evO1 = self.mm(sl2(pO1), qc, sl2(self.Sb), deps=(), tp=(r0, r0))
            e_r1 = self.dve(lambda e, chl=chl, cb=cb: e.tensor_mul(
                gv(self.rbt), gv(pY), cb(2)), deps=(evY, evO1))
            e_m = self.dve(lambda e, chl=chl, cb=cb: e.tensor_mul(
                gv(self.msk), gv(self.vtm[chl]), cb(3)),
                deps=(self.ev_prep[chl][3],))
            e_r2 = self.dve(lambda e: e.tensor_add(gv(self.rbf), gv(self.msk),
                                                   gv(self.rbt)),
                            deps=(e_r1, e_m))
            bd["msk"] = e_r2
            evA = None
            for h in range(H):
                par, pa = h % 2, h // 2
                r0 = 64 * par
                q = chl * 8 + pa
                d0 = (e_r2, bd.get("pU")) if h == 0 else ()
                self.mm(pU[r0:r0 + 32, pa * 64:pa * 64 + 64],
                        self.t32t[r0:r0 + 32, q * 32:q * 32 + 32],
                        self.rbf[r0:r0 + 32, pa * 64:pa * 64 + 64],
                        deps=d0, tp=(r0, r0))
                evA = self.mm(pU[r0 + 32:r0 + 64, pa * 64:pa * 64 + 64],
                              self.pxs[r0:r0 + 64, q * 32:q * 32 + 32],
                              self.rbf[r0:r0 + 64, pa * 64:pa * 64 + 64],
                              start=True, stop=True, tp=(r0, r0 + 32))
            evU = self.act(lambda e: e.activation(self.ubf[:], pU[:], AF.Copy),
                           deps=(evA,))
            evO2 = evDS = None
            for h in range(H):
                par, pa = h % 2, h // 2
                r0 = 64 * par
                sl2 = lambda t, r0=r0, pa=pa: t[r0:r0 + 64, pa * 64:pa * 64 + 64]
                d0 = (evU, bd.get("pO2"), bd.get("pDS")) if h == 0 else ()
                evO2 = self.mm(sl2(pO2), sl2(self.ams[chl]), sl2(self.ubf),
                               deps=d0, tp=(r0, r0))
                evDS = self.mm(sl2(pDS), sl2(self.khat[chl]), sl2(self.ubf),
                               deps=(), tp=(r0, r0))
            e_o1 = self.dve(lambda e, cb=cb: e.tensor_mul(
                gv(self.rbt), gv(pO1), cb(6)), deps=(evO2, bd.get("rbt")))
            e_oc = self.dve(lambda e, chl=chl: e.tensor_add(
                self.oc[chl % 2][:], self.rbt[:], pO2[:]),
                deps=(e_o1, evO2, self.ev_oc_free))
            bd["rbt"] = e_oc
            e_s1 = self.pool(lambda e, cb=cb: e.tensor_mul(
                gv(self.Sm), gv(self.Sm), cb(5)),
                deps=(self.ev_S[0], bd.get("sm")))
            e_s2 = self.dve(lambda e: e.tensor_add(self.Sm[:], self.Sm[:], pDS[:]),
                            deps=(e_s1, evDS))
            bd["sm"] = e_s2
            ev_cast = self.act(lambda e: e.activation(self.Sb[:], self.Sm[:], AF.Copy),
                               deps=(e_s2,))
            for h in range(H):
                self.ev_S[h] = ev_cast
            bd["pY"] = e_r1
            bd["pO1"] = e_o1
            bd["pU"] = evU
            bd["pO2"] = e_oc
            bd["pDS"] = e_s2
            if chl % 2 == 1:
                self._pair_wo(sl, chl, ev_wo)

    def _pair_wo(self, sl, chl, ev_wo):
        s = self.s
        ch = sl * SLAB + chl
        ti = (ch - 1) // 2
        wsrc = self.wT["o"].rearrange("(j p) c -> p j c", p=128)
        ev_ot = []
        for h in range(H):
            par, pa = h % 2, h // 2
            r0 = 64 * par
            for cpar in range(2):
                pb = self.pp[4 + pa // 4]
                ev_ot.append(self.mm(
                    pb[r0:r0 + 64, (pa % 4) * 128 + cpar * 64:
                       (pa % 4) * 128 + cpar * 64 + 64],
                    self.oc[(chl - 1 + cpar) % 2][r0:r0 + 64, pa * 64:pa * 64 + 64],
                    self.c_i64d[r0:r0 + 64, :],
                    deps=(s.now(EV),) if (h == 0 and cpar == 0) else (),
                    tp=(r0, r0)))
        ev_oT1 = self.dve(lambda e: e.tensor_copy(self.oT[:, 0:512], self.pp[4][:]),
                          deps=(ev_ot[-1],))
        ev_oT2 = self.dve(lambda e: e.tensor_copy(self.oT[:, 512:1024], self.pp[5][:]))
        self.ev_oc_free = ev_oT2
        ev_last = [None, None]
        for j in range(8):
            qt = j // 2
            slot = self.wring[qt % 2]
            if j % 2 == 0:
                ev_wo[qt] = self.dma(
                    QG, slot[:].rearrange("p (jj c) -> p jj c", jj=2),
                    wsrc[:, qt * 2:qt * 2 + 2, :], deps=(s.now(ET),))
            for half in range(2):
                c0 = (j % 2) * 1024 + half * 512
                ev_last[half] = self.mm(
                    self.pp[2 + half][:, 0:512], self.oT[:, j * 128:(j + 1) * 128],
                    slot[:, c0:c0 + 512],
                    deps=(ev_oT1, ev_oT2, ev_wo[qt], s.now(EV)) if j == 0 else
                    ((ev_wo[qt],) if j % 2 == 0 and half == 0 else ()),
                    start=(j == 0), stop=(j == 7))
        for half in range(2):
            self.ev_wo_R = self.dve(lambda e, ti=ti, half=half: e.tensor_add(
                self.Rt[ti][:, half * 512:(half + 1) * 512],
                self.Rt[ti][:, half * 512:(half + 1) * 512],
                self.pp[2 + half][:, 0:512]), deps=(ev_last[half],))

    def p9_out(self):
        s = self.s
        deps = s.barrier()
        for i in range(NTT):
            self.dma(QS, self.out_x[i * 128:(i + 1) * 128, :], self.Rt[i][:],
                     deps=deps if i == 0 else ())
        self.s.emit("gpsimd", lambda e: e.engine_nop(), deps=(s.now(QS),), sig=EP)

    def p_debug(self):
        s = self.s
        deps = s.barrier()
        first = True
        for key, get_ap, shape, dtype in self.debug:
            ap = get_ap(self)
            self.dma(QS, self.dbg_out[key], ap, deps=deps if first else ())
            first = False
        if self.debug:
            self.s.emit("gpsimd", lambda e: e.engine_nop(), deps=(s.now(QS),), sig=EP)

    def finalize(self):
        nc, s = self.nc, self.s
        with contextlib.ExitStack() as es:
            for name in list(s.cnt):
                s.sems[name] = es.enter_context(nc.semaphore(name))
            with nc.Block() as block:
                block.tensor(lambda e: s.replay("tensor", e))
                block.vector(lambda e: s.replay("vector", e))
                block.scalar(lambda e: s.replay("scalar", e))
                block.gpsimd(lambda e: s.replay("gpsimd", e))
                block.sync(lambda e: s.replay("sync", e))


AUTOSER_MODE = False
PERM_GB = [0, 2, 4, 6, 8, 10, 12, 14, 1, 3, 5, 7, 9, 11, 13, 15]


def _stage_inputs(inp):
    bf = ml_dtypes.bfloat16
    f32 = np.float32
    x = np.ascontiguousarray(inp["L0_attn_x"], dtype=f32)
    com = {}
    com["wqT"] = np.ascontiguousarray(np.asarray(inp["L0_attn_Wq"], f32).T)
    com["wkT"] = np.ascontiguousarray(np.asarray(inp["L0_attn_Wk"], f32).T)
    com["wvT"] = np.ascontiguousarray(np.asarray(inp["L0_attn_Wv"], f32).T)
    com["woT"] = np.ascontiguousarray(np.asarray(inp["L0_attn_Wo"], f32).T)
    wg = np.asarray(inp["L0_attn_Wg"], f32).T[:, PERM_GB]
    wb = np.asarray(inp["L0_attn_Wb"], f32).T[:, PERM_GB]
    com["wgbT"] = np.ascontiguousarray(np.concatenate([wg, wb], 1))
    cw = np.zeros((1024, 12), f32)
    cw[:, 0:4] = np.asarray(inp["L0_attn_cq_w"], f32)[:, 0, :]
    cw[:, 4:8] = np.asarray(inp["L0_attn_ck_w"], f32)[:, 0, :]
    cw[:, 8:12] = np.asarray(inp["L0_attn_cv_w"], f32)[:, 0, :]
    com["convw"] = np.ascontiguousarray(cw.reshape(8, 128, 12).transpose(1, 0, 2)
                                        .reshape(128, 96))
    cb = np.stack([np.asarray(inp["L0_attn_cq_b"], f32),
                   np.asarray(inp["L0_attn_ck_b"], f32),
                   np.asarray(inp["L0_attn_cv_b"], f32)], 1)
    com["convb"] = np.ascontiguousarray(cb.reshape(8, 128, 3).transpose(1, 0, 2)
                                        .reshape(128, 24))
    g1 = []
    g2 = []
    for k1, k2 in (("L0_ffn_G1_R", "L0_ffn_G2_R"), ("L0_ffn_G1_I", "L0_ffn_G2_I")):
        G1 = np.asarray(inp[k1], f32)[0]
        g1.append(np.ascontiguousarray(G1.transpose(1, 2, 0).reshape(4, 1024)))
        G2 = np.asarray(inp[k2], f32)[:, :, 0, :]
        g2.append(np.ascontiguousarray(G2.reshape(4, 256)))
    com["g1p"] = np.ascontiguousarray(np.stack(g1))
    com["g2p"] = np.ascontiguousarray(np.stack(g2))
    com["fbias"] = np.ascontiguousarray(np.concatenate(
        [np.asarray(inp["L0_ffn_bias_real"], f32),
         np.asarray(inp["L0_ffn_bias_imag"], f32)])[None, :])
    com["ident128"] = np.eye(128, dtype=f32).astype(bf)
    com["ident64d"] = np.ascontiguousarray(
        np.concatenate([np.eye(64, dtype=f32)] * 2, 0)).astype(bf)
    com["ident96f"] = np.eye(96, dtype=f32)
    com["ident128f"] = np.eye(128, dtype=f32)
    sel = np.zeros((16, 1024), f32)
    for pa in range(8):
        for s in range(128):
            sel[pa + 8 * (s // 64), pa * 128 + s] = 1.0
    com["selb3"] = sel.astype(bf)
    r = np.arange(64)
    ml64 = np.where(r[None, :] < r[:, None], 0.0, NEG).astype(f32)
    mu64 = np.where(r[:, None] <= r[None, :], LN_KD, NEG).astype(f32)
    com["maskLd"] = np.ascontiguousarray(np.concatenate([ml64, ml64], 0))
    com["maskUd"] = np.ascontiguousarray(np.concatenate([mu64, mu64], 0))
    in_maps = []
    for b in range(8):
        m = dict(com)
        m["x"] = np.ascontiguousarray(x[b])
        in_maps.append(m)
    return in_maps


_BUILT = {}


def _get_nc(upto=9, debug=()):
    key = (upto, tuple(k for k, _, _, _ in debug))
    if key not in _BUILT:
        mk = MK(debug=debug)
        mk.build(upto=upto)
        _BUILT[key] = mk
    return _BUILT[key]


def kernel(**inputs):
    Sched.AUTOSER = AUTOSER_MODE
    in_maps = _stage_inputs(inputs)
    mk = _get_nc()
    res = run_bass_kernel_spmd(mk.nc, in_maps, core_ids=list(range(8)))
    return np.stack([np.asarray(r["out"], np.float32) for r in res.results])


# version 24
# speedup vs baseline: 1.1714x; 1.0093x over previous
import contextlib
import numpy as np
import ml_dtypes
import concourse.bass as bass
from concourse.bass import AP
import concourse.mybir as mybir
from concourse.bass_utils import run_bass_kernel_spmd

F32 = mybir.dt.float32
BF16 = mybir.dt.bfloat16
AF = mybir.ActivationFunctionType
ALU = mybir.AluOpType

T, D, H, Kd = 2048, 1024, 16, 64
C = 64
NCH = T // C
SLAB = 8
NSLAB = NCH // SLAB
TSLAB = C * SLAB
NTT = T // 128
NEG = -1.0e30
LN_KD = float(np.log(Kd ** -0.5))
LNRN_MAX = float(np.log(1e12))
EPS_LN = 1e-5

ENGINES = ("tensor", "vector", "scalar", "gpsimd", "sync")
QS, QG, QA = "qs", "qg", "qa"
ET, EV, EA, EP = "et", "ev", "ea", "ep"


class Sched:
    NPOOL = 8

    def __init__(self):
        self.prog = {e: [] for e in ENGINES}
        self.sems = {}
        self.cnt = {}
        self.lastwait = {}
        self.lastop = {}
        self.qpool = {}
        self.quse = {}
        for q in (QS, QG, QA):
            self.qpool[q] = ["%s%d" % (q, i) for i in range(self.NPOOL)]
            self.quse[q] = 0
            for nm in self.qpool[q]:
                self.sem(nm)

    def sem(self, name):
        self.cnt.setdefault(name, 0)

    def wait(self, eng, ev):
        if ev is None:
            return
        if isinstance(ev, (list, tuple)) and ev and isinstance(ev[0], (list, tuple)):
            for e2 in ev:
                self.wait(eng, e2)
            return
        sem, val = ev
        if val <= 0:
            return
        key = (eng, sem)
        if self.lastwait.get(key, 0) >= val:
            return
        self.lastwait[key] = val
        self.prog[eng].append(("w", sem, val))

    AUTOSER = True

    def emit(self, eng, fn, deps=(), sig=None, dma=False):
        if Sched.AUTOSER and not dma and eng in ("vector", "scalar", "gpsimd"):
            self.wait(eng, self.lastop.get(eng))
        for d in deps:
            self.wait(eng, d)
        ev = None
        if sig is not None:
            self.sem(sig)
            inc = 16 if dma else 1
            self.cnt[sig] += inc
            ev = (sig, self.cnt[sig])
            self.prog[eng].append(("i", fn, sig, inc))
            if not dma:
                self.lastop[eng] = ev
        else:
            self.prog[eng].append(("n", fn))
        return ev

    def now(self, sem):
        if sem in self.qpool:
            return [(nm, self.cnt.get(nm, 0)) for nm in self.qpool[sem]]
        return (sem, self.cnt.get(sem, 0))

    def barrier(self):
        evs = [self.now(x) for x in (ET, EV, EA, EP)]
        for q in (QS, QG, QA):
            evs.extend(self.now(q))
        return evs

    def replay(self, eng_name, eng):
        for item in self.prog[eng_name]:
            if item[0] == "w":
                eng.wait_ge(self.sems[item[1]], item[2])
            elif item[0] == "i":
                item[1](eng).then_inc(self.sems[item[2]], item[3])
            else:
                item[1](eng)


def hr(h):
    return 64 * (h % 2)


class MK:
    def __init__(self, debug=()):
        self.nc = bass.Bass("TRN2", target_bir_lowering=False)
        self.s = Sched()
        for q in (QS, QG, QA, ET, EV, EA, EP):
            self.s.sem(q)
        self.debug = list(debug)
        self._declare_dram()
        self._declare_sbuf()

    def _declare_dram(self):
        nc = self.nc
        def dinp(name, shape, dtype=F32):
            return nc.dram_tensor(name, list(shape), dtype, kind="ExternalInput").ap()
        self.x_in = dinp("x", (T, D))
        self.wT = {k: dinp("w%sT" % k, (D, D)) for k in "qkvo"}
        self.wgbT = dinp("wgbT", (D, 32))
        self.convw_in = dinp("convw", (128, 96))
        self.convb_in = dinp("convb", (128, 24))
        self.g1p = dinp("g1p", (2, 4, 1024))
        self.g2p = dinp("g2p", (2, 4, 256))
        self.fbias = dinp("fbias", (1, D))
        self.i128 = dinp("ident128", (128, 128), BF16)
        self.i64d = dinp("ident64d", (128, 64), BF16)
        self.i96f = dinp("ident96f", (96, 96))
        self.i128f = dinp("ident128f", (128, 128))
        self.maskLd_in = dinp("maskLd", (128, 64))
        self.selb3_in = dinp("selb3", (16, 1024), BF16)
        self.maskUd_in = dinp("maskUd", (128, 64))
        self.out_x = nc.dram_tensor("out", [T, D], F32, kind="ExternalOutput").ap()
        self.scr_w = nc.dram_tensor("scr_w", [2, 256, 1024], F32, kind="Internal").ap()
        self.scr_mp = nc.dram_tensor("scr_mp", [2, NCH, 8, 64, 64], BF16, kind="Internal").ap()
        self.scr_t16 = nc.dram_tensor("scr_t16", [NSLAB, 128, 4, 256], F32, kind="Internal").ap()
        self.scr_t16t = nc.dram_tensor("scr_t16t", [NSLAB, 128, 4, 256], F32, kind="Internal").ap()
        self.scr_flat = nc.dram_tensor("scr_flat", [2, H, T], F32, kind="Internal").ap()
        self.scr_x2t = nc.dram_tensor("scr_x2t", [8, 128, T], BF16, kind="Internal").ap()
        self.scr_tc = nc.dram_tensor("scr_tc", [NCH, 64, 96], F32, kind="Internal").ap()
        self.dbg_out = {}
        for key, _, shape, dtype in self.debug:
            self.dbg_out[key] = nc.dram_tensor(
                "dbg_" + key, list(shape), dtype, kind="ExternalOutput").ap()

    def _alias(self, name, shape, dtype, backing, byte_off=0):
        nc = self.nc
        base = nc.lookup_mloc(backing).addr
        return nc.alloc_sbuf_tensor_at(name, list(shape), dtype,
                                       offset=base + byte_off)

    def _declare_sbuf(self):
        nc = self.nc
        def sbuf(name, shape, dtype=F32):
            return nc.alloc_sbuf_tensor(name, list(shape), dtype)
        U8 = mybir.dt.uint8
        self.ovl1 = sbuf("ovl1", (128, 32768), U8)
        self.ovl2 = sbuf("ovl2", (128, 16384), U8)
        self.ovl3 = sbuf("ovl3", (128, 8192), U8)
        self.ovl4 = sbuf("ovl4", (128, 8192), U8)
        self.Rt = [sbuf("R%d" % i, (128, D)) for i in range(NTT)]
        self.big4 = [self._alias("big4_%d" % i, (128, 2048), BF16, self.ovl2, i * 4096)
                     for i in range(4)]
        self.wring = [sbuf("wring%d" % i, (128, 2048), BF16) for i in range(2)]
        self.x2s = sbuf("x2s", (128, 8 * 512), BF16)
        self.xln = [self._alias("xln%d" % i, (128, D), BF16, self.ovl4, i * 2048)
                    for i in range(2)]
        self.xTt = [self._alias("xTt%d" % i, (128, 128 * 8), BF16, self.ovl4,
                                4096 + i * 2048) for i in range(2)]
        self.ln_stats = [sbuf("lnst%d" % i, (128, 12)) for i in range(2)]
        self.ln_cols = [sbuf("lncol%d" % i, (128, 4)) for i in range(2)]
        self.c_i128 = sbuf("ci128", (128, 128), BF16)
        self.c_i64d = sbuf("ci64d", (128, 64), BF16)
        self.c_i96f = sbuf("ci96f", (96, 96))
        self.c_maskLd = sbuf("cmaskLd", (128, 64))
        self.c_maskUd = sbuf("cmaskUd", (128, 64))
        self.c_ones = sbuf("cones", (1, 128), BF16)
        self.c_fbias = sbuf("cfbias", (1, D), BF16)
        self.c_gbw = sbuf("cgbw", (128, 8 * 32), BF16)
        self.c_convw = sbuf("cconvw", (128, 96))
        self.c_convb = sbuf("cconvb", (128, 24))
        self.gc = self._alias("gc", (16, T), F32, self.ovl1, 0)
        self.lnbc = self._alias("lnbc", (16, T), F32, self.ovl1, 8192)
        self.lgc = self._alias("lgc", (16, T), F32, self.ovl1, 16384)
        self.zstack = self._alias("zstack", (96, T), F32, self.ovl1, 24576)
        self.ztmp = [self._alias("ztmp%d" % i, (16, 512), F32, self.ovl3,
                                 4096 + i * 2048) for i in range(2)]
        self.tcst = sbuf("tcst", (64, 96))
        self.tcols2 = [sbuf("tcols2_%d" % i, (128, 96)) for i in range(2)]
        self.wb = self._alias("wb", (128, 1024), F32, self.ovl3, 0)
        self.rawt = sbuf("rawt", (128, 3 + TSLAB))
        self.cacc = sbuf("cacc", (128, TSLAB))
        self.tails = sbuf("tails", (128, 8 * 3 * 3))
        self.g1s = [self._alias("g1s0", (4, 1024), BF16, self.cacc, 0),
                    self._alias("g1s1", (4, 1024), BF16, self.rawt, 0)]
        self.g2s = [self._alias("g2s%d" % i, (4, 256), BF16, self.x2s, i * 512)
                    for i in range(2)]
        self.qcs = self._alias("qcs", (128, 8 * TSLAB), BF16, self.ovl1, 0)
        self.kcs = self._alias("kcs", (128, 8 * TSLAB), BF16, self.ovl1, 8192)
        self.vcs = self._alias("vcs", (128, 8 * TSLAB), BF16, self.ovl1, 16384)
        self.mps2 = [sbuf("mps2_%d" % i, (128, 512), BF16) for i in range(2)]
        self.m32s = sbuf("m32s", (128, 2048), BF16)
        self.ams = [sbuf("ams%d" % i, (128, 512), BF16) for i in range(SLAB)]
        self.khat = [sbuf("khat%d" % i, (128, 512), BF16) for i in range(SLAB)]
        self.cc = [sbuf("cc%d" % i, (128, 80)) for i in range(SLAB)]
        self.lgrow = sbuf("lgrow", (128, 512))
        self.c_selb = sbuf("cselb", (16, 1024), BF16)
        self.lgT2 = sbuf("lgT2", (16, 64))
        self.lgs = sbuf("lgs", (16, 192), BF16)
        self.dgr = self._alias("dgr", (128, 512), BF16, self.ovl2, 2048)
        self.vtm = [sbuf("vtm%d" % i, (128, 512), BF16) for i in range(SLAB)]
        self.ksq = self._alias("ksq", (128, 512), F32, self.ovl4, 6144)
        self.msk = self._alias("msk", (128, 512), F32, self.ovl3, 4096)
        self.sstb = self._alias("sstb", (128, 4 * 256), BF16, self.ovl3, 4096)
        self.ktmb = self._alias("ktmb", (128, 512), BF16, self.ovl2, 4096)
        self.subtmp = self._alias("subtmp", (128, 4 * 16 * 15), BF16, self.ovl3, 6144)
        self.msk2 = self._alias("msk2", (128, 512), F32, self.ovl3, 6144)
        self.rbt = self._alias("rbt", (128, 512), F32, self.ovl4, 0)
        self.cst = sbuf("cst", (128, 3))
        self.c_i128f = sbuf("ci128f", (128, 128))
        self.oc = [sbuf("oc%d" % i, (128, 512), BF16) for i in range(2)]
        self.oT = self._alias("oT", (128, 8 * 128), BF16, self.ovl4, 2048)
        self.Sm = sbuf("Sm", (128, 512))
        self.Sb = sbuf("Sb", (128, 512), BF16)
        self.ubf = sbuf("ubf", (128, 512), BF16)
        self.rbf = sbuf("rbf", (128, 512), BF16)
        self.sst = self._alias("sst", (128, 4 * 256), F32, self.ovl1, 24576)
        self.tstr = self._alias("tstr", (128, 4 * 256), F32, self.ovl1, 28672)
        self.tstrT = self._alias("tstrT", (128, 4 * 256), F32, self.ovl3, 0)
        self.stripA = self._alias("stripA", (128, 2048), BF16, self.ovl2, 0)
        self.stripB = self._alias("stripB", (128, 2048), BF16, self.ovl2, 4096)
        self.t32t = self._alias("t32t", (128, 2048), BF16, self.ovl2, 8192)
        self.zstr = self._alias("zstr", (128, 2048), BF16, self.ovl2, 12288)
        self.pxs = sbuf("pxs", (128, 4096), BF16)
        self.pp = [nc.alloc_psum_tensor("pp%d" % i, [128, 512], F32) for i in range(6)]

    def bwt(self, j):
        return self.big4[j // 2][:, (j % 2) * 1024:(j % 2) * 1024 + 1024]

    def mm(self, out, lhsT, rhs, deps=(), start=True, stop=True, tp=None):
        def f(e, out=out, lhsT=lhsT, rhs=rhs, start=start, stop=stop, tp=tp):
            return e.matmul(out, lhsT, rhs, start=start, stop=stop, tile_position=tp)
        return self.s.emit("tensor", f, deps=deps, sig=ET)

    def dve(self, fn, deps=()):
        return self.s.emit("vector", fn, deps=deps, sig=EV)

    def act(self, fn, deps=()):
        return self.s.emit("scalar", fn, deps=deps, sig=EA)

    def pool(self, fn, deps=()):
        return self.s.emit("gpsimd", fn, deps=deps, sig=EP)

    def dma(self, q, out, in_, deps=()):
        s = self.s
        eng = {"qs": "sync", "qg": "gpsimd", "qa": "scalar"}[q]
        idx = s.quse[q] % Sched.NPOOL
        s.quse[q] += 1
        nm = s.qpool[q][idx]
        prev = s.cnt.get(nm, 0)
        if prev:
            s.wait(eng, (nm, prev))

        def f(e, out=out, in_=in_):
            return e.dma_start(out=out, in_=in_)
        return s.emit(eng, f, deps=deps, sig=nm, dma=True)

    def build(self, upto=9):
        self.p0_loads()
        if MK.SUB >= 10 and MK.SUB < 13:
            self.ev_bigw_done = None
        if upto >= 2:
            self.p_ln_ffn(1)
        if upto >= 3:
            self.p3_ln2()
        if upto >= 4:
            self.p4_gbeta()
        if upto >= 5:
            self.p5_delta(upto)
        if upto >= 9:
            self.p_ln_ffn(2)
        self.p9_out()
        self.p_debug()
        self.finalize()
        return self.nc

    def p0_loads(self):
        xr = self.x_in.rearrange("(n p) m -> n p m", p=128)
        for dst, src in ((self.c_i128, self.i128), (self.c_i64d, self.i64d),
                         (self.c_i96f, self.i96f), (self.c_i128f, self.i128f),
                         (self.c_maskLd, self.maskLd_in),
                         (self.c_selb, self.selb3_in),
                         (self.c_maskUd, self.maskUd_in),
                         (self.c_convw, self.convw_in), (self.c_convb, self.convb_in)):
            self.dma(QS, dst[:], src)
        self.ev_cs = self.s.now(QS)
        self.dma(QG, self.c_fbias[:], self.fbias)
        self.ev_R = None
        self.dma(QG, self.c_gbw[:].rearrange("p (i j) -> p i j", i=8),
                 self.wgbT.rearrange("(i p) j -> p i j", p=128))
        for t in range(2):
            self.dma(QG, self.g1s[t][:], self.g1p[t])
            self.dma(QG, self.g2s[t][:], self.g2p[t])
        self.ev_cg = self.s.now(QG)
        self.ev_R = [self.dma((QG, QA)[i % 2], self.Rt[i][:], xr[i])
                     for i in range(NTT)]
        self.dve(lambda e: e.memset(self.c_ones[:], 1.0))
        self.dve(lambda e: e.memset(self.cst[:, 0:1], EPS_LN))
        self.dve(lambda e: e.memset(self.cst[:, 1:2], 1.0))
        self.dve(lambda e: e.memset(self.cst[:, 2:3], LN_KD))
        self.dve(lambda e: e.memset(self.zstack[:], 0.0))
        self.dve(lambda e: e.memset(self.Sm[:], 0.0))
        self.dve(lambda e: e.memset(self.tails[:], 0.0))
        self.ev_zero = self.dve(lambda e: e.memset(self.Sb[:], 0.0))
        pass

    def p1_ffnw(self, extra=(), reload=False):
        s = self.s
        ev_wb = None
        if reload:
            for t in range(2):
                self.dma(QG, self.g1s[t][:], self.g1p[t], deps=extra)
                self.dma(QG, self.g2s[t][:], self.g2p[t], deps=extra)
            self.ev_cg = s.now(QG)
        ev_wb_t = [(), ()]
        for t in range(2):
            evd = []
            for mt in range(2):
                evs = []
                for nk in range(2):
                    ev_mm = self.mm(self.pp[nk][:, 0:512],
                                    self.g2s[t][:, mt * 128:(mt + 1) * 128],
                                    self.g1s[t][:, nk * 512:(nk + 1) * 512],
                                    deps=(self.ev_cg, ev_wb) + tuple(extra))
                    evs.append(self.dve(
                        lambda e, nk=nk: e.tensor_copy(
                            self.wb[:, nk * 512:(nk + 1) * 512], self.pp[nk][:, 0:512]),
                        deps=(ev_mm,)))
                ev_wb = self.dma((QS, QG)[mt],
                                 self.scr_w[t, mt * 128:(mt + 1) * 128, :],
                                 self.wb[:], deps=evs)
                evd.append(ev_wb)
            ev_wb_t[t] = tuple(evd)
        self.ev_bigw = []
        self.ev_piece = [None] * 8
        ev_cp = None
        qrr = (QS, QG, QA)
        for dt in range(4):
            for t in range(2):
                ev_in = []
                for d in range(8):
                    perm = AP(self.scr_w.tensor,
                              self.scr_w.offset + t * 262144 + (dt * 8 + d) * 32,
                              [[1024, 16], [16384, 16], [1, 32]])
                    ev_in.append(self.dma(
                        qrr[d % 3], self.wb[d * 16:(d + 1) * 16, 0:512], perm,
                        deps=ev_wb_t[0] + ev_wb_t[1]
                        + ((ev_cp,) if ev_cp else ())))
                ev_in = tuple(ev_in)
                src = self.wb[:, 0:512].rearrange("p (e b) -> p b e", e=16)
                if t == 0:
                    dsts = [(self.bwt(dt)[:, 0:512], 1.0),
                            (self.bwt(4 + dt)[:, 512:1024], 1.0)]
                else:
                    dsts = [(self.bwt(dt)[:, 512:1024], 1.0),
                            (self.bwt(4 + dt)[:, 0:512], -1.0)]
                for dst, sc in dsts:
                    ov = dst.rearrange("p (b e) -> p b e", b=32)
                    if sc == 1.0:
                        ev_cp = self.dve(lambda e, ov=ov, src=src: e.tensor_copy(
                            ov, src), deps=ev_in)
                    else:
                        ev_cp = self.dve(
                            lambda e, ov=ov, src=src: e.tensor_scalar_mul(ov, src, -1.0),
                            deps=ev_in)
                self.ev_bigw.append(ev_cp)
            self.ev_piece[dt] = ev_cp
            self.ev_piece[4 + dt] = ev_cp
        self.ev_bigw_done = ev_cp

    def _emit_ln(self, i, par, extra):
        st, co = self.ln_stats[par], self.ln_cols[par]
        ev1 = self.dve(lambda e, i=i: e.bn_stats(st[:, 0:6], self.Rt[i][:, 0:512]),
                       deps=extra)
        ev2 = self.dve(lambda e, i=i: e.bn_stats(st[:, 6:12], self.Rt[i][:, 512:1024]))
        ev3 = self.dve(lambda e: e.bn_aggr(co[:, 0:2], st[:, 0:12]), deps=(ev1, ev2))
        if not hasattr(self, "ln_rd"):
            self.ln_rd = [(), ()]
        ev4 = self.act(lambda e: e.activation(co[:, 2:3], co[:, 1:2], AF.Ln,
                                              bias=self.cst[:, 0:1]),
                       deps=(ev3,) + tuple(self.ln_rd[par]))
        ev5 = self.act(lambda e: e.activation(co[:, 2:3], co[:, 2:3], AF.Exp,
                                              scale=-0.5), deps=(ev4,))
        ev7 = self.dve(lambda e: e.scalar_tensor_tensor(
            co[:, 3:4], co[:, 0:1], -1.0, co[:, 2:3], ALU.mult, ALU.mult),
            deps=(ev5,))
        self.ln_rd[par] = (ev7,)
        return self.act(lambda e, i=i, par=par: e.activation(
            self.xln[par][:], self.Rt[i][:], AF.Identity, bias=co[:, 3:4],
            scale=co[:, 2:3]), deps=(ev7, ev5))

    SUB = 3
    NTILES = NTT

    def p_ln_ffn(self, which):
        s = self.s
        evs = [None]
        if which == 1:
            pre = [self.ev_cs]
            self.ev_ffn = [None] * NTT
        else:
            pre = s.barrier()
            self.ev_ffn2 = [None] * NTT
        sub = MK.SUB if MK.SUB < 10 else MK.SUB - 10
        ev_ffn_prev = [None, None]
        ev_tp_prev = [None] * 8
        nt = min(NTT, MK.NTILES)

        def emit_tile_ln(i):
            par = i % 2
            extra = list(pre) + ([self.ev_R[i]] if which == 1 else [])
            if MK.SUB >= 10:
                return self.act(lambda e, i=i, par=par: e.activation(
                    self.xln[par][:], self.Rt[i][:], AF.Copy), deps=extra)
            return self._emit_ln(i, par, extra)

        ap_pend = {0: emit_tile_ln(0)} if nt else {}
        if nt > 1:
            ap_pend[1] = emit_tile_ln(1)
        if MK.SUB < 10 or MK.SUB >= 13:
            if which == 1:
                self.p1_ffnw()
            else:
                self.p1_ffnw(extra=tuple(s.barrier()), reload=True)
        for i in range(nt):
            par = i % 2
            ev_ap = ap_pend.pop(i)
            if i + 1 < nt and i + 1 not in ap_pend:
                ap_pend[i + 1] = emit_tile_ln(i + 1)
            if sub < 2:
                continue
            ev_tp = []
            slot_ev = [ev_tp_prev[6], ev_tp_prev[7]]
            for j in range(8):
                pslc = self.pp[2 + (j % 2)][:, 0:128]
                ev_mm = self.mm(pslc, self.xln[par][:, j * 128:(j + 1) * 128],
                                self.c_i128[:],
                                deps=(ev_ap, self.ev_cs, slot_ev[j % 2]))
                ev_e = self.act(
                    lambda e, j=j, par=par, pslc=pslc: e.activation(
                        self.xTt[par][:, j * 128:(j + 1) * 128], pslc, AF.Copy),
                    deps=(ev_mm,))
                ev_tp.append(ev_e)
                slot_ev[j % 2] = ev_e
            ev_tp_prev = ev_tp
            if sub < 3:
                continue
            evs = []
            for half in range(2):
                pb = self.pp[half]
                dep0 = [ev_tp[j] for j in range(8)] + [ev_ffn_prev[half]]
                ev_last = None
                for j in range(8):
                    ev_last = self.mm(
                        pb[:, 0:512], self.xTt[par][:, j * 128:(j + 1) * 128],
                        self.bwt(j)[:, half * 512:(half + 1) * 512],
                        deps=(tuple(dep0) + (self.ev_piece[j],)) if j == 0
                        else (self.ev_piece[j],), start=(j == 0), stop=False)
                ev_last = self.mm(
                    pb[:, 0:512], self.c_ones[0:1, :],
                    self.c_fbias[0:1, half * 512:(half + 1) * 512],
                    deps=(self.ev_cg,), start=False, stop=True)
                evs.append(self.dve(
                    lambda e, pb=pb, i=i, half=half: e.scalar_tensor_tensor(
                        self.Rt[i][:, half * 512:(half + 1) * 512], pb[:, 0:512], 0.5,
                        self.Rt[i][:, half * 512:(half + 1) * 512], ALU.mult, ALU.add),
                    deps=(ev_last,)))
            ev_ffn_prev = evs
            if which == 1:
                self.ev_ffn[i] = evs[-1]
            else:
                self.ev_ffn2[i] = tuple(evs)
        self.ev_ffn_last = evs

    def p3_ln2(self):
        s = self.s
        ev_t = [None, None]
        ev_dump = [None, None]
        self.ev_x2t = []
        ap_pend = {0: self._emit_ln(0, 0, (self.ev_ffn[0],))}
        for i in range(NTT):
            par = i % 2
            ev_ap = ap_pend.pop(i)
            if i + 1 < NTT:
                ap_pend[i + 1] = self._emit_ln(i + 1, (i + 1) % 2,
                                               (self.ev_ffn[i + 1],))
            for j in range(8):
                pslc = self.pp[4 + (j % 2)][:, 0:128]
                ev_mm = self.mm(pslc, self.xln[par][:, j * 128:(j + 1) * 128],
                                self.c_i128[:], deps=(ev_ap, ev_t[j % 2]))
                ev_t[j % 2] = self.act(
                    lambda e, j=j, par=par, pslc=pslc: e.activation(
                        self.xTt[par][:, j * 128:(j + 1) * 128], pslc, AF.Copy),
                    deps=(ev_mm, ev_dump[par]))
            ev_dump[par] = self.dma(
                QS, self.scr_x2t[:, :, i * 128:(i + 1) * 128].rearrange("a p c -> p a c"),
                self.xTt[par][:].rearrange("p (a c) -> p a c", a=8),
                deps=(ev_t[0], ev_t[1]))
            self.ev_x2t.append(ev_dump[par])
        self.ev_x2t_all = tuple(self.ev_x2t)
        self.ev_x2t_done = self.ev_x2t[-1]

    P4SUB = 4
    P4OPS = 4
    P5SUB = 4
    CHSER = 0

    def p4_gbeta(self):
        s = self.s
        evz = []
        for sl4 in range(4):
            ev_ld = self.dma(QS, self.x2s[:],
                             self.scr_x2t[:, :, sl4 * 512:(sl4 + 1) * 512]
                             .rearrange("a p c -> p a c"),
                             deps=tuple(self.ev_x2t[sl4 * 4:sl4 * 4 + 4]) + (s.now(ET),))
            for gb in range(2):
                pb = self.pp[gb]
                ev_last = None
                for j in range(8):
                    ev_last = self.mm(
                        pb[0:16, 0:512],
                        self.c_gbw[:, j * 32 + gb * 16: j * 32 + gb * 16 + 16],
                        self.x2s[:, j * 512:(j + 1) * 512],
                        deps=(ev_ld, self.ev_cg, s.now(EV), s.now(EA)) if j == 0 else (),
                        start=(j == 0), stop=(j == 7))
                plane = self.gc if gb == 0 else self.lnbc
                za, zb2 = self.ztmp
                if MK.P4SUB < 1:
                    continue
                e1 = self.act(lambda e, pb=pb: e.activation(za[:], pb[0:16, 0:512],
                                                            AF.Abs), deps=(ev_last,))
                e1 = self.act(lambda e: e.activation(za[:], za[:], AF.Exp, scale=-1.0),
                              deps=(e1,))
                e1 = self.act(lambda e: e.activation(za[:], za[:], AF.Ln,
                                                     bias=self.cst[0:16, 1:2]),
                              deps=(e1,))
                e3 = e1
                if MK.P4OPS >= 4:
                    e4 = self.dve(lambda e, pb=pb: e.tensor_scalar_min(
                        zb2[:], pb[0:16, 0:512], 0.0), deps=(ev_last, e1))
                else:
                    e4 = self.dve(lambda e, pb=pb: e.tensor_copy(zb2[:], za[:]),
                                  deps=(e1,))
                evz.append(self.dve(lambda e, plane=plane, sl4=sl4: e.tensor_sub(
                    plane[:, sl4 * 512:(sl4 + 1) * 512], zb2[:], za[:]),
                    deps=(e3, e4, s.now(EV))))
        if MK.P4SUB < 2:
            self.ev_tcols = None
            self.ev_lgflat = None
            return
        ev_cs = []
        for ch in range(NCH):
            ev_cs.append(self.dve(lambda e, ch=ch: e.tensor_tensor_scan(
                self.lgc[:, ch * 64:(ch + 1) * 64], self.gc[:, ch * 64:(ch + 1) * 64],
                self.gc[:, ch * 64:(ch + 1) * 64], 0.0, ALU.add, ALU.bypass),
                deps=evz if ch == 0 else ()))
        ez1 = self.dve(lambda e: e.tensor_copy(self.zstack[0:16, :], self.gc[:]),
                       deps=tuple(ev_cs))
        ez2 = self.dve(lambda e: e.tensor_copy(self.zstack[32:48, :], self.lnbc[:]),
                       deps=(ez1,))
        ez3 = self.dve(lambda e: e.tensor_copy(self.zstack[64:80, :], self.lgc[:]),
                       deps=(ez2,))
        if MK.P4SUB < 3:
            self.ev_tcols = None
            self.ev_lgflat = None
            return
        evt = None
        ev_tc = None
        for ch in range(NCH):
            pv = self.pp[ch % 2][0:64, 0:96]
            ev_mm = self.mm(pv, self.zstack[:, ch * 64:(ch + 1) * 64], self.c_i96f[:],
                            deps=(ez1, ez2, ez3, evt, self.ev_cs, ev_tc))
            evt = self.dve(lambda e, pv=pv: e.tensor_copy(self.tcst[:], pv),
                           deps=(ev_mm, ev_tc))
            ev_tc = self.dma(QS, self.scr_tc[ch], self.tcst[:], deps=(evt,))
        self.ev_tcols = ev_tc
        if MK.P4SUB < 4:
            self.ev_lgflat = None
            return
        self.ev_lgflat = self.dma(QS, self.scr_flat[0], self.lgc[:], deps=(ev_cs[-1],))

    def p5_delta(self, upto):
        s = self.s
        self.ev_S = [self.ev_zero] * H
        self.ev_wo_R = None
        self.ev_oc_free = None
        nslab = NSLAB if upto >= 6 else 1
        self.nslab_built = nslab
        for sl in range(nslab):
            self._slab_qkv(sl)
            if MK.P5SUB >= 2:
                self._slab_prep(sl)
            if MK.P5SUB >= 3:
                self._slab_solve(sl)
            if MK.P5SUB >= 4:
                self._slab_chain(sl)
    def _slab_qkv(self, sl):
        s = self.s
        ev_x2 = self.dma(QS, self.x2s[:],
                         self.scr_x2t[:, :, sl * 512:(sl + 1) * 512]
                         .rearrange("a p c -> p a c"),
                         deps=tuple(self.ev_x2t[sl * 4:sl * 4 + 4]) + (s.now(ET),))
        strips = {"q": self.qcs, "k": self.kcs, "v": self.vcs}
        wbase = {"q": 0, "k": 4, "v": 8}
        self.ev_strip = {}
        ev_conv_last = None
        for wi, w in enumerate("qkv"):
            wsrc = self.wT[w].rearrange("(j p) c -> p j c", p=128)
            ev_w = {}
            for ct in range(8):
                qt = ct // 2
                slot = self.wring[qt % 2]
                if ct % 2 == 0:
                    ev_w[qt] = self.dma(
                        QG, slot[:].rearrange("p (j cc) -> p j cc", j=8),
                        wsrc[:, :, qt * 256:(qt + 1) * 256],
                        deps=(s.now(ET),))
                pb = self.pp[ct % 2]
                dep0 = (ev_x2, ev_w[qt], s.now(EV), s.now(EP), s.now(EA))
                ev_last = None
                for j in range(8):
                    c0 = j * 256 + (ct % 2) * 128
                    ev_last = self.mm(
                        pb[:, 0:512], slot[:, c0:c0 + 128],
                        self.x2s[:, j * 512:(j + 1) * 512],
                        deps=dep0 if j == 0 else (), start=(j == 0), stop=(j == 7))
                tl = self.tails[:, (wi * 8 + ct) * 3:(wi * 8 + ct) * 3 + 3]
                dpre = [ev_conv_last]
                if wi == 0 and ct == 0:
                    dpre += [s.now(ET), self.ev_lgflat, self.ev_tcols,
                             s.now(EV), s.now(EA)]
                e_pref = self.dve(lambda e, tl=tl: e.tensor_copy(self.rawt[:, 0:3], tl),
                                  deps=dpre)
                e_raw = self.dve(lambda e, pb=pb: e.tensor_copy(
                    self.rawt[:, 3:3 + 512], pb[:, 0:512]), deps=(ev_last,))
                self.dve(lambda e, tl=tl: e.tensor_copy(tl, self.rawt[:, 512:515]),
                         deps=(e_raw, e_pref))

                def wc(j_, ct=ct, w=w):
                    cix = ct * 12 + wbase[w] + j_
                    return self.c_convw[:, cix:cix + 1]
                bc = self.c_convb[:, ct * 3 + wi:ct * 3 + wi + 1]
                e_c0 = self.act(lambda e, wc=wc, bc=bc: e.activation(
                    self.cacc[:], self.rawt[:, 3:515], AF.Identity, bias=bc, scale=wc(3)),
                    deps=(e_pref, e_raw, self.ev_cs))
                e_c1 = self.dve(lambda e, wc=wc: e.scalar_tensor_tensor(
                    self.cacc[:], self.rawt[:, 2:514], wc(2), self.cacc[:],
                    ALU.mult, ALU.add), deps=(e_c0,))
                e_c2 = self.dve(lambda e, wc=wc: e.scalar_tensor_tensor(
                    self.cacc[:], self.rawt[:, 1:513], wc(1), self.cacc[:],
                    ALU.mult, ALU.add), deps=(e_c1,))
                e_c3 = self.dve(lambda e, wc=wc: e.scalar_tensor_tensor(
                    self.cacc[:], self.rawt[:, 0:512], wc(0), self.cacc[:],
                    ALU.mult, ALU.add), deps=(e_c2,))
                ev_sp = self.dve(lambda e, w=w, ct=ct, strips=strips: e.tensor_copy(
                    strips[w][:, ct * 512:(ct + 1) * 512], self.cacc[:]), deps=(e_c3,))
                self.ev_strip[(w, ct)] = ev_sp
                ev_conv_last = ev_sp
        self.ev_strip_done = ev_conv_last

    def _kc(self, h, chl, strip):
        return strip[hr(h):hr(h) + 64,
                     (h // 2) * 512 + chl * 64:(h // 2) * 512 + chl * 64 + 64]

    def ccv(self, chl, slot):
        return self.cc[chl][:].rearrange("p (g s) -> p g s", s=10)[:, :, slot]

    def ccp(self, chl, slot, par):
        return self.cc[chl][par * 64:par * 64 + 64].rearrange(
            "p (g s) -> p g s", s=10)[:, :, slot]

    def ccol(self, chl, slot, h):
        pa = h // 2
        r0 = hr(h)
        return self.cc[chl][r0:r0 + 64, pa * 10 + slot:pa * 10 + slot + 1]

    def _slab_prep(self, sl):
        s = self.s
        self.ev_prep = []
        if not hasattr(self, "ev_mp_out"):
            self.ev_mp_out = [(), ()]
            self.ev_lgrd = ()
            self.ev_tc2rd = [(), ()]
            self.ev_ppK_rd = None
            self.ev_ppV_rd = None
            self.ev_kc_prev = None
            self.ev_bc_prev = None
            self.ev_lgt_rd = ()
            self.ev_ktm_rd = ()
            self.ev_ksq_rd = None
        bd = getattr(self, "bankdep", {})
        ppK, ppV = self.pp[2], self.pp[3]
        for chl in range(SLAB):
            ch = sl * SLAB + chl
            ev_tc2a = self.dma(QA, self.tcols2[chl % 2][0:64, :], self.scr_tc[ch],
                               deps=(self.ev_tcols,) + tuple(self.ev_tc2rd[chl % 2]))
            ev_tc2 = (ev_tc2a, self.dma(
                QA, self.tcols2[chl % 2][64:128, :].rearrange(
                    "p (b c) -> p b c", c=32)[:, :, 0:8],
                AP(self.scr_tc.tensor, self.scr_tc.offset + ch * 64 * 96 + 8,
                   [[96, 64], [32, 3], [1, 8]])))
            ev_lgt = self.dma(QA, self.lgT2[:],
                              self.scr_flat[0][:, ch * 64:ch * 64 + 64],
                              deps=(self.ev_lgflat,) + tuple(self.ev_lgt_rd))
            e_h = self.dve(lambda e: e.tensor_copy(self.lgs[:, 0:64], self.lgT2[:]),
                           deps=(ev_lgt, self.ev_bc_prev))
            e_h = self.dve(lambda e: e.tensor_sub(
                self.lgT2[:], self.lgT2[:], self.lgs[:, 0:64]), deps=(e_h,))
            e_h = self.dve(lambda e: e.tensor_copy(self.lgs[:, 64:128], self.lgT2[:]),
                           deps=(e_h,))
            e_h = self.dve(lambda e: e.tensor_sub(
                self.lgT2[:], self.lgT2[:], self.lgs[:, 64:128]), deps=(e_h,))
            e_h = self.dve(lambda e: e.tensor_copy(self.lgs[:, 128:192], self.lgT2[:]),
                           deps=(e_h,))
            self.ev_lgt_rd = (e_h,)
            ev_bc = None
            for pa in range(8):
                for k3 in range(3):
                    ev_bc = self.mm(
                        ppV[:, pa * 64:pa * 64 + 64],
                        self.c_selb[:, pa * 128:(pa + 1) * 128],
                        self.lgs[:, k3 * 64:k3 * 64 + 64],
                        deps=(e_h, self.ev_ppV_rd, bd.get("pO2"))
                        if pa == 0 and k3 == 0 else (),
                        start=(k3 == 0), stop=(k3 == 2))
            self.ev_bc_prev = ev_bc
            ev_lgrow = self.act(lambda e: e.activation(self.lgrow[:], ppV[:], AF.Copy),
                                deps=(ev_bc,) + tuple(self.ev_lgrd))
            ev_lg = (ev_lgrow,)
            ev_kt = []
            for h in range(H):
                r0 = hr(h)
                dep = (self.ev_strip[("k", h // 2)],)
                if h == 0:
                    dep += (self.ev_ppK_rd, bd.get("pO1"))
                ev_kt.append(self.mm(
                    ppK[r0:r0 + 64, (h // 2) * 64:(h // 2) * 64 + 64],
                    self._kc(h, chl, self.kcs), self.c_i64d[r0:r0 + 64, :],
                    deps=dep, tp=(r0, r0)))
            ev_ktmb = self.dve(
                lambda e: e.tensor_copy(self.ktmb[:], ppK[:]),
                deps=(ev_kt[-1], s.now(ET)) + tuple(self.ev_ktm_rd) if chl == 0
                else (ev_kt[-1], self.ev_kc_prev) + tuple(self.ev_ktm_rd))
            ev_sq = self.act(lambda e: e.activation(self.ksq[:], ppK[:], AF.Square),
                             deps=(ev_ktmb, self.ev_ksq_rd))
            ev_ss = self.dve(lambda e, chl=chl: e.reduce_sum(
                self.ccv(chl, 0), self.ksq[:].rearrange("p (g x) -> p g x", g=8),
                mybir.AxisListType.X), deps=(ev_sq, s.now(EV), s.now(EA)))
            self.ev_ksq_rd = ev_ss
            ev_ln = self.act(lambda e, chl=chl: e.activation(
                self.ccv(chl, 8), self.ccv(chl, 0), AF.Ln), deps=(ev_ss,))
            ev_lnrn = self.dve(lambda e, chl=chl: e.tensor_scalar(
                self.ccv(chl, 0), self.ccv(chl, 8), -0.5, LNRN_MAX,
                ALU.mult, ALU.min), deps=(ev_ln, ev_ss))
            ev_rn = self.act(lambda e, chl=chl: e.activation(
                self.ccv(chl, 9), self.ccv(chl, 0), AF.Exp), deps=(ev_lnrn,))
            ev_eL = self.act(lambda e, chl=chl: e.activation(
                self.ccv(chl, 5), self.lgrow[:].rearrange(
                    "p (g x) -> p g x", g=8)[:, :, 63], AF.Exp),
                deps=ev_lg + (ev_lnrn,))
            tc = lambda c0, chl=chl: self.tcols2[chl % 2][:, c0:c0 + 8]
            cp = lambda sl_, chl=chl: self.ccv(chl, sl_)
            ee1 = self.dve(lambda e, tc=tc, cp=cp: e.scalar_tensor_tensor(
                cp(8), tc(0), -1.0, tc(64), ALU.mult, ALU.add),
                deps=ev_tc2 + (ev_lnrn,))
            ev_m = self.dve(lambda e, tc=tc, cp=cp: e.tensor_add(
                cp(1), cp(8), tc(32)), deps=(ee1,))
            ev_e1 = self.act(lambda e, cp=cp: e.activation(cp(8), cp(1), AF.Exp),
                             deps=(ev_m,))
            ee3 = self.dve(lambda e, cp=cp: e.tensor_scalar_mul(cp(2), cp(8), -1.0),
                           deps=(ev_e1,))
            e_b = self.act(lambda e, tc=tc, cp=cp: e.activation(
                cp(3), tc(32), AF.Exp), deps=(ev_lnrn,))
            e_k = self.act(lambda e, tc=tc, cp=cp: e.activation(
                cp(6), tc(64), AF.Exp, bias=self.cst[:, 2:3]), deps=(ev_lnrn,))
            ev_E = self.dve(lambda e, tc=tc, cp=cp: e.tensor_scalar_mul(
                cp(4), tc(64), -1.0), deps=ev_tc2 + (ev_lnrn,))
            Lv = self.lgrow[:].rearrange("p (g x) -> p g x", g=8)[:, :, 63]
            ee4 = self.dve(lambda e, tc=tc, cp=cp: e.tensor_sub(
                cp(8), cp(0), tc(64)), deps=(ee3,))
            e_t = self.dve(lambda e, cp=cp, Lv=Lv: e.tensor_add(cp(8), cp(8), Lv),
                           deps=(ee4,) + ev_lg)
            ev_kh = self.act(lambda e, cp=cp: e.activation(cp(7), cp(8), AF.Exp),
                             deps=(e_t,))
            ev_ms = (ev_m,)
            ev_Es = (ev_E,)
            ev_khs = (ev_kh,)
            ev_ts = (e_t,)
            rdt = [ee1, ev_m, e_b, e_k, ev_E, ee4]
            ev_khat = self.dve(lambda e, chl=chl: e.tensor_mul(
                self.khat[chl][:].rearrange("p (g x) -> p g x", g=8),
                self.ktmb[:].rearrange("p (g x) -> p g x", g=8),
                self.ccv(chl, 7)[:, :, None].broadcast_to((128, 8, 64))),
                deps=tuple(ev_khs) + (ev_ktmb,))
            self.ev_ktm_rd = (ev_khat,)
            ev_dgr = self.dve(lambda e, chl=chl: e.tensor_mul(
                self.dgr[:].rearrange("p (g x) -> p g x", g=8),
                self.c_i64d[:][:, None, :].broadcast_to((128, 8, 64)),
                self.ccv(chl, 9)[:, :, None].broadcast_to((128, 8, 64))),
                deps=(ev_rn, self.ev_kc_prev if chl else s.now(ET)))
            ev_kc = None
            for h in range(H):
                pa = h // 2
                r0 = hr(h)
                ev_kc = self.mm(
                    ppK[r0:r0 + 64, pa * 64:pa * 64 + 64],
                    self.ktmb[r0:r0 + 64, pa * 64:pa * 64 + 64],
                    self.dgr[r0:r0 + 64, pa * 64:pa * 64 + 64],
                    deps=(ev_dgr, ev_ktmb, ev_sq) if h == 0 else (), tp=(r0, r0))
            self.ev_kc_prev = ev_kc
            ev_kce = self.dve(lambda e, chl=chl: e.tensor_copy(
                self.kcs[:].rearrange("p (g t) -> p g t", g=8)
                [:, :, chl * 64:chl * 64 + 64],
                ppK[:].rearrange("p (g x) -> p g x", g=8)), deps=(ev_kc,))
            ev_vt = []
            for h in range(H):
                r0 = hr(h)
                dep = (self.ev_strip[("v", h // 2)],)
                if h == 0:
                    dep += (ev_lgrow, self.ev_ppV_rd)
                ev_vt.append(self.mm(
                    ppV[r0:r0 + 64, (h // 2) * 64:(h // 2) * 64 + 64],
                    self._kc(h, chl, self.vcs), self.c_i64d[r0:r0 + 64, :],
                    deps=dep, tp=(r0, r0)))
            ev_vtm = self.dve(lambda e, chl=chl: e.tensor_copy(
                self.vtm[chl][:], ppV[:]), deps=(ev_vt[-1],))
            ev_g = ev_a = None
            for h in range(H):
                r0 = hr(h)
                ev_g = self.mm(
                    ppK[r0:r0 + 64, (h // 2) * 64:(h // 2) * 64 + 64],
                    self._kc(h, chl, self.kcs), self._kc(h, chl, self.kcs),
                    deps=(ev_kce,) if h == 0 else (), tp=(r0, r0))
            for h in range(H):
                r0 = hr(h)
                ev_a = self.mm(
                    ppV[r0:r0 + 64, (h // 2) * 64:(h // 2) * 64 + 64],
                    self._kc(h, chl, self.kcs), self._kc(h, chl, self.qcs),
                    deps=(ev_vtm,) if h == 0 else (), tp=(r0, r0))
            mskt = self.msk if chl % 2 == 0 else self.rbt
            mskv = lambda mskt=mskt: mskt[:].rearrange("p (g x) -> p g x", g=8)
            msk2v = lambda: self.msk2[:].rearrange("p (g x) -> p g x", g=8)
            lgv = lambda: self.lgrow[:].rearrange("p (g x) -> p g x", g=8)
            mb = self.ccv(chl, 1)[:, :, None].broadcast_to((128, 8, 64))
            ecb = self.ccv(chl, 4)[:, :, None].broadcast_to((128, 8, 64))
            mLv = self.c_maskLd[:][:, None, :].broadcast_to((128, 8, 64))
            mUv = self.c_maskUd[:][:, None, :].broadcast_to((128, 8, 64))
            e1 = self.dve(lambda e, mskv=mskv, mb=mb: e.scalar_tensor_tensor(
                mskv(), lgv(), -1.0, mb, ALU.mult, ALU.add),
                deps=ev_lg + tuple(ev_ms) + (s.now(EV),))
            e3 = self.dve(lambda e, mskv=mskv, mLv=mLv: e.tensor_add(
                mskv(), mskv(), mLv), deps=(e1,))
            e4 = self.act(lambda e, mskt=mskt: e.activation(mskt[:], mskt[:], AF.Exp),
                          deps=(e3,))
            e5 = self.dve(lambda e, chl=chl, mskt=mskt, ppK=ppK: e.tensor_mul(
                self.mps2[chl % 2][:], mskt[:], ppK[:]),
                deps=(e4, ev_g) + tuple(self.ev_mp_out[chl % 2]))
            self.ev_ppK_rd = e5
            f1 = self.dve(lambda e, msk2v=msk2v, ecb=ecb: e.tensor_add(
                msk2v(), lgv(), ecb), deps=ev_lg + tuple(ev_Es) + (s.now(EV),))
            f2 = self.dve(lambda e, msk2v=msk2v, mUv=mUv: e.tensor_add(
                msk2v(), msk2v(), mUv), deps=(f1,))
            f3 = self.act(lambda e: e.activation(self.msk2[:], self.msk2[:], AF.Exp),
                          deps=(f2,))
            f4 = self.dve(lambda e, chl=chl, ppV=ppV: e.tensor_mul(
                self.ams[chl][:], self.msk2[:], ppV[:]), deps=(f3, ev_a))
            self.ev_ppV_rd = f4
            ev_mpa = self.dma(QA, self.scr_mp[0, ch].rearrange("pa s r -> s pa r"),
                              self.mps2[chl % 2][0:64, :].rearrange(
                                  "s (pa r) -> s pa r", pa=8), deps=(e5,))
            ev_mp = (ev_mpa,
                     self.dma(QG, self.scr_mp[1, ch].rearrange("pa s r -> s pa r"),
                              self.mps2[chl % 2][64:128, :].rearrange(
                                  "s (pa r) -> s pa r", pa=8), deps=(e5,)))
            self.ev_mp_out[chl % 2] = ev_mp
            self.ev_lgrd = (e1, f1, ev_eL) + tuple(ev_ts)
            self.ev_tc2rd[chl % 2] = tuple(rdt)
            self.ev_prep.append((ev_mp, f4, ev_khat, ev_vtm))

    def _slab_solve(self, sl):
        s = self.s
        deps = sum((tuple(p[0]) for p in self.ev_prep), ()) + (
            s.now(ET), s.now(EV), s.now(EP))
        SMP = self.scr_mp
        rs = 512
        cs = 128 * rs
        base = sl * SLAB * cs
        CST, PST = NCH * 8 * 64 * 64, 8 * 64 * 64
        b0 = sl * SLAB * PST
        ev_g1 = []
        for par in range(2):
            for jb in range(4):
                ev_g1.append(self.dma(
                    QS, self.sstb[par * 64:par * 64 + 64,
                                  jb * 256:(jb + 1) * 256].rearrange(
                        "p (r c) -> p r c", r=16),
                    AP(SMP.tensor, SMP.offset + par * CST + b0 + jb * 1040,
                       [[4096, 64], [64, 16], [1, 16]]), deps=deps))
        ev_g1 = tuple(ev_g1)
        ev_g2 = []
        for par in range(2):
            for a in range(2):
                ev_g2.append(self.dma(
                    QS, self.stripA[par * 64:par * 64 + 16, :].rearrange(
                        "r (q a2 c) -> r q a2 c", a2=2, c=16)[:, :, a],
                    AP(SMP.tensor, SMP.offset + par * CST + b0
                       + (16 + 32 * a) * 64 + 32 * a,
                       [[64, 16], [4096, 64], [1, 16]]), deps=deps))
        ev_g2 = tuple(ev_g2)
        ev_i0 = self.dve(lambda e: e.memset(self.tstr[:], 0.0),
                         deps=(s.now(ET), s.now(QG)))
        ev_i1 = self.dve(lambda e: e.memset(
            self.tstr[:].rearrange("p (b r c) -> p b r c", b=4, r=16)
            .rearrange("p b r c -> p b r c")[:, :, 0, 0] if False else
            AP(self.tstr.ap().tensor, self.tstr.ap().offset,
               [self.tstr.ap().ap[0], [256, 4], [17, 16]]), 1.0), deps=(ev_i0,))
        ev_i2 = self.dve(lambda e: e.tensor_copy(self.sst[:], self.sstb[:]),
                         deps=ev_g1)
        ev_pr = (ev_i1, ev_i2)
        for r in range(1, 16):
            def fmul(e, r=r):
                tv = AP(self.tstr.ap().tensor, self.tstr.ap().offset,
                        [self.tstr.ap().ap[0], [256, 4], [1, 16], [16, r]])
                mv = AP(self.sst.ap().tensor, self.sst.ap().offset + r * 16,
                        [self.sst.ap().ap[0], [256, 4], [0, 16], [1, r]])
                ov = AP(self.subtmp.ap().tensor, self.subtmp.ap().offset,
                        [self.subtmp.ap().ap[0], [240, 4], [15, 16], [1, r]])
                return e.tensor_mul(ov, tv, mv)
            e1 = self.dve(fmul, deps=ev_pr)
            def fred(e, r=r):
                iv = AP(self.subtmp.ap().tensor, self.subtmp.ap().offset,
                        [self.subtmp.ap().ap[0], [240, 4], [15, 16], [1, r]])
                ov = AP(self.cacc.ap().tensor, self.cacc.ap().offset,
                        [self.cacc.ap().ap[0], [16, 4], [1, 16], [0, 1]])
                return e.reduce_sum(ov, iv, mybir.AxisListType.X)
            e2 = self.dve(fred, deps=(e1,))
            def fsub(e, r=r):
                rv = AP(self.tstr.ap().tensor, self.tstr.ap().offset + r * 16,
                        [self.tstr.ap().ap[0], [256, 4], [1, 16]])
                sv = AP(self.cacc.ap().tensor, self.cacc.ap().offset,
                        [self.cacc.ap().ap[0], [16, 4], [1, 16]])
                return e.tensor_sub(rv, rv, sv)
            ev_pr = (self.dve(fsub, deps=(e2,)),)
        ev_sub = (s.now(EV), s.now(EP))
        ev_tt = self.dve(lambda e: e.tensor_copy(
            self.tstrT[:].rearrange("p (b x y) -> p b x y", b=4, x=16),
            self.tstr[:].rearrange("p (b y x) -> p b x y", b=4, y=16)), deps=ev_sub)
        ev_d1 = self.dma(QS, self.scr_t16[sl],
                         self.tstr[:].rearrange("p (b x) -> p b x", b=4), deps=ev_sub)
        ev_d2 = self.dma(QS, self.scr_t16t[sl],
                         self.tstrT[:].rearrange("p (b x) -> p b x", b=4), deps=(ev_tt,))
        ev_z = self.dve(lambda e: e.memset(self.t32t[:], 0.0), deps=(s.now(ET),))
        ts = 4 * 256
        T16T = self.scr_t16t
        T16 = self.scr_t16
        lands = []
        for par in range(2):
            for a in range(2):
                p0 = par * 64 + a * 32
                dst = self.t32t[p0:p0 + 16, :].rearrange(
                    "r (q c) -> r q c", c=32)[:, :, 0:16]
                lands.append(self.dma(
                    QG, dst,
                    AP(T16T.tensor,
                       T16T.offset + sl * 128 * ts + par * 64 * ts + a * 512,
                       [[16, 16], [ts, 64], [1, 16]]),
                    deps=(ev_d2, ev_z)))
                dst = self.t32t[p0 + 16:p0 + 32, :].rearrange(
                    "r (q c) -> r q c", c=32)[:, :, 16:32]
                lands.append(self.dma(
                    QG, dst,
                    AP(T16T.tensor,
                       T16T.offset + sl * 128 * ts + par * 64 * ts + a * 512 + 256,
                       [[16, 16], [ts, 64], [1, 16]]),
                    deps=(ev_d2, ev_z)))
        for par in range(2):
            p0 = par * 64
            for a in range(2):
                dst = self.stripA[p0 + 32:p0 + 48, :].rearrange(
                    "r (q a2 c) -> r q a2 c", a2=2, c=16)[:, :, a]
                lands.append(self.dma(
                    QG, dst,
                    AP(T16.tensor,
                       T16.offset + sl * 128 * ts + par * 64 * ts + a * 512,
                       [[16, 16], [ts, 64], [1, 16]]),
                    deps=(ev_d1,) + ev_g2))
                dst = self.stripB[p0:p0 + 16, :].rearrange(
                    "r (q a2 c) -> r q a2 c", a2=2, c=16)[:, :, a]
                lands.append(self.dma(
                    QG, dst,
                    AP(T16T.tensor,
                       T16T.offset + sl * 128 * ts + par * 64 * ts + a * 512 + 256,
                       [[16, 16], [ts, 64], [1, 16]]),
                    deps=(ev_d2,)))
            lands.append(self.dma(
                QS, self.m32s[p0 + 32:p0 + 64, :].rearrange(
                    "i (q j) -> i q j", j=32),
                AP(SMP.tensor, SMP.offset + par * CST + b0 + 32 * 64,
                   [[64, 32], [4096, 64], [1, 32]]),
                deps=(deps + (s.now(ET),)) if par == 0 else ()))
        ev_land = tuple(lands) + ev_g2
        ev_x16 = []
        ev_ev = (s.now(EV),)
        for par in range(2):
            r0 = par * 64
            for a in range(2):
                for qh in range(2):
                    ev_mmb = []
                    for qq in range(32):
                        q = qh * 32 + qq
                        col = (q * 2 + a) * 16
                        d0 = ()
                        if qq == 0:
                            d0 = (ev_land + ev_ev) if (par == 0 and a == 0
                                                       and qh == 0) else ev_ev
                        ev_mmb.append(self.mm(
                            self.pp[4][r0:r0 + 16, qq * 16:qq * 16 + 16],
                            self.stripA[r0:r0 + 16, col:col + 16],
                            self.stripB[r0:r0 + 16, col:col + 16],
                            deps=d0, tp=(r0, r0)))
                    dstv = self.stripB[r0 + 32:r0 + 48].rearrange(
                        "r (q a c) -> r q a c", a=2, c=16)[:, qh * 32:qh * 32 + 32, a]
                    ev_ev0 = self.dve(lambda e, dstv=dstv, r0=r0: e.tensor_copy(
                        dstv, self.pp[4][r0:r0 + 16, 0:512].rearrange(
                            "p (b x) -> p b x", b=32)), deps=(ev_mmb[-1],))
                    ev_ev = (ev_ev0,)
                    ev_mmc = []
                    for qq in range(32):
                        q = qh * 32 + qq
                        col = (q * 2 + a) * 16
                        ev_mmc.append(self.mm(
                            self.pp[5][r0:r0 + 16, qq * 16:qq * 16 + 16],
                            self.stripA[r0 + 32:r0 + 48, col:col + 16],
                            self.stripB[r0 + 32:r0 + 48, col:col + 16],
                            deps=(ev_ev0, ev_x16[-1]) if qq == 0 and ev_x16
                            else ((ev_ev0,) if qq == 0 else ()), tp=(r0 + 32, r0)))
                    dstx = self.t32t[r0 + a * 32:r0 + a * 32 + 16].rearrange(
                        "r (q c) -> r q c", c=32)[:, qh * 32:qh * 32 + 32, 16:32]
                    ev_x16.append(self.dve(lambda e, dstx=dstx, r0=r0:
                                           e.tensor_scalar_mul(
                        dstx, self.pp[5][r0:r0 + 16, 0:512].rearrange(
                            "p (b x) -> p b x", b=32), -1.0), deps=(ev_mmc[-1],)))
        ev_zs = self.dve(lambda e: e.transpose(self.zstr[:], self.t32t[:]),
                         deps=tuple(ev_x16))
        ev_x32 = []
        ev_pe = None
        for par in range(2):
            r0 = par * 64
            for qh in range(4):
                ev_pm = []
                for qq in range(16):
                    q = qh * 16 + qq
                    ev_pm.append(self.mm(
                        self.pp[4][r0 + 32:r0 + 64, qq * 32:qq * 32 + 32],
                        self.m32s[r0 + 32:r0 + 64, q * 32:q * 32 + 32],
                        self.t32t[r0 + 32:r0 + 64, q * 32:q * 32 + 32],
                        deps=(ev_zs, ev_pe) if qq == 0 else (),
                        tp=(r0 + 32, r0 + 32)))
                ev_pe = self.dve(lambda e, r0=r0, qh=qh: e.tensor_copy(
                    self.pxs[r0:r0 + 32, qh * 512:(qh + 1) * 512],
                    self.pp[4][r0 + 32:r0 + 64, 0:512]), deps=(ev_pm[-1],))
                ev_xm = []
                for qq in range(16):
                    q = qh * 16 + qq
                    ev_xm.append(self.mm(
                        self.pp[5][r0:r0 + 32, qq * 32:qq * 32 + 32],
                        self.zstr[r0:r0 + 32, q * 32:q * 32 + 32],
                        self.pxs[r0:r0 + 32, qh * 512 + qq * 32:qh * 512 + qq * 32 + 32],
                        deps=(ev_pe, ev_x32[-1] if ev_x32 else None)
                        if qq == 0 else (), tp=(r0, r0)))
                ev_x32.append(self.dve(lambda e, r0=r0, qh=qh: e.tensor_scalar_mul(
                    self.pxs[r0:r0 + 32, qh * 512:(qh + 1) * 512],
                    self.pp[5][r0:r0 + 32, 0:512], -1.0), deps=(ev_xm[-1],)))
        for par in range(2):
            r0 = par * 64
            ev_x32.append(self.dve(
                lambda e, r0=r0: e.tensor_copy(
                    self.pxs[r0 + 32:r0 + 64, 0:2048],
                    self.t32t[r0 + 32:r0 + 64, 0:2048]),
                deps=tuple(ev_x32)))
        self.ev_solve = tuple(ev_x32)

    def _slab_chain(self, sl):
        s = self.s
        bar = tuple(s.barrier())
        ev_wo = {}
        pY, pU, pO1, pO2, pDS = (self.pp[0], self.pp[1], self.pp[2],
                                 self.pp[3], self.pp[4])
        if not hasattr(self, "bankdep"):
            self.bankdep = {}
        bd = self.bankdep
        gv = lambda t: t[:].rearrange("p (g x) -> p g x", g=8)
        for chl in range(SLAB):
            ch = sl * SLAB + chl
            cb = lambda slot, chl=chl: self.ccv(chl, slot)[:, :, None] \
                .broadcast_to((128, 8, 64))
            evY = evO1 = None
            for h in range(H):
                par, pa = h % 2, h // 2
                r0 = 64 * par
                sl2 = lambda t, r0=r0, pa=pa: t[r0:r0 + 64, pa * 64:pa * 64 + 64]
                kc = self._kc(h, chl, self.kcs)
                qc = self._kc(h, chl, self.qcs)
                dep0 = bar + (self.ev_S[h], self.ev_wo_R, self.ev_solve[-1],
                              bd.get("pY"), bd.get("pO1")) if h == 0 else ()
                evY = self.mm(sl2(pY), kc, sl2(self.Sb), deps=dep0, tp=(r0, r0))
                evO1 = self.mm(sl2(pO1), qc, sl2(self.Sb), deps=(), tp=(r0, r0))
            e_r1 = self.dve(lambda e, chl=chl, cb=cb: e.tensor_mul(
                gv(self.rbt), gv(pY), cb(2)), deps=(evY, evO1))
            e_m = self.dve(lambda e, chl=chl, cb=cb: e.tensor_mul(
                gv(self.msk), gv(self.vtm[chl]), cb(3)),
                deps=(self.ev_prep[chl][3],))
            e_r2 = self.dve(lambda e: e.tensor_add(gv(self.rbf), gv(self.msk),
                                                   gv(self.rbt)),
                            deps=(e_r1, e_m))
            bd["msk"] = e_r2
            evA = None
            for h in range(H):
                par, pa = h % 2, h // 2
                r0 = 64 * par
                q = chl * 8 + pa
                d0 = (e_r2, bd.get("pU")) if h == 0 else ()
                self.mm(pU[r0:r0 + 32, pa * 64:pa * 64 + 64],
                        self.t32t[r0:r0 + 32, q * 32:q * 32 + 32],
                        self.rbf[r0:r0 + 32, pa * 64:pa * 64 + 64],
                        deps=d0, tp=(r0, r0))
                evA = self.mm(pU[r0 + 32:r0 + 64, pa * 64:pa * 64 + 64],
                              self.pxs[r0:r0 + 64, q * 32:q * 32 + 32],
                              self.rbf[r0:r0 + 64, pa * 64:pa * 64 + 64],
                              start=True, stop=True, tp=(r0, r0 + 32))
            evU = self.act(lambda e: e.activation(self.ubf[:], pU[:], AF.Copy),
                           deps=(evA,))
            evO2 = evDS = None
            for h in range(H):
                par, pa = h % 2, h // 2
                r0 = 64 * par
                sl2 = lambda t, r0=r0, pa=pa: t[r0:r0 + 64, pa * 64:pa * 64 + 64]
                d0 = (evU, bd.get("pO2"), bd.get("pDS")) if h == 0 else ()
                evO2 = self.mm(sl2(pO2), sl2(self.ams[chl]), sl2(self.ubf),
                               deps=d0, tp=(r0, r0))
                evDS = self.mm(sl2(pDS), sl2(self.khat[chl]), sl2(self.ubf),
                               deps=(), tp=(r0, r0))
            e_o1 = self.dve(lambda e, cb=cb: e.tensor_mul(
                gv(self.rbt), gv(pO1), cb(6)), deps=(evO2, bd.get("rbt")))
            e_oc = self.dve(lambda e, chl=chl: e.tensor_add(
                self.oc[chl % 2][:], self.rbt[:], pO2[:]),
                deps=(e_o1, evO2, self.ev_oc_free))
            bd["rbt"] = e_oc
            e_s1 = self.pool(lambda e, cb=cb: e.tensor_mul(
                gv(self.Sm), gv(self.Sm), cb(5)),
                deps=(self.ev_S[0], bd.get("sm")))
            e_s2 = self.dve(lambda e: e.tensor_add(self.Sm[:], self.Sm[:], pDS[:]),
                            deps=(e_s1, evDS))
            bd["sm"] = e_s2
            ev_cast = self.act(lambda e: e.activation(self.Sb[:], self.Sm[:], AF.Copy),
                               deps=(e_s2,))
            for h in range(H):
                self.ev_S[h] = ev_cast
            bd["pY"] = e_r1
            bd["pO1"] = e_o1
            bd["pU"] = evU
            bd["pO2"] = e_oc
            bd["pDS"] = e_s2
            if chl % 2 == 1:
                self._pair_wo(sl, chl, ev_wo)

    def _pair_wo(self, sl, chl, ev_wo):
        s = self.s
        ch = sl * SLAB + chl
        ti = (ch - 1) // 2
        wsrc = self.wT["o"].rearrange("(j p) c -> p j c", p=128)
        ev_ot = []
        for h in range(H):
            par, pa = h % 2, h // 2
            r0 = 64 * par
            for cpar in range(2):
                pb = self.pp[4 + pa // 4]
                ev_ot.append(self.mm(
                    pb[r0:r0 + 64, (pa % 4) * 128 + cpar * 64:
                       (pa % 4) * 128 + cpar * 64 + 64],
                    self.oc[(chl - 1 + cpar) % 2][r0:r0 + 64, pa * 64:pa * 64 + 64],
                    self.c_i64d[r0:r0 + 64, :],
                    deps=(s.now(EV),) if (h == 0 and cpar == 0) else (),
                    tp=(r0, r0)))
        ev_oT1 = self.dve(lambda e: e.tensor_copy(self.oT[:, 0:512], self.pp[4][:]),
                          deps=(ev_ot[-1],))
        ev_oT2 = self.dve(lambda e: e.tensor_copy(self.oT[:, 512:1024], self.pp[5][:]))
        self.ev_oc_free = ev_oT2
        ev_last = [None, None]
        for j in range(8):
            qt = j // 2
            slot = self.wring[qt % 2]
            if j % 2 == 0:
                ev_wo[qt] = self.dma(
                    QG, slot[:].rearrange("p (jj c) -> p jj c", jj=2),
                    wsrc[:, qt * 2:qt * 2 + 2, :], deps=(s.now(ET),))
            for half in range(2):
                c0 = (j % 2) * 1024 + half * 512
                ev_last[half] = self.mm(
                    self.pp[2 + half][:, 0:512], self.oT[:, j * 128:(j + 1) * 128],
                    slot[:, c0:c0 + 512],
                    deps=(ev_oT1, ev_oT2, ev_wo[qt], s.now(EV)) if j == 0 else
                    ((ev_wo[qt],) if j % 2 == 0 and half == 0 else ()),
                    start=(j == 0), stop=(j == 7))
        for half in range(2):
            self.ev_wo_R = self.dve(lambda e, ti=ti, half=half: e.tensor_add(
                self.Rt[ti][:, half * 512:(half + 1) * 512],
                self.Rt[ti][:, half * 512:(half + 1) * 512],
                self.pp[2 + half][:, 0:512]), deps=(ev_last[half],))

    def p9_out(self):
        s = self.s
        qrr = (QS, QG, QA)
        for i in range(NTT):
            self.dma(qrr[i % 3], self.out_x[i * 128:(i + 1) * 128, :],
                     self.Rt[i][:], deps=tuple(self.ev_ffn2[i]))
        self.s.emit("gpsimd", lambda e: e.engine_nop(),
                    deps=(s.now(QS), s.now(QG), s.now(QA)), sig=EP)

    def p_debug(self):
        s = self.s
        deps = s.barrier()
        first = True
        for key, get_ap, shape, dtype in self.debug:
            ap = get_ap(self)
            self.dma(QS, self.dbg_out[key], ap, deps=deps if first else ())
            first = False
        if self.debug:
            self.s.emit("gpsimd", lambda e: e.engine_nop(), deps=(s.now(QS),), sig=EP)

    def finalize(self):
        nc, s = self.nc, self.s
        with contextlib.ExitStack() as es:
            for name in list(s.cnt):
                s.sems[name] = es.enter_context(nc.semaphore(name))
            with nc.Block() as block:
                block.tensor(lambda e: s.replay("tensor", e))
                block.vector(lambda e: s.replay("vector", e))
                block.scalar(lambda e: s.replay("scalar", e))
                block.gpsimd(lambda e: s.replay("gpsimd", e))
                block.sync(lambda e: s.replay("sync", e))


AUTOSER_MODE = False
PERM_GB = [0, 2, 4, 6, 8, 10, 12, 14, 1, 3, 5, 7, 9, 11, 13, 15]


def _stage_inputs(inp):
    bf = ml_dtypes.bfloat16
    f32 = np.float32
    x = np.ascontiguousarray(inp["L0_attn_x"], dtype=f32)
    com = {}
    com["wqT"] = np.ascontiguousarray(np.asarray(inp["L0_attn_Wq"], f32).T)
    com["wkT"] = np.ascontiguousarray(np.asarray(inp["L0_attn_Wk"], f32).T)
    com["wvT"] = np.ascontiguousarray(np.asarray(inp["L0_attn_Wv"], f32).T)
    com["woT"] = np.ascontiguousarray(np.asarray(inp["L0_attn_Wo"], f32).T)
    wg = np.asarray(inp["L0_attn_Wg"], f32).T[:, PERM_GB]
    wb = np.asarray(inp["L0_attn_Wb"], f32).T[:, PERM_GB]
    com["wgbT"] = np.ascontiguousarray(np.concatenate([wg, wb], 1))
    cw = np.zeros((1024, 12), f32)
    cw[:, 0:4] = np.asarray(inp["L0_attn_cq_w"], f32)[:, 0, :]
    cw[:, 4:8] = np.asarray(inp["L0_attn_ck_w"], f32)[:, 0, :]
    cw[:, 8:12] = np.asarray(inp["L0_attn_cv_w"], f32)[:, 0, :]
    com["convw"] = np.ascontiguousarray(cw.reshape(8, 128, 12).transpose(1, 0, 2)
                                        .reshape(128, 96))
    cb = np.stack([np.asarray(inp["L0_attn_cq_b"], f32),
                   np.asarray(inp["L0_attn_ck_b"], f32),
                   np.asarray(inp["L0_attn_cv_b"], f32)], 1)
    com["convb"] = np.ascontiguousarray(cb.reshape(8, 128, 3).transpose(1, 0, 2)
                                        .reshape(128, 24))
    g1 = []
    g2 = []
    for k1, k2 in (("L0_ffn_G1_R", "L0_ffn_G2_R"), ("L0_ffn_G1_I", "L0_ffn_G2_I")):
        G1 = np.asarray(inp[k1], f32)[0]
        g1.append(np.ascontiguousarray(G1.transpose(1, 2, 0).reshape(4, 1024)))
        G2 = np.asarray(inp[k2], f32)[:, :, 0, :]
        g2.append(np.ascontiguousarray(G2.reshape(4, 256)))
    com["g1p"] = np.ascontiguousarray(np.stack(g1))
    com["g2p"] = np.ascontiguousarray(np.stack(g2))
    com["fbias"] = np.ascontiguousarray(np.concatenate(
        [np.asarray(inp["L0_ffn_bias_real"], f32),
         np.asarray(inp["L0_ffn_bias_imag"], f32)])[None, :])
    com["ident128"] = np.eye(128, dtype=f32).astype(bf)
    com["ident64d"] = np.ascontiguousarray(
        np.concatenate([np.eye(64, dtype=f32)] * 2, 0)).astype(bf)
    com["ident96f"] = np.eye(96, dtype=f32)
    com["ident128f"] = np.eye(128, dtype=f32)
    sel = np.zeros((16, 1024), f32)
    for pa in range(8):
        for s in range(128):
            sel[pa + 8 * (s // 64), pa * 128 + s] = 1.0
    com["selb3"] = sel.astype(bf)
    r = np.arange(64)
    ml64 = np.where(r[None, :] < r[:, None], 0.0, NEG).astype(f32)
    mu64 = np.where(r[:, None] <= r[None, :], LN_KD, NEG).astype(f32)
    com["maskLd"] = np.ascontiguousarray(np.concatenate([ml64, ml64], 0))
    com["maskUd"] = np.ascontiguousarray(np.concatenate([mu64, mu64], 0))
    in_maps = []
    for b in range(8):
        m = dict(com)
        m["x"] = np.ascontiguousarray(x[b])
        in_maps.append(m)
    return in_maps


_BUILT = {}


def _get_nc(upto=9, debug=()):
    key = (upto, tuple(k for k, _, _, _ in debug))
    if key not in _BUILT:
        mk = MK(debug=debug)
        mk.build(upto=upto)
        _BUILT[key] = mk
    return _BUILT[key]


def kernel(**inputs):
    Sched.AUTOSER = AUTOSER_MODE
    in_maps = _stage_inputs(inputs)
    mk = _get_nc()
    res = run_bass_kernel_spmd(mk.nc, in_maps, core_ids=list(range(8)))
    return np.stack([np.asarray(r["out"], np.float32) for r in res.results])
